# version 11
# baseline (speedup 1.0000x reference)
import numpy as np
import ml_dtypes
import concourse.bass as bass
import concourse.mybir as mybir
from concourse.bass_utils import run_bass_kernel_spmd

F32 = mybir.dt.float32
BF16 = mybir.dt.bfloat16
AF = mybir.ActivationFunctionType
ALU = mybir.AluOpType
AX = mybir.AxisListType

NCORES = 8
L = 2048
D = 1024
FFN = 4096
NT = 16
NH = 8
EPS = 1e-5
GRAN = 256


def g_of_f():
    f = np.arange(4096)
    hi, lo = f // 128, f % 128
    return 128 * (lo // 4) + 32 * (lo % 4) + hi


def f_of_g():
    g = g_of_f()
    inv = np.empty_like(g)
    inv[g] = np.arange(4096)
    return inv


def stage_factor(weights, stages, s, I, J):
    I = np.asarray(I)
    J = np.asarray(J)
    sbits = 0
    for t in stages:
        sbits |= 1 << t
    done = 0
    for t in stages:
        if t == s:
            break
        done |= 1 << t
    ok = (I & ~sbits) == (J & ~sbits)
    s2 = 1 << s
    Z = (I & done) | (J & ~done)
    c_out = (I >> s) & 1
    c_in = (J >> s) & 1
    k = ((Z >> (s + 1)) << s) | (Z & (s2 - 1))
    vals = weights[s][k, c_out, c_in]
    return np.where(ok, vals, np.float32(0.0)).astype(np.float32)


def factor_stack(weights, stages, tile_idx, rows_are_out):
    stages = list(stages)
    out = np.zeros((len(stages), 128, 32 * 128), np.float32)
    for t in range(32):
        idx = tile_idx(t)
        for si, s in enumerate(stages):
            if rows_are_out:
                blk = stage_factor(weights, stages, s, idx[:, None], idx[None, :])
            else:
                blk = stage_factor(weights, stages, s, idx[None, :], idx[:, None])
            out[si, :, t * 128:(t + 1) * 128] = blk
    return out


ENGS = ("TEN", "DVE", "ACT", "POOL", "SP")


class Op:
    __slots__ = ("eng", "fn", "reads", "writes", "dma", "idx", "dma_no",
                 "need_inc", "inc_no", "waits", "tag")

    def __init__(self, eng, fn, reads, writes, dma, tag=""):
        self.eng = eng
        self.fn = fn
        self.reads = reads
        self.writes = writes
        self.dma = dma
        self.need_inc = False
        self.inc_no = None
        self.waits = []
        self.tag = tag


class Prog:
    def __init__(self):
        self.ops = {e: [] for e in ENGS}
        self.reg_w = {}
        self.reg_r = {}
        self.known = {e: {} for e in ENGS}

    def _dep(self, op, ref, kind):
        peng, pidx, pdma = ref
        if peng == op.eng and not pdma and not op.dma:
            if peng == "TEN":
                return
        if pdma:
            k = (peng, True, self.ops[peng][pidx].dma_no % self.NSLOT)
        else:
            k = (peng, False)
        if self.known[op.eng].get(k, -1) >= pidx:
            return
        self.known[op.eng][k] = pidx
        op.waits.append(ref)
        if not pdma:
            self.ops[peng][pidx].need_inc = True

    def add(self, eng, fn, reads=(), writes=(), dma=False, tag=""):
        idx = len(self.ops[eng])
        op = Op(eng, fn, reads, writes, dma, tag)
        deps = {}
        for g in reads:
            w = self.reg_w.get(g)
            if w is not None:
                deps[w] = "raw"
        for g in writes:
            w = self.reg_w.get(g)
            if w is not None and w not in deps:
                deps[w] = "waw"
            for r in self.reg_r.get(g, ()):
                if r not in deps:
                    deps[r] = "war"
        myref = (eng, idx, dma)
        for ref, kind in deps.items():
            if ref == myref:
                continue
            self._dep(op, ref, kind)
        for g in writes:
            self.reg_w[g] = myref
            self.reg_r[g] = []
        for g in reads:
            self.reg_r.setdefault(g, []).append(myref)
        op.idx = idx
        op.dma_no = None
        if dma:
            op.dma_no = sum(1 for o in self.ops[eng] if o.dma)
        self.ops[eng].append(op)
        return op

    NSLOT = 8

    def finalize(self, nc):
        for e in ENGS:
            n = 0
            for o in self.ops[e]:
                if (not o.dma) and o.need_inc:
                    n += 1
                    o.inc_no = n
        sems = {}
        dsems = {}
        for e in ENGS:
            sems[e] = nc.alloc_semaphore("sem_" + e)
            if any(o.dma for o in self.ops[e]):
                dsems[e] = [nc.alloc_semaphore(f"dsem_{e}{s}") for s in range(self.NSLOT)]

        def emit_for(engobj, e):
            known = {}
            def waitv(sem, key, v):
                if known.get(key, -1) < v:
                    known[key] = v
                    engobj.wait_ge(sem, v)
            for o in self.ops[e]:
                for (pe, pi, pdma) in o.waits:
                    if pdma:
                        dno = self.ops[pe][pi].dma_no
                        sl = dno % self.NSLOT
                        waitv(dsems[pe][sl], ("d", pe, sl), 16 * (dno // self.NSLOT + 1))
                    else:
                        v = self.ops[pe][pi].inc_no
                        assert v is not None, (e, o.tag, pe, pi)
                        waitv(sems[pe], ("e", pe), v)
                if o.dma:
                    sl = o.dma_no % self.NSLOT
                    if o.dma_no >= self.NSLOT:
                        waitv(dsems[e][sl], ("d", e, sl), 16 * (o.dma_no // self.NSLOT))
                inst = o.fn(engobj)
                if o.tag:
                    inst.annotate(o.tag)
                if o.dma:
                    inst.then_inc(dsems[e][o.dma_no % self.NSLOT], 16)
                elif o.need_inc:
                    inst.then_inc(sems[e], 1)
            ndma = sum(1 for o in self.ops[e] if o.dma)
            for sl in range(min(self.NSLOT, ndma)):
                cnt = (ndma - sl + self.NSLOT - 1) // self.NSLOT
                waitv(dsems[e][sl], ("d", e, sl), 16 * cnt)

        with nc.Block() as block:
            @block.tensor
            def _(t):
                emit_for(t, "TEN")

            @block.vector
            def _(v):
                emit_for(v, "DVE")

            @block.scalar
            def _(s):
                emit_for(s, "ACT")

            @block.gpsimd
            def _(g):
                emit_for(g, "POOL")

            @block.sync
            def _(sy):
                emit_for(sy, "SP")
        return sems, dsems


class View:

    def __init__(self, h, off, itemsize):
        self.h = h
        self.off = off
        self.it = itemsize

    def g(self, c0=0, c1=None):
        n = self.h.shape[-1] if c1 is None else c1
        tot = 1
        for s in self.h.shape[1:-1]:
            tot *= s
        b0 = self.off + c0 * self.it
        b1 = self.off + n * self.it
        return set(range(b0 // GRAN, (b1 + GRAN - 1) // GRAN))


HREG_CNT = [0]


def hreg(name):
    return {("hbm", name)}


def hregs(name, i0, i1):
    return {("hbm", name, i) for i in range(i0, i1)}


TWOPI = float(2 * np.pi)
HALFPI = float(np.pi / 2)
LN1E4 = float(np.log(10000.0))
GAMMAS = (1.0 - np.exp(np.linspace(np.log(1 / 32), np.log(1 / 512), NH))).astype(np.float64)
GAMC = [float(g ** 128) for g in GAMMAS]


def build_program(stages=9):
    nc = bass.Bass("TRN2", target_bir_lowering=False)

    di = {}
    def inp(name, shape, dt=F32):
        di[name] = nc.dram_tensor(name, list(shape), dt, kind="ExternalInput")
        return di[name]

    x0 = inp("x0", (L, D))
    wqkv = inp("wqkv", (2, D, 3 * D))
    wg = inp("wg", (2, D, D))
    wo = inp("wo", (2, D, D))
    w1 = inp("w1", (2, D, FFN))
    w2 = inp("w2", (2, FFN, D))
    wexpT = inp("wexpT", (FFN, D))
    wcong = inp("wcong", (FFN, D))
    fs1u = inp("fs1u", (7, 128, FFN))
    fs2u = inp("fs2u", (5, 128, FFN))
    fs2d = inp("fs2d", (5, 128, FFN))
    fs1d = inp("fs1d", (7, 128, FFN))
    lnv_i = inp("lnv", (128, 64))
    gnv_i = inp("gnv", (128, 32))
    b1c_i = inp("b1c", (128, 64))
    bexp_i = inp("bexp", (128, 32))
    upbg_i = inp("upbg", (128, 32))
    rows_i = inp("rows", (128, 2176))
    ident_i = inp("identbf", (128, 128), BF16)
    mask_i = inp("mask8", (128, 1024))
    jrep_i = inp("jrep", (128, 64))
    posg_i = inp("posg", (128, 16))
    hrep_i = inp("hrep", (128, 8))
    out_t = nc.dram_tensor("out", [L, D], F32, kind="ExternalOutput")

    spillX = nc.dram_tensor("spillX", [L, D], F32, kind="Internal")
    spillY = nc.dram_tensor("spillY", [L, D], F32, kind="Internal")
    spillH = nc.dram_tensor("spillH", [NT, 128, 32, 128], BF16, kind="Internal")
    wtexpd = nc.dram_tensor("wtexpd", [8, 128, FFN], BF16, kind="Internal")
    amidd = nc.dram_tensor("amidd", [FFN, D], BF16, kind="Internal")
    wtcond = nc.dram_tensor("wtcond", [FFN, D], BF16, kind="Internal")

    SB0 = (nc.sbuf_base + 31) // 32 * 32
    cur = [SB0]
    views = {}

    def sb(name, shape, dt, at=None, size=None):
        nbytes = int(np.prod(shape[1:])) * (2 if dt == BF16 else 4)
        if at is None:
            at = cur[0]
            cur[0] += (nbytes + 31) // 32 * 32
        h = nc.alloc_sbuf_tensor_at(name, list(shape), dt, offset=at)
        views[name] = (h, at, nbytes)
        return h

    def rg(name, e0=None, e1=None):
        h, at, nbytes = views[name]
        it = 2 if h.dtype == BF16 else 4
        b0 = at if e0 is None else at + e0 * it
        b1 = at + nbytes if e1 is None else at + e1 * it
        return set(range(b0 // GRAN, (b1 + GRAN - 1) // GRAN))

    ident = sb("ident", (128, 128), BF16)
    mask8 = sb("mask8", (128, 1024), F32)
    tqc = sb("tqc", (128, 16, 64), BF16)
    tqs = sb("tqs", (128, 16, 64), BF16)
    tkc = sb("tkc", (128, 16, 64), BF16)
    tks = sb("tks", (128, 16, 64), BF16)
    gamw = sb("gamw", (128, 8, 128), BF16)
    gamiw = sb("gamiw", (128, 8, 128), BF16)
    c2up = sb("c2up", (128, 32, 128), BF16)
    lnv = sb("lnv", (128, 64), F32)
    gnv = sb("gnv", (128, 32), F32)
    b1c = sb("b1c", (128, 64), BF16)
    bexpc = sb("bexpc", (128, 32), BF16)
    upbg = sb("upbg", (128, 32), BF16)
    btldc = sb("btldc", (128, 32), BF16)
    rows = sb("rows", (128, 2176), F32)
    stats = sb("stats", (128, 96), F32)
    sqd = sb("sqd", (128, 1024), F32)
    sqd_bf = sb("sqd_bf", (128, 2048), BF16, at=views["sqd"][1])
    tmA = sb("tmA", (128, 1024), BF16)
    qh = sb("qh", (128, 1024), BF16)
    kh = sb("kh", (128, 1024), BF16)
    vv = sb("vv", (128, 1024), BF16)
    x1f = sb("x1f", (128, 1024), F32, at=views["qh"][1])
    xres = sb("xres", (128, 1024), F32)
    Sst = sb("Sst", (128, 1024), F32)
    outstg = sb("outstg", (128, 1024), F32, at=views["Sst"][1])
    Sbf = sb("Sbf", (128, 1024), BF16)
    hstage = sb("hstage", (128, 1024), BF16)
    wconb = sb("wconb", (128, 1024), BF16)
    wconbB = sb("wconbB", (128, 1024), BF16)
    wowg = sb("wowg", (128, 8, 1024), BF16)
    WOW0 = views["wowg"][1]
    sqdB = sb("sqdB", (128, 1024), F32, at=WOW0)
    sqdB_bf = sb("sqdB_bf", (128, 2048), BF16, at=WOW0)
    tmAB = sb("tmAB", (128, 1024), BF16, at=WOW0 + 4096)
    qhB = sb("qhB", (128, 1024), BF16, at=WOW0 + 6144)
    khB = sb("khB", (128, 1024), BF16, at=WOW0 + 8192)
    vvB = sb("vvB", (128, 1024), BF16, at=WOW0 + 10240)
    xresB = sb("xresB", (128, 1024), F32, at=WOW0 + 12288)
    wcrot = [sb(f"wcrot{i}", (128, 1024), BF16, at=WOW0 + i * 2048) for i in range(8)]
    BIG1 = cur[0]
    stream = sb("stream", (128, 16, 1024), F32)
    wqkvb = sb("wqkvb", (128, 8, 3072), BF16, at=BIG1)
    w1b = sb("w1b", (128, 8, 4096), BF16, at=BIG1)
    wteb = sb("wteb", (128, 8, 4096), BF16, at=BIG1)
    facA = sb("facA", (128, 4096), F32, at=BIG1)
    facB = sb("facB", (128, 4096), F32, at=BIG1 + 16384)
    comp1 = sb("comp1", (128, 4096), BF16, at=BIG1 + 32768)
    fstg = sb("fstg", (128, 4096), BF16, at=BIG1 + 40960)
    XNT0 = cur[0]
    xnT = sb("xnT", (128, 8, 2048), BF16)
    fmaj = sb("fmaj", (128, 8, 2048), BF16)
    w2b = sb("w2b", (128, 32, 1024), BF16, at=XNT0)
    h1g = sb("h1g", (128, 32, 256), BF16, at=views["fmaj"][1])
    zg = sb("zg", (128, 32, 256), BF16, at=views["fmaj"][1] + 16384)
    assert cur[0] <= 229000, (cur[0], SB0)
    import sys; print('SBUF cur:', cur[0], 'base:', SB0, file=sys.stderr)

    pbA = nc.alloc_psum_tensor("pbA", [128, 512], F32)
    pbB = nc.alloc_psum_tensor("pbB", [128, 512], F32)
    pbC = nc.alloc_psum_tensor("pbC", [128, 512], F32)
    pbD = nc.alloc_psum_tensor("pbD", [128, 512], F32)
    pbT = nc.alloc_psum_tensor("pbT", [128, 1024], BF16)
    pbM = nc.alloc_psum_tensor("pbM", [128, 1024], BF16)
    pbY = nc.alloc_psum_tensor("pbY", [128, 1024], F32)
    PS = {"pbA": {("ps", 0)}, "pbB": {("ps", 1)}, "pbC": {("ps", 2)},
          "pbD": {("ps", 3)}, "pbT": {("ps", 4)}, "pbM": {("ps", 5)},
          "pbY": {("ps", 6), ("ps", 7)}, "pbY0": {("ps", 6)}, "pbY1": {("ps", 7)}}

    P = Prog()

    def H(name, i=0):
        return {("h", name, i)}

    def Hr(name, i0, i1):
        return {("h", name, i) for i in range(i0, i1)}

    def dma(q, dst, src, reads, writes, tag=""):
        P.add(q, lambda e, d=dst, s=src: e.dma_start(d, s), reads, writes, dma=True, tag=tag)

    def mm(outap, lhs, rhs, rd, wr, start=True, stop=True, tag=""):
        P.add("TEN", lambda e, o=outap, l=lhs, r=rhs, st=start, sp=stop:
              e.matmul(o, l, r, start=st, stop=sp), rd, wr, tag=tag)

    def tp(outap, inap, rd, wr, tag=""):
        P.add("TEN", lambda e, o=outap, i=inap: e.transpose(o, i, ident[:, :]),
              rd, wr, tag=tag)

    def act(fn, outap, inap, rd, wr, func, bias=0.0, scale=1.0, accum=None, tag=""):
        P.add("ACT", lambda e, o=outap, i=inap, f=func, b=bias, s=scale, a=accum:
              e.activation(o, i, f, bias=b, scale=s, accum_out=a), rd, wr, tag=tag)

    def dve(kind, *a, rd=(), wr=(), tag=""):
        if kind == "tt":
            o, i0, i1, op = a
            P.add("DVE", lambda e: e.tensor_tensor(o, i0, i1, op), rd, wr, tag=tag)
        elif kind == "ts":
            o, i0, s1, s2, op0, op1 = a
            P.add("DVE", lambda e: e.tensor_scalar(o, i0, s1, s2, op0, op1), rd, wr, tag=tag)
        elif kind == "ts1":
            o, i0, s1, op0 = a
            P.add("DVE", lambda e: e.tensor_scalar(o, i0, s1, None, op0), rd, wr, tag=tag)
        elif kind == "stt":
            o, i0, s, i1, op0, op1 = a
            P.add("DVE", lambda e: e.scalar_tensor_tensor(o, i0, s, i1, op0, op1), rd, wr, tag=tag)
        elif kind == "red":
            o, i0 = a
            P.add("DVE", lambda e: e.reduce_sum(o, i0, axis=AX.X), rd, wr, tag=tag)
        elif kind == "cp":
            o, i0 = a
            P.add("DVE", lambda e: e.tensor_copy(o, i0), rd, wr, tag=tag)
        elif kind == "memset":
            o, val = a
            P.add("DVE", lambda e: e.memset(o, val), rd, wr, tag=tag)

    def stats_chain(tileap, tile_rd, W, c, tag=""):
        sc = stats[:, c:c + 1]
        sc2 = stats[:, c + 1:c + 2]
        sc3 = stats[:, c + 2:c + 3]
        g_c = rg("stats", c, c + 4)
        dve("red", sc, tileap, rd=tile_rd, wr=g_c, tag=tag + ".sum")
        act(None, sqd[:, 0:W], tileap, tile_rd, rg("sqd", 0, W) | g_c, AF.Square,
            accum=sc3, tag=tag + ".sq")
        dve("ts", sc, sc, 1.0 / W, None, ALU.mult, ALU.bypass, rd=g_c, wr=g_c, tag=tag + ".mu")
        dve("ts", sc3, sc3, 1.0 / W, EPS, ALU.mult, ALU.add, rd=g_c, wr=g_c, tag=tag + ".m2")
        dve("ts1", sc2, sc, -1.0, ALU.mult, rd=g_c, wr=g_c, tag=tag + ".nmu")
        P.add("DVE", lambda e, o=sc2, i0=sc, s=sc2, i1=sc3:
              e.scalar_tensor_tensor(o, i0, s, i1, ALU.mult, ALU.add), g_c, g_c, tag=tag + ".var")
        act(None, sc2, sc2, g_c, g_c, AF.Ln, tag=tag + ".ln")
        act(None, sc2, sc2, g_c, g_c, AF.Exp, scale=-0.5, tag=tag + ".rstd")
        return sc, sc2

    dma("SP", ident[:, :], ident_i[:, :], set(), rg("ident"), tag="ld.ident")
    dma("SP", mask8[:, :], mask_i[:, :], set(), rg("mask8"), tag="ld.mask")
    dma("SP", lnv[:, :], lnv_i[:, :], set(), rg("lnv"))
    dma("SP", gnv[:, :], gnv_i[:, :], set(), rg("gnv"))
    dma("POOL", b1c[:, :], b1c_i[:, :], set(), rg("b1c"))
    dma("POOL", bexpc[:, :], bexp_i[:, :], set(), rg("bexpc"))
    dma("POOL", upbg[:, :], upbg_i[:, :], set(), rg("upbg"))
    dma("SP", rows[:, :], rows_i[:, :], set(), rg("rows"))
    dma("SP", stats[:, 0:16], posg_i[:, :], set(), rg("stats", 0, 16))
    dma("SP", xres[:, 0:64], jrep_i[:, :], set(), rg("xres", 0, 64))
    dma("SP", xres[:, 64:72], hrep_i[:, :], set(), rg("xres", 64, 72))
    P.add("POOL", lambda e: e.iota(stats[:, 16:17], [[0, 1]], channel_multiplier=1,
                                   allow_small_or_imprecise_dtypes=True),
          set(), rg("stats", 16, 17), tag="iota.p")

    jr = xres[:, 0:64]
    g_j = rg("xres", 0, 64)
    invf = xres[:, 128:192]
    lnb = xres[:, 192:256]
    g_iv = rg("xres", 128, 256)
    act(None, invf, jr, g_j, g_iv, AF.Exp, scale=-LN1E4 / 64.0, tag="tab.invf")
    dve("ts", lnb, jr, 2.0 / 179.2, 51.2 / 179.2, ALU.mult, ALU.add, rd=g_j, wr=g_iv, tag="tab.b")
    act(None, lnb, lnb, g_iv, g_iv, AF.Ln, tag="tab.lnb")
    posb = stats[:, 0:16].unsqueeze(2).broadcast_to([128, 16, 64])
    g_pos = rg("stats", 0, 17)
    g_sq = rg("sqd")
    g_ws = rg("qh") | rg("kh")
    ivb = invf.unsqueeze(1).broadcast_to([128, 16, 64])
    ang = sqd[:, :].rearrange("p (t j) -> p t j", t=16)
    angc = Sst[:, 0:1024].rearrange("p (t j) -> p t j", t=16)
    g_ss = rg("Sst")
    dve("tt", ang, posb, ivb, ALU.mult, rd=g_pos | g_iv, wr=g_sq, tag="tab.ang")
    dve("ts1", Sst[:, 0:1024], sqd[:, :], HALFPI, ALU.add, rd=g_sq, wr=g_ss, tag="tab.angc")
    for tgt, g_t in ((sqd[:, :], g_sq), (Sst[:, 0:1024], g_ss)):
        kf = x1f[:, :]
        dve("ts", kf, tgt, 1.0 / TWOPI, 0.5, ALU.mult, ALU.add, rd=g_t, wr=g_ws)
        dve("cp", kf.bitcast(mybir.dt.int32), kf, rd=g_ws, wr=g_ws, tag="tab.ri")
        dve("cp", kf, kf.bitcast(mybir.dt.int32), rd=g_ws, wr=g_ws)
        dve("stt", tgt, kf, -TWOPI, tgt, ALU.mult, ALU.add, rd=g_t | g_ws, wr=g_t)
    act(None, sqd[:, :], sqd[:, :], g_sq, g_sq, AF.Sin, tag="tab.sin")
    act(None, Sst[:, 0:1024], Sst[:, 0:1024], g_ss, g_ss, AF.Sin, tag="tab.cos")
    lnsc = x1f[:, :].rearrange("p (t j) -> p t j", t=16)
    dve("tt", lnsc, posb, lnb.unsqueeze(1).broadcast_to([128, 16, 64]), ALU.mult,
        rd=g_pos | g_iv, wr=g_ws, tag="tab.lnsc")
    dve("ts1", x1f[:, :], x1f[:, :], 1.0 / 512.0, ALU.mult, rd=g_ws, wr=g_ws)
    sinv = sqd[:, :]
    cosv = Sst[:, 0:1024]
    for nm, srcv, g_src, sgn in (("tqc", cosv, g_ss, 1.0), ("tqs", sinv, g_sq, 1.0),
                                 ("tkc", cosv, g_ss, -1.0), ("tks", sinv, g_sq, -1.0)):
        t_v = views[nm][0]
        tv = t_v[:, :, :].rearrange("p t j -> p (t j)")
        act(None, tv, x1f[:, :], g_ws, rg(nm), AF.Exp, scale=sgn, tag="tab.e" + nm)
        dve("tt", tv, tv, srcv, ALU.mult, rd=rg(nm) | g_src, wr=rg(nm), tag="tab.m" + nm)
    hr = xres[:, 64:72]
    g_h = rg("xres", 64, 72)
    lng = stats[:, 32:40]
    g_lng = rg("stats", 32, 40)
    dve("ts", lng, hr, float((np.log(1 / 512) - np.log(1 / 32)) / 7), float(np.log(1 / 32)),
        ALU.mult, ALU.add, rd=g_h, wr=g_lng, tag="gam.lin")
    act(None, lng, lng, g_lng, g_lng, AF.Exp)
    dve("ts", lng, lng, -1.0, 1.0, ALU.mult, ALU.add, rd=g_lng, wr=g_lng)
    act(None, lng, lng, g_lng, g_lng, AF.Ln)
    lngb = lng.unsqueeze(2).broadcast_to([128, 8, 128])
    pc = stats[:, 16:17]
    dve("ts1", sqd[:, :].rearrange("p (h e) -> p h e", h=8), lngb, pc, ALU.mult,
        rd=g_lng | g_pos, wr=g_sq, tag="gam.pe")
    act(None, gamw[:, :, :].rearrange("p h e -> p (h e)"), sqd[:, :], g_sq, rg("gamw"), AF.Exp)
    act(None, gamiw[:, :, :].rearrange("p h e -> p (h e)"), sqd[:, :], g_sq, rg("gamiw"),
        AF.Exp, scale=-1.0)

    facC = sb("facC", (128, 4096), F32, at=BIG1 + 49152)

    def compose(fsrc, nst, dst_ap, dst_rg, tag):
        dma("SP", facA[:, :], fsrc[0], set(), rg("facA"), tag=tag + ".l0")
        bufs = [(facB, rg("facB")), (facC, rg("facC"))]
        dma("POOL", bufs[0][0][:, :], fsrc[1], set(), bufs[0][1], tag=tag + ".l1")
        for s in range(1, nst):
            cu, cu_g = bufs[(s - 1) % 2]
            if s + 1 < nst:
                nx, nx_g = bufs[s % 2]
                dma("SP" if s % 2 else "POOL", nx[:, :], fsrc[s + 1], set(), nx_g,
                    tag=tag + f".l{s+1}")
            last = s == nst - 1
            o = dst_ap if last else facA[:, :]
            og = dst_rg if last else rg("facA")
            dve("tt", o, facA[:, :], cu[:, :], ALU.mult, rd=rg("facA") | cu_g, wr=og,
                tag=tag + f".m{s}")

    if stages >= 2:
     compose(fs2u, 5, c2up[:, :, :].rearrange("p t c -> p (t c)"), rg("c2up"), "c2u")
     compose(fs1u, 7, comp1[:, :], rg("comp1"), "c1u")

    G_STR = [rg("stream", t * 1024, (t + 1) * 1024) for t in range(NT)]

    def xnt_slice(dt, c0, c1):
        return xnT[:, dt, c0:c1]

    def g_xnt(dt, c0, c1):
        base = dt * 2048
        return rg("xnT", base + c0, base + c1)

    def g_fmaj(fb, c0, c1):
        base = fb * 2048
        return rg("fmaj", base + c0, base + c1)

    def stats8(c_in, c_in2, c_mu, c_rstd, tag):
        s_sum = stats[:, c_in:c_in + 8]
        s_sq = stats[:, c_in2:c_in2 + 8]
        s_mu = stats[:, c_mu:c_mu + 8]
        s_rs = stats[:, c_rstd:c_rstd + 8]
        g_all = rg("stats", 16, 64)
        dve("ts", s_mu, s_sum, 1.0 / 128, None, ALU.mult, ALU.bypass, rd=g_all, wr=g_all, tag=tag + ".mu")
        dve("ts", s_sq, s_sq, 1.0 / 128, EPS, ALU.mult, ALU.add, rd=g_all, wr=g_all)
        dve("ts", s_rs, s_mu, -1.0, None, ALU.mult, ALU.bypass, rd=g_all, wr=g_all)
        dve("tt", s_rs, s_rs, s_mu, ALU.mult, rd=g_all, wr=g_all)
        dve("tt", s_rs, s_rs, s_sq, ALU.add, rd=g_all, wr=g_all)
        act(None, s_rs, s_rs, g_all, g_all, AF.Ln, tag=tag + ".ln")
        act(None, s_rs, s_rs, g_all, g_all, AF.Exp, scale=-0.5)
        return s_mu, s_rs

    def we_iter(ft):
        w2bf = w2b[:, :, :].rearrange("p a b -> p (a b)")
        wcb, wcn = wcrot[ft % 8], f"wcrot{ft % 8}"
        dma("POOL", wcb[:, :], wexpT[ft * 128:(ft + 1) * 128, :], set(), rg(wcn),
            tag=f"we.l{ft}")
        c1t = comp1[:, ft * 128:(ft + 1) * 128]
        g_c1t = rg("comp1", ft * 128, (ft + 1) * 128)
        for dq in range(2):
            ps, pg = (pbA, PS["pbA"]) if dq == 0 else (pbB, PS["pbB"])
            for di in range(4):
                dt = dq * 4 + di
                mm(ps[:, di * 128:(di + 1) * 128], wcb[:, dt * 128:(dt + 1) * 128], c1t,
                   rg(wcn) | g_c1t, pg, tag=f"we.mm{ft}.{dt}")
            dst = w2bf.rearrange("p (dt a b c) -> p dt a b c", dt=8, a=32, b=4)[:, dq * 4:(dq + 1) * 4, :, :, ft]
            srcp = ps[:, 0:512].rearrange("p (di a b) -> p di a b", di=4, a=32)
            e0 = dq * 4 * 4096
            if dq == 0:
                P.add("DVE", lambda e, o=dst, i=srcp: e.tensor_copy(o, i),
                      pg, rg("w2b", e0, e0 + 16384), tag=f"we.ev{ft}.{dq}")
            else:
                P.add("ACT", lambda e, o=dst, i=srcp: e.activation(o, i, AF.Copy),
                      pg, rg("w2b", e0, e0 + 16384), tag=f"we.ev{ft}.{dq}")
        mm(pbC[0:1, ft % 4 * 128:(ft % 4 + 1) * 128], bexpc[:, ft:ft + 1], c1t,
           rg("bexpc") | g_c1t, PS["pbC"], tag=f"bt.mm{ft}")
        btdst = fstg[0:1, :].rearrange("p (a b c) -> p a b c", a=32, b=4)[:, :, :, ft]
        P.add("ACT", lambda e, o=btdst,
              i=pbC[0:1, ft % 4 * 128:(ft % 4 + 1) * 128].rearrange("p (a b) -> p a b", a=32):
              e.activation(o, i, AF.Copy),
              PS["pbC"], rg("fstg"), tag=f"bt.ev{ft}")

    def we_finish():
        for dt in range(8):
            dma("SP", wtexpd[dt, :, :], w2b[:, :, :].rearrange("p a b -> p (a b)")
                .rearrange("p (dt c) -> p dt c", dt=8)[:, dt, :],
                rg("w2b", dt * 4096, (dt + 1) * 4096), H("wtexpd", dt), tag=f"we.st{dt}")
        for gt in range(32):
            mm(pbD[0:128, gt:gt + 1], fstg[0:1, gt * 128:(gt + 1) * 128], ident[0:1, 0:1],
               rg("fstg") | rg("ident"), PS["pbD"], tag=f"bt.t{gt}")
        dve("cp", btldc[:, :], pbD[:, 0:32], rd=PS["pbD"], wr=rg("btldc"), tag="bt.col")

    def a1_iter(gt):
        wcb, wcn = wcrot[gt % 8], f"wcrot{gt % 8}"
        hsb, hsn = [(hstage, "hstage"), (wconb, "wconb"), (wconbB, "wconbB"), (tmA, "tmA")][gt % 4]
        dma("POOL", wcb[:, :], wcong[gt * 128:(gt + 1) * 128, :], set(), rg(wcn),
            tag=f"a1.l{gt}")
        for half in range(2):
            ps, pg = (pbC, PS["pbC"]) if half == 0 else (pbD, PS["pbD"])
            mm(ps[:, :], comp1[:, gt * 128:(gt + 1) * 128],
               wcb[:, half * 512:(half + 1) * 512],
               rg(wcn) | rg("comp1", gt * 128, (gt + 1) * 128), pg, tag=f"a1.mm{gt}.{half}")
            P.add("ACT", lambda e, o=hsb[:, half * 512:(half + 1) * 512], i=ps[:, :]:
                  e.activation(o, i, AF.Copy), pg,
                  rg(hsn, half * 512, (half + 1) * 512), tag=f"a1.ev{gt}.{half}")
        dma("SP", amidd[gt * 128:(gt + 1) * 128, :], hsb[:, :], rg(hsn),
            H("amidd", gt), tag=f"a1.st{gt}")

    def a2_iter(ft):
        wcb, wcn = wcrot[ft % 8], f"wcrot{ft % 8}"
        hsb, hsn = [(hstage, "hstage"), (wconb, "wconb"), (wconbB, "wconbB"), (tmA, "tmA")][ft % 4]
        srcap = amidd.ap().rearrange("(a b c) d -> a b c d", a=32, b=4)[:, :, ft, :]
        dma("SP", wcb[:, :], srcap, Hr("amidd", 0, 32), rg(wcn), tag=f"a2.l{ft}")
        for half in range(2):
            ps, pg = (pbC, PS["pbC"]) if half == 0 else (pbD, PS["pbD"])
            mm(ps[:, :], comp1[:, ft * 128:(ft + 1) * 128],
               wcb[:, half * 512:(half + 1) * 512],
               rg(wcn) | rg("comp1", ft * 128, (ft + 1) * 128), pg, tag=f"a2.mm{ft}.{half}")
            P.add("ACT", lambda e, o=hsb[:, half * 512:(half + 1) * 512], i=ps[:, :]:
                  e.activation(o, i, AF.Copy), pg,
                  rg(hsn, half * 512, (half + 1) * 512), tag=f"a2.ev{ft}.{half}")
        dma("SP", wtcond[ft * 128:(ft + 1) * 128, :], hsb[:, :], rg(hsn),
            H("wtcond", ft), tag=f"a2.st{ft}")

    def ln0_tile(t):
        tc = (t * 128, (t + 1) * 128)
        dma("SP", xres[:, :], x0[tc[0]:tc[1], :], set(), rg("xres"), tag=f"ln0.l{t}")
        mu, rs = stats_chain(xres[:, :], rg("xres"), 1024, (t % 4) * 4, tag=f"ln0.{t}")
        dve("ts", stream[:, t, :], xres[:, :], mu, rs, ALU.subtract, ALU.mult,
            rd=rg("xres") | rg("stats", (t % 4) * 4, (t % 4) * 4 + 4), wr=G_STR[t],
            tag=f"ln0.n{t}")

    ln0_left = (list(range(0, 8)) + list(range(10, 16))) if stages >= 4 else []
    if stages >= 3:
        for ft in range(32):
            we_iter(ft)
        we_finish()
        compose(fs2d, 5, comp1[:, :], rg("comp1"), "c2d")
        for gt in range(32):
            a1_iter(gt)
        compose(fs1d, 7, comp1[:, :], rg("comp1"), "c1d")
        for ft in range(32):
            a2_iter(ft)
            if ft % 2 == 1 and ln0_left:
                ln0_tile(ln0_left.pop(0))
    ln0_left += [8, 9] if stages >= 4 else []
    while ln0_left:
        ln0_tile(ln0_left.pop(0))

    SCR = [
        {"qh": (qh, "qh"), "kh": (kh, "kh"), "vv": (vv, "vv"),
         "tmA": (tmA, "tmA"), "sqd": (sqd, "sqd"), "sqd_bf": (sqd_bf, "sqd")},
        {"qh": (qhB, "qhB"), "kh": (khB, "khB"), "vv": (vvB, "vvB"),
         "tmA": (tmAB, "tmAB"), "sqd": (sqdB, "sqdB"), "sqd_bf": (sqdB_bf, "sqdB")},
    ]

    def attn_layer(li):
        for t in range(NT):
            par = t % 2
            tA, tAn = SCR[par]["tmA"]
            strt = stream[:, t, :]
            dma("SP", spillX[t * 128:(t + 1) * 128, :], strt, G_STR[t], H("spX", t),
                tag=f"L{li}.spX{t}")
            mu, rs = stats_chain(strt, G_STR[t], 1024, (t % 4) * 4, tag=f"L{li}.A{t}")
            g_st = rg("stats", (t % 4) * 4, (t % 4) * 4 + 4)
            dve("ts", tA[:, :], strt, mu, rs, ALU.subtract, ALU.mult,
                rd=G_STR[t] | g_st, wr=rg(tAn), tag=f"L{li}.An{t}")
            for dt in range(8):
                tp(pbT[:, dt * 128:(dt + 1) * 128], tA[:, dt * 128:(dt + 1) * 128],
                   rg(tAn, dt * 128, (dt + 1) * 128) | rg("ident"), PS["pbT"],
                   tag=f"L{li}.At{t}.{dt}")
            for dt in range(8):
                act(None, xnt_slice(dt, t * 128, (t + 1) * 128),
                    pbT[:, dt * 128:(dt + 1) * 128], PS["pbT"],
                    g_xnt(dt, t * 128, (t + 1) * 128), AF.Identity,
                    bias=lnv[:, li * 32 + 8 + dt:li * 32 + 8 + dt + 1],
                    scale=lnv[:, li * 32 + dt:li * 32 + dt + 1], tag=f"L{li}.Ae{t}.{dt}")
        dma("POOL", wqkvb[:, :, :], wqkv.ap()[li].rearrange("(dt p) c -> p dt c", p=128),
            set(), rg("wqkvb"), tag=f"L{li}.wqkv")
        dma("POOL", wowg[:, :, :], wg.ap()[li].rearrange("(dt p) c -> p dt c", p=128),
            set(), rg("wowg"), tag=f"L{li}.wg")
        for fb in range(8):
            for ch in range(4):
                ps, pg = (pbA, PS["pbA"]) if (fb * 4 + ch) % 2 == 0 else (pbB, PS["pbB"])
                for dt in range(8):
                    mm(ps[:, :], wowg[:, dt, fb * 128:(fb + 1) * 128],
                       xnT[:, dt, ch * 512:(ch + 1) * 512],
                       rg("wowg", dt * 1024 + fb * 128, dt * 1024 + (fb + 1) * 128)
                       | g_xnt(dt, ch * 512, (ch + 1) * 512),
                       pg, start=(dt == 0), stop=(dt == 7), tag=f"L{li}.G{fb}.{ch}.{dt}")
                act(None, fmaj[:, fb, ch * 512:(ch + 1) * 512], ps[:, :], pg,
                    g_fmaj(fb, ch * 512, (ch + 1) * 512), AF.Silu, tag=f"L{li}.Gs{fb}.{ch}")
        dve("memset", Sst[:, :], 0.0, rd=set(), wr=rg("Sst"), tag=f"L{li}.S0")
        dve("memset", Sbf[:, :], 0.0, rd=set(), wr=rg("Sbf"), tag=f"L{li}.Sb0")
        for t in range(NT):
            par = t % 2
            q_h, qn = SCR[par]["qh"]
            k_h, kn = SCR[par]["kh"]
            v_v, vn = SCR[par]["vv"]
            tA, tAn = SCR[par]["tmA"]
            sQ, sQn = SCR[par]["sqd"]
            sQb, _ = SCR[par]["sqd_bf"]
            gnb = 16 + 32 * par
            tc0, tc1 = t * 128, (t + 1) * 128
            for which, dst, dgn in (("q", q_h, qn), ("k", k_h, kn), ("v", v_v, vn)):
                base = {"q": 0, "k": 1024, "v": 2048}[which]
                for half in range(2):
                    ps, pg = (pbA, PS["pbA"]) if half == 0 else (pbB, PS["pbB"])
                    for dt in range(8):
                        mm(ps[:, :], xnT[:, dt, tc0:tc1],
                           wqkvb[:, dt, base + half * 512:base + (half + 1) * 512],
                           g_xnt(dt, tc0, tc1)
                           | rg("wqkvb", dt * 3072 + base + half * 512,
                                dt * 3072 + base + (half + 1) * 512),
                           pg, start=(dt == 0), stop=(dt == 7),
                           tag=f"L{li}.{which}{t}.{half}.{dt}")
                    act(None, dst[:, half * 512:(half + 1) * 512], ps[:, :], pg,
                        rg(dgn, half * 512, (half + 1) * 512), AF.Copy,
                        tag=f"L{li}.{which}e{t}.{half}")
            for src_, sgn, cb, sb_, gw, gwg in (
                    (q_h, qn, tqc, tqs, gamw, rg("gamw")),
                    (k_h, kn, tkc, tks, gamiw, rg("gamiw"))):
                sg = rg(sgn)
                sv = src_[:, :].rearrange("p (h two j) -> p h two j", h=8, two=2)
                e_v = sv[:, :, 0, :]
                o_v = sv[:, :, 1, :]
                Cb = cb[:, t, :].unsqueeze(1).broadcast_to([128, 8, 64])
                Sb = sb_[:, t, :].unsqueeze(1).broadcast_to([128, 8, 64])
                g_tab = rg("tqc" if cb is tqc else "tkc") | rg("tqs" if sb_ is tqs else "tks")
                t1 = sQb[:, 1024:1536].rearrange("p (h j) -> p h j", h=8)
                t2 = sQb[:, 1536:2048].rearrange("p (h j) -> p h j", h=8)
                g_t12 = rg(sQn, 512, 1024)
                dve("tt", t1, e_v, Sb, ALU.mult, rd=sg | g_tab, wr=g_t12)
                P.add("POOL", lambda e, o=t2, i0=o_v, i1=Sb: e.tensor_tensor(o, i0, i1, ALU.mult),
                      sg | g_tab, g_t12, tag=f"L{li}.xpp{t}")
                dve("tt", e_v, e_v, Cb, ALU.mult, rd=sg | g_tab, wr=sg)
                dve("tt", e_v, e_v, t2, ALU.subtract, rd=sg | g_t12, wr=sg)
                dve("tt", o_v, o_v, Cb, ALU.mult, rd=sg | g_tab, wr=sg)
                dve("tt", o_v, o_v, t1, ALU.add, rd=sg | g_t12, wr=sg)
                dve("tt", src_[:, :], src_[:, :],
                    gw[:, :, :].rearrange("p h e -> p (h e)"), ALU.mult,
                    rd=sg | gwg, wr=sg, tag=f"L{li}.xp{t}")
            for h in range(NH):
                ps, pg = (pbA, PS["pbA"]) if h < 4 else (pbB, PS["pbB"])
                mm(ps[:, (h % 4) * 128:(h % 4 + 1) * 128], k_h[:, h * 128:(h + 1) * 128],
                   v_v[:, h * 128:(h + 1) * 128],
                   rg(kn, h * 128, (h + 1) * 128) | rg(vn, h * 128, (h + 1) * 128),
                   pg, tag=f"L{li}.T{t}.{h}")
            for h in range(NH):
                tp(pbT[:, h * 128:(h + 1) * 128], q_h[:, h * 128:(h + 1) * 128],
                   rg(qn, h * 128, (h + 1) * 128) | rg("ident"), PS["pbT"],
                   tag=f"L{li}.tq{t}.{h}")
            dve("cp", q_h[:, :], pbT[:, :], rd=PS["pbT"], wr=rg(qn), tag=f"L{li}.tqe{t}")
            for h in range(NH):
                tp(pbM[:, h * 128:(h + 1) * 128], k_h[:, h * 128:(h + 1) * 128],
                   rg(kn, h * 128, (h + 1) * 128) | rg("ident"), PS["pbM"],
                   tag=f"L{li}.tk{t}.{h}")
            dve("cp", k_h[:, :], pbM[:, :], rd=PS["pbM"], wr=rg(kn), tag=f"L{li}.tke{t}")
            for h in range(NH):
                pg = PS["pbY0"] if h < 4 else PS["pbY1"]
                mm(pbY[:, h * 128:(h + 1) * 128], k_h[:, h * 128:(h + 1) * 128],
                   q_h[:, h * 128:(h + 1) * 128],
                   rg(kn, h * 128, (h + 1) * 128) | rg(qn, h * 128, (h + 1) * 128),
                   pg, tag=f"L{li}.A{t}.{h}")
            for half in range(2):
                dve("tt", tA[:, half * 512:(half + 1) * 512],
                    pbY[:, half * 512:(half + 1) * 512], mask8[:, half * 512:(half + 1) * 512],
                    ALU.mult, rd=(PS["pbY0"] if half == 0 else PS["pbY1"]) | rg("mask8"),
                    wr=rg(tAn, half * 512, (half + 1) * 512), tag=f"L{li}.Am{t}.{half}")
            for h in range(NH):
                pg = PS["pbY0"] if h < 4 else PS["pbY1"]
                mm(pbY[:, h * 128:(h + 1) * 128], tA[:, h * 128:(h + 1) * 128],
                   v_v[:, h * 128:(h + 1) * 128],
                   rg(tAn, h * 128, (h + 1) * 128) | rg(vn, h * 128, (h + 1) * 128),
                   pg, start=True, stop=(t == 0), tag=f"L{li}.Y{t}.{h}")
                if t > 0:
                    mm(pbY[:, h * 128:(h + 1) * 128], q_h[:, h * 128:(h + 1) * 128],
                       Sbf[:, h * 128:(h + 1) * 128],
                       rg(qn, h * 128, (h + 1) * 128) | rg("Sbf", h * 128, (h + 1) * 128),
                       pg, start=False, stop=True, tag=f"L{li}.Yc{t}.{h}")
            for half, (ps, pg) in enumerate(((pbA, PS["pbA"]), (pbB, PS["pbB"]))):
                dve("tt", Sst[:, half * 512:(half + 1) * 512],
                    Sst[:, half * 512:(half + 1) * 512], ps[:, :], ALU.add,
                    rd=rg("Sst", half * 512, (half + 1) * 512) | pg,
                    wr=rg("Sst", half * 512, (half + 1) * 512), tag=f"L{li}.Su{t}.{half}")
            for h in range(NH):
                dve("ts1", Sst[:, h * 128:(h + 1) * 128], Sst[:, h * 128:(h + 1) * 128],
                    GAMC[h], ALU.mult, rd=rg("Sst", h * 128, (h + 1) * 128),
                    wr=rg("Sst", h * 128, (h + 1) * 128), tag=f"L{li}.Sg{t}.{h}")
            dve("cp", Sbf[:, :], Sst[:, :], rd=rg("Sst"), wr=rg("Sbf"), tag=f"L{li}.Sb{t}")
            act(None, sQ[:, :], pbY[:, :], PS["pbY"], rg(sQn, 0, 1024), AF.Square,
                tag=f"L{li}.gsq{t}")
            dve("red", stats[:, gnb:gnb + 8], pbY[:, :].rearrange("p (h e) -> p h e", h=8),
                rd=PS["pbY"], wr=rg("stats", gnb, gnb + 8), tag=f"L{li}.gr{t}")
            dve("red", stats[:, gnb + 8:gnb + 16], sQ[:, :].rearrange("p (h e) -> p h e", h=8),
                rd=rg(sQn, 0, 1024), wr=rg("stats", gnb + 8, gnb + 16))
            mu8, rs8 = stats8(gnb, gnb + 8, gnb + 16, gnb + 24, tag=f"L{li}.g8{t}")
            for h in range(NH):
                pg = PS["pbY0"] if h < 4 else PS["pbY1"]
                dve("ts", tA[:, h * 128:(h + 1) * 128], pbY[:, h * 128:(h + 1) * 128],
                    stats[:, gnb + 16 + h:gnb + 17 + h],
                    stats[:, gnb + 24 + h:gnb + 25 + h], ALU.subtract, ALU.mult,
                    rd=pg | rg("stats", gnb + 16, gnb + 32), wr=rg(tAn, h * 128, (h + 1) * 128),
                    tag=f"L{li}.gn{t}.{h}")
            for h in range(NH):
                tp(pbT[:, h * 128:(h + 1) * 128], tA[:, h * 128:(h + 1) * 128],
                   rg(tAn, h * 128, (h + 1) * 128) | rg("ident"), PS["pbT"],
                   tag=f"L{li}.gt{t}.{h}")
            for h in range(NH):
                act(None, sQb[:, h * 128:(h + 1) * 128], pbT[:, h * 128:(h + 1) * 128],
                    PS["pbT"], rg(sQn, 0, 512), AF.Identity,
                    bias=gnv[:, li * 16 + 8 + h:li * 16 + 8 + h + 1],
                    scale=gnv[:, li * 16 + h:li * 16 + h + 1], tag=f"L{li}.ga{t}.{h}")
            g_fm = set()
            for fb in range(8):
                g_fm |= rg("fmaj", fb * 2048 + tc0, fb * 2048 + tc1)
            dve("tt", fmaj[:, :, tc0:tc1],
                sQb[:, 0:1024].rearrange("p (h c) -> p h c", h=8),
                fmaj[:, :, tc0:tc1], ALU.mult,
                rd=rg(sQn, 0, 512) | g_fm, wr=g_fm, tag=f"L{li}.z{t}")

    def attn_layer_tail(li):
        dma("POOL", wowg[:, :, :], wo.ap()[li].rearrange("(dt p) c -> p dt c", p=128),
            set(), rg("wowg"), tag=f"L{li}.wo")
        for t in range(NT):
            par = t % 2
            xr = xres if par == 0 else x1f
            g_xr = rg("xres") if par == 0 else (rg("qh") | rg("kh"))
            tc0, tc1 = t * 128, (t + 1) * 128
            dma("SP", xr[:, :], spillX[tc0:tc1, :], H("spX", t), g_xr, tag=f"L{li}.rX{t}")
            for half in range(2):
                ps, pg = (pbA, PS["pbA"]) if half == 0 else (pbB, PS["pbB"])
                for vt in range(8):
                    mm(ps[:, :], fmaj[:, vt, tc0:tc1],
                       wowg[:, vt, half * 512:(half + 1) * 512],
                       rg("fmaj", vt * 2048 + tc0, vt * 2048 + tc1)
                       | rg("wowg", vt * 1024 + half * 512, vt * 1024 + (half + 1) * 512),
                       pg, start=(vt == 0), stop=(vt == 7), tag=f"L{li}.O{t}.{half}.{vt}")
                dve("tt", stream[:, t, half * 512:(half + 1) * 512], ps[:, :],
                    xr[:, half * 512:(half + 1) * 512], ALU.add,
                    rd=pg | g_xr,
                    wr=rg("stream", t * 1024 + half * 512, t * 1024 + (half + 1) * 512),
                    tag=f"L{li}.Oe{t}.{half}")
        for t in range(NT):
            par = t % 2
            tA, tAn = SCR[par]["tmA"]
            strt = stream[:, t, :]
            tc0, tc1 = t * 128, (t + 1) * 128
            dma("SP", spillY[tc0:tc1, :], strt, G_STR[t], H("spY", t), tag=f"L{li}.spY{t}")
            mu, rs = stats_chain(strt, G_STR[t], 1024, (t % 4) * 4, tag=f"L{li}.B{t}")
            g_st = rg("stats", (t % 4) * 4, (t % 4) * 4 + 4)
            dve("ts", tA[:, :], strt, mu, rs, ALU.subtract, ALU.mult,
                rd=G_STR[t] | g_st, wr=rg(tAn), tag=f"L{li}.Bn{t}")
            for dt in range(8):
                tp(pbT[:, dt * 128:(dt + 1) * 128], tA[:, dt * 128:(dt + 1) * 128],
                   rg(tAn, dt * 128, (dt + 1) * 128) | rg("ident"), PS["pbT"],
                   tag=f"L{li}.Bt{t}.{dt}")
            for dt in range(8):
                act(None, xnt_slice(dt, tc0, tc1), pbT[:, dt * 128:(dt + 1) * 128],
                    PS["pbT"], g_xnt(dt, tc0, tc1), AF.Identity,
                    bias=lnv[:, li * 32 + 24 + dt:li * 32 + 24 + dt + 1],
                    scale=lnv[:, li * 32 + 16 + dt:li * 32 + 16 + dt + 1],
                    tag=f"L{li}.Be{t}.{dt}")
        dma("POOL", w1b[:, :, :], w1.ap()[li].rearrange("(dt p) c -> p dt c", p=128),
            set(), rg("w1b"), tag=f"L{li}.w1")
        for ch in range(4):
            cc0, cc1 = ch * 512, (ch + 1) * 512
            for fb in range(32):
                ps, pg = (pbA, PS["pbA"]) if fb % 2 == 0 else (pbB, PS["pbB"])
                for dt in range(8):
                    mm(ps[:, :], w1b[:, dt, fb * 128:(fb + 1) * 128],
                       xnT[:, dt, cc0:cc1],
                       rg("w1b", dt * 4096 + fb * 128, dt * 4096 + (fb + 1) * 128)
                       | g_xnt(dt, cc0, cc1),
                       pg, start=(dt == 0), stop=(dt == 7), tag=f"L{li}.F1.{ch}.{fb}.{dt}")
                sl4 = fb % 4
                slb, sln, slo = wcrot[sl4 // 2], f"wcrot{sl4 // 2}", (sl4 % 2) * 512
                act(None, slb[:, slo:slo + 512], ps[:, :], pg,
                    rg(sln, slo, slo + 512), AF.Gelu,
                    bias=b1c[:, li * 32 + fb:li * 32 + fb + 1],
                    tag=f"L{li}.F1g.{ch}.{fb}")
                dma("SP", spillH.ap()[ch * 4:(ch + 1) * 4, :, fb, :]
                    .rearrange("tt p c -> p tt c"),
                    slb[:, slo:slo + 512].rearrange("p (tt c) -> p tt c", tt=4),
                    rg(sln, slo, slo + 512), H("spH", ch * 32 + fb),
                    tag=f"L{li}.F1s.{ch}.{fb}")
        dma("POOL", w2b[:, :, :], w2.ap()[li].rearrange("(ft p) c -> p ft c", p=128),
            set(), rg("w2b"), tag=f"L{li}.w2")
        hload0 = nc.alloc_sbuf_tensor_at(f"hload0{li}", [128, 32, 128], BF16,
                                         offset=views["tmA"][1])
        hload1 = nc.alloc_sbuf_tensor_at(f"hload1{li}", [128, 32, 128], BF16,
                                         offset=WOW0)
        g_h0 = rg("tmA") | rg("qh") | rg("kh") | rg("vv")
        g_h1 = rg("sqdB") | rg("tmAB") | rg("qhB")
        for t in range(NT):
            par = t % 2
            hl = hload0 if par == 0 else hload1
            g_hl = g_h0 if par == 0 else g_h1
            xr = xres if par == 0 else xresB
            g_xr = rg("xres") if par == 0 else rg("xresB")
            tc0, tc1 = t * 128, (t + 1) * 128
            ch = t // 4
            dma("SP", hl[:, :, :], spillH.ap()[t, :, :, :],
                {("h", "spH", ch * 32 + fb) for fb in range(32)}, g_hl,
                tag=f"L{li}.F2l{t}")
            dma("SP", xr[:, :], spillY[tc0:tc1, :], H("spY", t), g_xr,
                tag=f"L{li}.rY{t}")
            for half in range(2):
                ps, pg = (pbA, PS["pbA"]) if half == 0 else (pbB, PS["pbB"])
                for ft in range(32):
                    mm(ps[:, :], hl[:, ft, :],
                       w2b[:, ft, half * 512:(half + 1) * 512],
                       g_hl | rg("w2b", ft * 1024 + half * 512, ft * 1024 + (half + 1) * 512),
                       pg, start=(ft == 0), stop=False, tag=f"L{li}.F2.{t}.{half}.{ft}")
                mm(ps[:, :], rows[li * 32:li * 32 + 1, 0:128],
                   rows[li * 32:li * 32 + 1, 128 + half * 512:128 + (half + 1) * 512],
                   rg("rows"), pg, start=False, stop=True,
                   tag=f"L{li}.F2b.{t}.{half}")
                dve("tt", stream[:, t, half * 512:(half + 1) * 512], ps[:, :],
                    xr[:, half * 512:(half + 1) * 512], ALU.add,
                    rd=pg | g_xr,
                    wr=rg("stream", t * 1024 + half * 512, t * 1024 + (half + 1) * 512),
                    tag=f"L{li}.F2e.{t}.{half}")

    if stages >= 5:
        attn_layer(0)
    if stages >= 6:
        attn_layer_tail(0)
    if stages >= 7:
        attn_layer(1)
        attn_layer_tail(1)

    for t in range(NT if stages >= 8 else 0):
        tc0, tc1 = t * 128, (t + 1) * 128
        strt = stream[:, t, :]
        dma("SP", xres[:, :], x0[tc0:tc1, :], set(), rg("xres"), tag=f"x1.l{t}")
        mu, rs = stats_chain(strt, G_STR[t], 1024, (t % 4) * 4, tag=f"x1.a{t}")
        g_st = rg("stats", (t % 4) * 4, (t % 4) * 4 + 4)
        dve("ts", sqd[:, :], strt, mu, rs, ALU.subtract, ALU.mult,
            rd=G_STR[t] | g_st, wr=rg("sqd"), tag=f"x1.n{t}")
        dve("tt", strt, sqd[:, :], xres[:, :], ALU.add,
            rd=rg("sqd") | rg("xres"), wr=G_STR[t], tag=f"x1.s{t}")
        dma("SP", spillX[tc0:tc1, :], strt, G_STR[t], H("spX", t), tag=f"x1.sp{t}")
        mu2, rs2 = stats_chain(strt, G_STR[t], 1024, (t % 4) * 4, tag=f"x1.b{t}")
        tA, tAn = SCR[t % 2]["tmA"]
        dve("ts", tA[:, :], strt, mu2, rs2, ALU.subtract, ALU.mult,
            rd=G_STR[t] | g_st, wr=rg(tAn), tag=f"x1.n2{t}")
        for dt in range(8):
            tp(pbT[:, dt * 128:(dt + 1) * 128], tA[:, dt * 128:(dt + 1) * 128],
               rg(tAn, dt * 128, (dt + 1) * 128) | rg("ident"), PS["pbT"],
               tag=f"x1.t{t}.{dt}")
        for dt in range(8):
            dve("cp", xnt_slice(dt, tc0, tc1), pbT[:, dt * 128:(dt + 1) * 128],
                rd=PS["pbT"], wr=g_xnt(dt, tc0, tc1), tag=f"x1.e{t}.{dt}")

    if stages >= 9:
        full9 = stages == 9
        dma("SP", wteb[:, :, :], wtexpd.ap(), Hr("wtexpd", 0, 8), rg("wteb"), tag="fs.wte")
        GT = 256
        h1g2 = nc.alloc_sbuf_tensor_at("h1g2", [128, 32, GT], BF16, offset=views["fmaj"][1])
        zg2 = nc.alloc_sbuf_tensor_at("zg2", [128, 32, GT], BF16,
                                      offset=views["fmaj"][1] + 16384)
        def g_h1(gt, c0, c1):
            return rg("fmaj", gt * GT + c0, gt * GT + c1)
        def g_zg(gt, c0, c1):
            return rg("fmaj", 8192 + gt * GT + c0, 8192 + gt * GT + c1)

        def fs_e(gi, gt):
            gc0, gc1 = gi * GT, (gi + 1) * GT
            ps, pg = (pbA, PS["pbA"]) if gt % 2 == 0 else (pbB, PS["pbB"])
            for dt in range(8):
                mm(ps[:, 0:GT], wteb[:, dt, gt * 128:(gt + 1) * 128],
                   xnT[:, dt, gc0:gc1],
                   rg("wteb", dt * 4096 + gt * 128, dt * 4096 + (gt + 1) * 128)
                   | g_xnt(dt, gc0, gc1), pg,
                   start=(dt == 0), stop=(dt == 7), tag=f"fs.e{gi}.{gt}.{dt}")
            act(None, h1g2[:, gt, :], ps[:, 0:GT], pg, g_h1(gt, 0, GT),
                AF.Identity, bias=btldc[:, gt:gt + 1], tag=f"fs.h1{gi}.{gt}")

        def fs_c2(gi, gt):
            ps, pg = (pbC, PS["pbC"]) if gt % 2 == 0 else (pbD, PS["pbD"])
            mm(ps[:, 0:GT], c2up[:, gt, :], h1g2[:, gt, :],
               rg("c2up", gt * 128, (gt + 1) * 128) | g_h1(gt, 0, GT),
               pg, tag=f"fs.c2{gi}.{gt}")
            act(None, zg2[:, gt, :], ps[:, 0:GT], pg, g_zg(gt, 0, GT),
                AF.Gelu, bias=upbg[:, gt:gt + 1], tag=f"fs.z{gi}.{gt}")

        def fs_y(gi, gt):
            wcb = wcrot[gt % 8]
            wcn = f"wcrot{gt % 8}"
            srcap = wtcond.ap().rearrange("(r2 gtx r1) d -> r1 r2 gtx d", r2=32, gtx=32)[:, :, gt, :]
            dma("SP", wcb[:, :], srcap, Hr("wtcond", 0, 32), rg(wcn),
                tag=f"fs.wc{gi}.{gt}")
            for tt in range(2):
                pslist = ((pbY, PS["pbY0"], 0), (pbY, PS["pbY1"], 512)) if tt == 0 else \
                         ((pbC, PS["pbC"], 0), (pbD, PS["pbD"], 512))
                for ps, pg, off in pslist:
                    o = ps[:, off:off + 512] if ps is pbY else ps[:, :]
                    mm(o, zg2[:, gt, tt * 128:(tt + 1) * 128], wcb[:, off:off + 512],
                       g_zg(gt, tt * 128, (tt + 1) * 128) | rg(wcn, off, off + 512),
                       pg, start=(gt == 0), stop=False, tag=f"fs.y{gi}.{gt}.{tt}.{off}")

        def fs_tail(gi):
            for tt in range(2):
                t = gi * 2 + tt
                tc0, tc1 = t * 128, (t + 1) * 128
                par = t % 2
                xr1 = x1f if par == 0 else xresB
                g_xr1 = (rg("qh") | rg("kh")) if par == 0 else rg("xresB")
                pslist = ((pbY, PS["pbY0"], 0), (pbY, PS["pbY1"], 512)) if tt == 0 else \
                         ((pbC, PS["pbC"], 0), (pbD, PS["pbD"], 512))
                dma("SP", xr1[:, :], spillX[tc0:tc1, :], H("spX", t), g_xr1,
                    tag=f"fs.x1{t}")
                for ps, pg, off in pslist:
                    o = ps[:, off:off + 512] if ps is pbY else ps[:, :]
                    mm(o, rows[64:65, 0:128], rows[64:65, 128 + off:128 + off + 512],
                       rg("rows"), pg, start=False, stop=False, tag=f"fs.yb{t}.{off}")
                    mm(o, rows[64:65, 0:128], rows[64:65, 1152 + off:1152 + off + 512],
                       rg("rows"), pg, start=False, stop=True, tag=f"fs.yd{t}.{off}")
                    dve("cp", xres[:, off:off + 512], o, rd=pg, wr=rg("xres", off, off + 512),
                        tag=f"fs.ye{t}.{off}")
                mu, rs = stats_chain(xres[:, :], rg("xres"), 1024, 80, tag=f"fs.st{t}")
                g_st = rg("stats", 80, 84)
                dve("ts", sqd[:, :], xres[:, :], mu, rs, ALU.subtract, ALU.mult,
                    rd=rg("xres") | g_st, wr=rg("sqd"), tag=f"fs.n{t}")
                dve("tt", outstg[:, :], sqd[:, :], xr1[:, :], ALU.add,
                    rd=rg("sqd") | g_xr1, wr=rg("Sst"), tag=f"fs.o{t}")
                dma("SP", out_t[tc0:tc1, :], outstg[:, :], rg("Sst"), H("out", t),
                    tag=f"fs.s{t}")

        NG = L // GT if full9 else 1
        for gi in range(NG + 1):
            for gt in range(32):
                if gi < NG:
                    fs_e(gi, gt)
                if gi > 0 and (full9 or stages >= 93):
                    fs_y(gi - 1, gt)
            if gi > 0 and (full9 or stages >= 95):
                fs_tail(gi - 1)
            if gi < NG and (full9 or stages >= 92):
                for gt in range(32):
                    fs_c2(gi, gt)
    P.add("SP", lambda e: e.nop(), {("h", "out", t) for t in range(NT)}, set(),
          tag="final")

    P.finalize(nc)
    return nc, P


def _host_prep(P):
    bf = ml_dtypes.bfloat16
    g2f = f_of_g()
    up_w = np.asarray(P["L0_ffn_up_weights"], np.float32)
    dn_w = np.asarray(P["L0_ffn_down_weights"], np.float32)

    def deint(w):
        wd = np.ascontiguousarray(np.transpose(w, (1, 0, 2)))
        wd = np.concatenate([wd[:, :, ::2], wd[:, :, 1::2]], axis=2)
        return wd.reshape(1024, 1024)

    wqkv = np.stack([
        np.concatenate([deint(np.asarray(P["L0_attn_Wq"][li], np.float32)),
                        deint(np.asarray(P["L0_attn_Wk"][li], np.float32)),
                        np.transpose(np.asarray(P["L0_attn_Wv"][li], np.float32),
                                     (1, 0, 2)).reshape(1024, 1024)], axis=1)
        for li in range(2)])

    ftile = lambda t: np.arange(t * 128, (t + 1) * 128)
    gtile = lambda t: g2f[t * 128:(t + 1) * 128]
    fs1u = factor_stack(up_w, range(0, 7), ftile, False)
    fs2u = factor_stack(up_w, range(7, 12), gtile, False)
    fs2d = factor_stack(dn_w, range(7, 12), gtile, True)
    fs1d = factor_stack(dn_w, range(0, 7), ftile, True)

    lnarr = np.stack([np.stack([np.asarray(P[k][li], np.float32) for k in
                                ("L0_attn_ln1_w", "L0_attn_ln1_b",
                                 "L0_attn_ln2_w", "L0_attn_ln2_b")])
                      for li in range(2)])
    lnv = np.ascontiguousarray(lnarr.reshape(2, 4, 8, 128).transpose(3, 0, 1, 2)
                               .reshape(128, 64))
    gnarr = np.stack([np.stack([np.asarray(P[k][li], np.float32) for k in
                                ("L0_attn_gn_w", "L0_attn_gn_b")]) for li in range(2)])
    gnv = np.ascontiguousarray(gnarr.reshape(2, 2, 8, 128).transpose(3, 0, 1, 2)
                               .reshape(128, 32))
    b1c = np.ascontiguousarray(np.asarray(P["L0_attn_ffn_b1"], np.float32)
                               .reshape(2, 32, 128).transpose(2, 0, 1).reshape(128, 64))
    bexp = np.ascontiguousarray(np.asarray(P["L0_ffn_b_exp"], np.float32)
                                .reshape(32, 128).T)
    upbg = np.ascontiguousarray(np.asarray(P["L0_ffn_up_bias"], np.float32)[g2f]
                                .reshape(32, 128).T)
    rows = np.zeros((128, 2176), np.float32)
    rows[0, 0:128] = 1.0
    rows[32, 0:128] = 1.0
    rows[64, 0:128] = 1.0
    rows[0, 128:1152] = np.asarray(P["L0_attn_ffn_b2"][0], np.float32)
    rows[32, 128:1152] = np.asarray(P["L0_attn_ffn_b2"][1], np.float32)
    rows[64, 128:1152] = np.asarray(P["L0_ffn_b_con"], np.float32)
    rows[64, 1152:2176] = np.asarray(P["L0_ffn_down_bias"], np.float32)
    tri = np.triu(np.ones((128, 128), np.float32)).astype(bf)
    com = {
        "wqkv": wqkv,
        "wg": np.asarray(P["L0_attn_W_G"], np.float32),
        "wo": np.asarray(P["L0_attn_W_O"], np.float32),
        "w1": np.asarray(P["L0_attn_ffn_w1"], np.float32),
        "w2": np.asarray(P["L0_attn_ffn_w2"], np.float32),
        "wexpT": np.ascontiguousarray(np.asarray(P["L0_ffn_W_exp"], np.float32).T),
        "wcong": np.ascontiguousarray(np.asarray(P["L0_ffn_W_con"], np.float32)[g2f]),
        "fs1u": fs1u, "fs2u": fs2u, "fs2d": fs2d, "fs1d": fs1d,
        "lnv": lnv, "gnv": gnv, "b1c": b1c, "bexp": bexp, "upbg": upbg,
        "rows": rows,
        "identbf": np.eye(128, dtype=np.float32).astype(bf),
        "mask8": np.tile(np.triu(np.ones((128, 128), np.float32)), (1, 8)),
        "jrep": np.broadcast_to(np.arange(64, dtype=np.float32), (128, 64)).copy(),
        "posg": (np.arange(16, dtype=np.float32)[None, :] * 128
                 + np.arange(128, dtype=np.float32)[:, None]).copy(),
        "hrep": np.broadcast_to(np.arange(8, dtype=np.float32), (128, 8)).copy(),
    }
    return com


_NC_CACHE = {}


def kernel(**inputs):
    com = _host_prep(inputs)
    if "nc" not in _NC_CACHE:
        _NC_CACHE["nc"] = build_program()[0]
    nc = _NC_CACHE["nc"]
    X = np.asarray(inputs["L0_attn_X"], np.float32)
    in_maps = [dict(com, x0=np.ascontiguousarray(X[b])) for b in range(NCORES)]
    res = run_bass_kernel_spmd(nc, in_maps, core_ids=list(range(NCORES)))
    return np.stack([np.asarray(res.results[b]["out"], np.float32)
                     for b in range(NCORES)])


# version 15
# speedup vs baseline: 1.0702x; 1.0702x over previous
import os
import numpy as np
import ml_dtypes
import concourse.bass as bass
import concourse.mybir as mybir
from concourse.bass_utils import run_bass_kernel_spmd

F32 = mybir.dt.float32
BF16 = mybir.dt.bfloat16
AF = mybir.ActivationFunctionType
ALU = mybir.AluOpType
AX = mybir.AxisListType

NCORES = 8
L = 2048
D = 1024
FFN = 4096
NT = 16
NH = 8
EPS = 1e-5
GRAN = 256


def g_of_f():
    f = np.arange(4096)
    hi, lo = f // 128, f % 128
    return 128 * (lo // 4) + 32 * (lo % 4) + hi


def f_of_g():
    g = g_of_f()
    inv = np.empty_like(g)
    inv[g] = np.arange(4096)
    return inv


def stage_factor(weights, stages, s, I, J):
    I = np.asarray(I)
    J = np.asarray(J)
    sbits = 0
    for t in stages:
        sbits |= 1 << t
    done = 0
    for t in stages:
        if t == s:
            break
        done |= 1 << t
    ok = (I & ~sbits) == (J & ~sbits)
    s2 = 1 << s
    Z = (I & done) | (J & ~done)
    c_out = (I >> s) & 1
    c_in = (J >> s) & 1
    k = ((Z >> (s + 1)) << s) | (Z & (s2 - 1))
    vals = weights[s][k, c_out, c_in]
    return np.where(ok, vals, np.float32(0.0)).astype(np.float32)


def factor_stack(weights, stages, tile_idx, rows_are_out):
    stages = list(stages)
    out = np.zeros((len(stages), 128, 32 * 128), np.float32)
    for t in range(32):
        idx = tile_idx(t)
        for si, s in enumerate(stages):
            if rows_are_out:
                blk = stage_factor(weights, stages, s, idx[:, None], idx[None, :])
            else:
                blk = stage_factor(weights, stages, s, idx[None, :], idx[:, None])
            out[si, :, t * 128:(t + 1) * 128] = blk
    return out


ENGS = ("TEN", "DVE", "ACT", "POOL", "SP")


class Op:
    __slots__ = ("eng", "fn", "reads", "writes", "dma", "idx", "dma_no",
                 "need_inc", "inc_no", "deps", "tag", "cost", "seq", "waits")

    def __init__(self, eng, fn, reads, writes, dma, tag="", cost=300.0):
        self.eng = eng
        self.fn = fn
        self.reads = reads
        self.writes = writes
        self.dma = dma
        self.need_inc = False
        self.inc_no = None
        self.deps = []
        self.tag = tag
        self.cost = cost
        self.seq = None


class Prog:
    def __init__(self):
        self.ops = {e: [] for e in ENGS}
        self.reg_w = {}
        self.reg_r = {}
        self.known = {e: {} for e in ENGS}

    def _dep(self, op, ref, kind):
        peng, pidx, pdma = ref
        if peng == op.eng and not pdma and not op.dma and peng == "TEN":
            kind = "order"
        op.deps.append((ref, kind))

    TEN_CHAIN = True

    def add(self, eng, fn, reads=(), writes=(), dma=False, tag="", cost=300.0):
        idx = len(self.ops[eng])
        op = Op(eng, fn, reads, writes, dma, tag, cost)
        if self.TEN_CHAIN and eng in () and idx > 0:
            op.deps.append(((eng, idx - 1, False), "order"))
        deps = {}
        for g in reads:
            w = self.reg_w.get(g)
            if w is not None:
                deps[w] = "raw"
        for g in writes:
            w = self.reg_w.get(g)
            if w is not None and w not in deps:
                deps[w] = "waw"
            for r in self.reg_r.get(g, ()):
                if r not in deps:
                    deps[r] = "war"
        myref = (eng, idx, dma)
        for ref, kind in deps.items():
            if ref == myref:
                continue
            self._dep(op, ref, kind)
        for g in writes:
            self.reg_w[g] = myref
            self.reg_r[g] = []
        for g in reads:
            self.reg_r.setdefault(g, []).append(myref)
        op.idx = idx
        op.dma_no = None
        self.ops[eng].append(op)
        return op

    NSLOT = 8

    WINDOW_NS = float(os.environ.get('WIN_NS', 60000.0))

    def schedule(self):
        import heapq
        allops = []
        for e in ENGS:
            allops.extend(self.ops[e])
        idmap = {}
        for e in ENGS:
            for i, o in enumerate(self.ops[e]):
                idmap[(e, i)] = o
        succs = {id(o): [] for o in allops}
        npred = {id(o): 0 for o in allops}
        for o in allops:
            for (ref, kind) in o.deps:
                p = idmap[(ref[0], ref[1])]
                succs[id(p)].append(o)
                npred[id(o)] += 1
        avail = {e: 0.0 for e in ENGS}
        finish = {}
        ready = []
        ctr = 0
        rank = {}
        for e in ENGS:
            for i, o in enumerate(self.ops[e]):
                rank[id(o)] = len(rank)
        rt = {}
        rtc = {e: 0.0 for e in ENGS}
        scale = {e: 1.0 for e in ENGS}
        for e in ENGS:
            for o in self.ops[e]:
                rt[id(o)] = rtc[e]
                rtc[e] += o.cost
        W = self.WINDOW_NS
        dep_finish = {id(o): 0.0 for o in allops}
        for o in allops:
            if npred[id(o)] == 0:
                heapq.heappush(ready, (max(0.0, rt[id(o)] - W), rank[id(o)], id(o), o))
        order = []
        emitted = {e: 0 for e in ENGS}
        while ready:
            est, rk, oid, o = heapq.heappop(ready)
            t0 = max(avail[o.eng], dep_finish[oid], rt[oid] - W)
            t1 = t0 + o.cost
            if o.dma:
                avail[o.eng] = t0 + 150.0
                fin = t1 + 1600.0
            else:
                avail[o.eng] = t1
                fin = t1
            finish[oid] = fin
            order.append(o)
            for s in succs[oid]:
                sid = id(s)
                dep_finish[sid] = max(dep_finish[sid], fin)
                npred[sid] -= 1
                if npred[sid] == 0:
                    heapq.heappush(ready, (max(dep_finish[sid], rt[sid] - W),
                                           rank[sid], sid, s))
        assert len(order) == len(allops), (len(order), len(allops))
        streams = {e: [] for e in ENGS}
        for o in order:
            o.seq = len(streams[o.eng])
            streams[o.eng].append(o)
        return streams

    def finalize(self, nc, reorder=True):
        if reorder:
            streams = self.schedule()
        else:
            streams = {e: list(self.ops[e]) for e in ENGS}
            for e in ENGS:
                for i, o in enumerate(streams[e]):
                    o.seq = i
        for e in ENGS:
            n = 0
            for o in streams[e]:
                if o.dma:
                    o.dma_no = n
                    n += 1
        for e in ENGS:
            known = {}
            for o in streams[e]:
                o.waits = []
                for (ref, kind) in o.deps:
                    if kind == "order":
                        continue
                    peng, pidx, pdma = ref
                    p = self.ops[peng][pidx]
                    if peng == e and not pdma and not o.dma and p.seq < o.seq:
                        pass
                    if pdma:
                        k = ("d", peng, p.dma_no % self.NSLOT)
                        v = 16 * (p.dma_no // self.NSLOT + 1)
                    else:
                        k = ("e", peng)
                        v = p.seq
                    if known.get(k, -1) >= v:
                        continue
                    known[k] = v
                    o.waits.append((peng, pidx, pdma))
                    if not pdma:
                        p.need_inc = True
        incno = {}
        for e in ENGS:
            n = 0
            for o in streams[e]:
                if (not o.dma) and o.need_inc:
                    n += 1
                    o.inc_no = n
        sems = {}
        dsems = {}
        for e in ENGS:
            sems[e] = nc.alloc_semaphore("sem_" + e)
            if any(o.dma for o in streams[e]):
                dsems[e] = [nc.alloc_semaphore(f"dsem_{e}{s}") for s in range(self.NSLOT)]

        def emit_for(engobj, e):
            known = {}

            def waitv(sem, key, v):
                if known.get(key, -1) < v:
                    known[key] = v
                    engobj.wait_ge(sem, v)
            for o in streams[e]:
                for (pe, pi, pdma) in o.waits:
                    p = self.ops[pe][pi]
                    if pdma:
                        waitv(dsems[pe][p.dma_no % self.NSLOT],
                              ("d", pe, p.dma_no % self.NSLOT),
                              16 * (p.dma_no // self.NSLOT + 1))
                    else:
                        v = p.inc_no
                        assert v is not None, (e, o.tag, pe, pi)
                        waitv(sems[pe], ("e", pe), v)
                if o.dma:
                    sl = o.dma_no % self.NSLOT
                    if o.dma_no >= self.NSLOT:
                        waitv(dsems[e][sl], ("d", e, sl), 16 * (o.dma_no // self.NSLOT))
                inst = o.fn(engobj)
                if o.tag:
                    inst.annotate(o.tag)
                if o.dma:
                    inst.then_inc(dsems[e][o.dma_no % self.NSLOT], 16)
                elif o.need_inc:
                    inst.then_inc(sems[e], 1)
            ndma = sum(1 for o in streams[e] if o.dma)
            for sl in range(min(self.NSLOT, ndma)):
                cnt = (ndma - sl + self.NSLOT - 1) // self.NSLOT
                waitv(dsems[e][sl], ("d", e, sl), 16 * cnt)

        with nc.Block() as block:
            @block.tensor
            def _(t):
                emit_for(t, "TEN")

            @block.vector
            def _(v):
                emit_for(v, "DVE")

            @block.scalar
            def _(s):
                emit_for(s, "ACT")

            @block.gpsimd
            def _(g):
                emit_for(g, "POOL")

            @block.sync
            def _(sy):
                emit_for(sy, "SP")
        return sems, dsems


class View:

    def __init__(self, h, off, itemsize):
        self.h = h
        self.off = off
        self.it = itemsize

    def g(self, c0=0, c1=None):
        n = self.h.shape[-1] if c1 is None else c1
        tot = 1
        for s in self.h.shape[1:-1]:
            tot *= s
        b0 = self.off + c0 * self.it
        b1 = self.off + n * self.it
        return set(range(b0 // GRAN, (b1 + GRAN - 1) // GRAN))


HREG_CNT = [0]


def hreg(name):
    return {("hbm", name)}


def hregs(name, i0, i1):
    return {("hbm", name, i) for i in range(i0, i1)}


TWOPI = float(2 * np.pi)
HALFPI = float(np.pi / 2)
LN1E4 = float(np.log(10000.0))
GAMMAS = (1.0 - np.exp(np.linspace(np.log(1 / 32), np.log(1 / 512), NH))).astype(np.float64)
GAMC = [float(g ** 128) for g in GAMMAS]


def build_program(stages=9, reorder=False):
    nc = bass.Bass("TRN2", target_bir_lowering=False)

    di = {}
    def inp(name, shape, dt=F32):
        di[name] = nc.dram_tensor(name, list(shape), dt, kind="ExternalInput")
        return di[name]

    x0 = inp("x0", (L, D))
    wqkv = inp("wqkv", (2, D, 3 * D))
    wg = inp("wg", (2, D, D))
    wo = inp("wo", (2, D, D))
    w1 = inp("w1", (2, D, FFN))
    w2 = inp("w2", (2, FFN, D))
    wexpT = inp("wexpT", (FFN, D))
    wcong = inp("wcong", (FFN, D))
    fs1u = inp("fs1u", (7, 128, FFN))
    fs2u = inp("fs2u", (5, 128, FFN))
    fs2d = inp("fs2d", (5, 128, FFN))
    fs1d = inp("fs1d", (7, 128, FFN))
    lnv_i = inp("lnv", (128, 64))
    gnv_i = inp("gnv", (128, 32))
    b1c_i = inp("b1c", (128, 64))
    bexp_i = inp("bexp", (128, 32))
    upbg_i = inp("upbg", (128, 32))
    rows_i = inp("rows", (128, 2176))
    esel_i = inp("esel", (128, 384))
    ident_i = inp("identbf", (128, 128), BF16)
    mask_i = inp("mask8", (128, 1024))
    jrep_i = inp("jrep", (128, 64))
    posg_i = inp("posg", (128, 16))
    hrep_i = inp("hrep", (128, 8))
    out_t = nc.dram_tensor("out", [L, D], F32, kind="ExternalOutput")

    spillX = nc.dram_tensor("spillX", [L, D], F32, kind="Internal")
    spillY = nc.dram_tensor("spillY", [L, D], F32, kind="Internal")
    spillH = nc.dram_tensor("spillH", [NT, 128, 32, 128], BF16, kind="Internal")
    wtexpd = nc.dram_tensor("wtexpd", [8, 128, FFN], BF16, kind="Internal")
    amidd = nc.dram_tensor("amidd", [FFN, D], BF16, kind="Internal")
    wtcond = nc.dram_tensor("wtcond", [FFN, D], BF16, kind="Internal")

    SB0 = (nc.sbuf_base + 31) // 32 * 32
    cur = [SB0]
    views = {}

    def sb(name, shape, dt, at=None, size=None):
        nbytes = int(np.prod(shape[1:])) * (2 if dt == BF16 else 4)
        if at is None:
            at = cur[0]
            cur[0] += (nbytes + 31) // 32 * 32
        h = nc.alloc_sbuf_tensor_at(name, list(shape), dt, offset=at)
        views[name] = (h, at, nbytes)
        return h

    def rg(name, e0=None, e1=None):
        h, at, nbytes = views[name]
        it = 2 if h.dtype == BF16 else 4
        b0 = at if e0 is None else at + e0 * it
        b1 = at + nbytes if e1 is None else at + e1 * it
        return set(range(b0 // GRAN, (b1 + GRAN - 1) // GRAN))

    ident = sb("ident", (128, 128), BF16)
    mask8 = sb("mask8", (128, 1024), F32)
    tqc = sb("tqc", (128, 16, 64), BF16)
    tqs = sb("tqs", (128, 16, 64), BF16)
    tkc = sb("tkc", (128, 16, 64), BF16)
    tks = sb("tks", (128, 16, 64), BF16)
    gamw = sb("gamw", (128, 8, 128), BF16)
    gamiw = sb("gamiw", (128, 8, 128), BF16)
    c2up = sb("c2up", (128, 32, 128), BF16)
    lnv = sb("lnv", (128, 64), F32)
    gnv = sb("gnv", (128, 32), F32)
    b1c = sb("b1c", (128, 64), BF16)
    bexpc = sb("bexpc", (128, 32), BF16)
    upbg = sb("upbg", (128, 32), BF16)
    btldc = sb("btldc", (128, 32), BF16)
    rows = sb("rows", (128, 2176), F32)
    esel = sb("esel", (128, 384), F32)
    stats = sb("stats", (128, 144), F32)
    sqd = sb("sqd", (128, 1024), F32)
    sqd_bf = sb("sqd_bf", (128, 2048), BF16, at=views["sqd"][1])
    tmA = sb("tmA", (128, 1024), BF16)
    qh = sb("qh", (128, 1024), BF16)
    kh = sb("kh", (128, 1024), BF16)
    vv = sb("vv", (128, 1024), BF16)
    x1f = sb("x1f", (128, 1024), F32, at=views["qh"][1])
    xres = sb("xres", (128, 1024), F32)
    Sst = sb("Sst", (128, 1024), F32)
    outstg = sb("outstg", (128, 1024), F32, at=views["Sst"][1])
    Sbf = sb("Sbf", (128, 1024), BF16)
    hstage = sb("hstage", (128, 1024), BF16)
    wconb = sb("wconb", (128, 1024), BF16)
    wconbB = sb("wconbB", (128, 1024), BF16)
    wowg = sb("wowg", (128, 8, 1024), BF16)
    WOW0 = views["wowg"][1]
    sqdB = sb("sqdB", (128, 1024), F32, at=WOW0)
    sqdB_bf = sb("sqdB_bf", (128, 2048), BF16, at=WOW0)
    tmAB = sb("tmAB", (128, 1024), BF16, at=WOW0 + 4096)
    qhB = sb("qhB", (128, 1024), BF16, at=WOW0 + 6144)
    khB = sb("khB", (128, 1024), BF16, at=WOW0 + 8192)
    vvB = sb("vvB", (128, 1024), BF16, at=WOW0 + 10240)
    xresB = sb("xresB", (128, 1024), F32, at=WOW0 + 12288)
    wcrot = [sb(f"wcrot{i}", (128, 1024), BF16, at=WOW0 + i * 2048) for i in range(8)]
    BIG1 = cur[0]
    stream = sb("stream", (128, 16, 1024), F32)
    wqkvb = sb("wqkvb", (128, 8, 3072), BF16, at=BIG1)
    w1b = sb("w1b", (128, 8, 4096), BF16, at=BIG1)
    wteb = sb("wteb", (128, 8, 4096), BF16, at=BIG1)
    facA = sb("facA", (128, 4096), F32, at=BIG1)
    facB = sb("facB", (128, 4096), F32, at=BIG1 + 16384)
    comp1 = sb("comp1", (128, 4096), BF16, at=BIG1 + 32768)
    fstg = sb("fstg", (128, 4096), BF16, at=BIG1 + 40960)
    XNT0 = cur[0]
    xnT = sb("xnT", (128, 8, 2048), BF16)
    fmaj = sb("fmaj", (128, 8, 2048), BF16)
    w2b = sb("w2b", (128, 32, 1024), BF16, at=XNT0)
    h1g = sb("h1g", (128, 32, 256), BF16, at=views["fmaj"][1])
    zg = sb("zg", (128, 32, 256), BF16, at=views["fmaj"][1] + 16384)
    assert cur[0] <= 229000, (cur[0], SB0)
    import sys; print('SBUF cur:', cur[0], 'base:', SB0, file=sys.stderr)

    pbA = nc.alloc_psum_tensor("pbA", [128, 512], F32)
    pbB = nc.alloc_psum_tensor("pbB", [128, 512], F32)
    pbC = nc.alloc_psum_tensor("pbC", [128, 512], F32)
    pbD = nc.alloc_psum_tensor("pbD", [128, 512], F32)
    pbT = nc.alloc_psum_tensor("pbT", [128, 1024], BF16)
    pbM = nc.alloc_psum_tensor("pbM", [128, 1024], BF16)
    pbY = nc.alloc_psum_tensor("pbY", [128, 1024], F32)
    PS = {"pbA": {("ps", 0)}, "pbB": {("ps", 1)}, "pbC": {("ps", 2)},
          "pbD": {("ps", 3)}, "pbT": {("ps", 4)}, "pbM": {("ps", 5)},
          "pbY": {("ps", 6), ("ps", 7)}, "pbY0": {("ps", 6)}, "pbY1": {("ps", 7)}}

    P = Prog()

    def H(name, i=0):
        return {("h", name, i)}

    def Hr(name, i0, i1):
        return {("h", name, i) for i in range(i0, i1)}

    def dma(q, dst, src, reads, writes, tag=""):
        try:
            nb = dst.nbytes() if hasattr(dst, "nbytes") else 262144
        except Exception:
            nb = 262144
        P.add(q, lambda e, d=dst, s=src: e.dma_start(d, s), reads, writes, dma=True,
              tag=tag, cost=float(nb) / 350.0)

    def mm(outap, lhs, rhs, rd, wr, start=True, stop=True, tag=""):
        try:
            ncols = outap.shape[-1]
        except Exception:
            ncols = 512
        P.add("TEN", lambda e, o=outap, l=lhs, r=rhs, st=start, sp=stop:
              e.matmul(o, l, r, start=st, stop=sp), rd, wr, tag=tag,
              cost=110.0 + 0.45 * ncols)

    def tp(outap, inap, rd, wr, tag=""):
        P.add("TEN", lambda e, o=outap, i=inap: e.transpose(o, i, ident[:, :]),
              rd, wr, tag=tag, cost=190.0)

    def act(fn, outap, inap, rd, wr, func, bias=0.0, scale=1.0, accum=None, tag=""):
        try:
            fd = 1
            for s_ in outap.shape[1:]:
                fd *= s_
        except Exception:
            fd = 512
        P.add("ACT", lambda e, o=outap, i=inap, f=func, b=bias, s=scale, a=accum:
              e.activation(o, i, f, bias=b, scale=s, accum_out=a), rd, wr, tag=tag,
              cost=170.0 + fd / 2.0)

    def dve(kind, *a, rd=(), wr=(), tag=""):
        try:
            fd = 1
            for s_ in a[0].shape[1:]:
                fd *= s_
        except Exception:
            fd = 512
        _c = 70.0 + fd * (1.0 if kind in ("tt", "stt", "red") else 0.5)
        kwargs = dict(rd=rd, wr=wr, tag=tag)
        if kind == "tt":
            o, i0, i1, op = a
            P.add("DVE", lambda e: e.tensor_tensor(o, i0, i1, op), rd, wr, tag=tag, cost=_c)
        elif kind == "ts":
            o, i0, s1, s2, op0, op1 = a
            P.add("DVE", lambda e: e.tensor_scalar(o, i0, s1, s2, op0, op1), rd, wr, tag=tag, cost=_c)
        elif kind == "ts1":
            o, i0, s1, op0 = a
            P.add("DVE", lambda e: e.tensor_scalar(o, i0, s1, None, op0), rd, wr, tag=tag, cost=_c)
        elif kind == "stt":
            o, i0, s, i1, op0, op1 = a
            P.add("DVE", lambda e: e.scalar_tensor_tensor(o, i0, s, i1, op0, op1), rd, wr, tag=tag, cost=_c)
        elif kind == "red":
            o, i0 = a
            P.add("DVE", lambda e: e.reduce_sum(o, i0, axis=AX.X), rd, wr, tag=tag, cost=_c)
        elif kind == "cp":
            o, i0 = a
            P.add("DVE", lambda e: e.tensor_copy(o, i0), rd, wr, tag=tag, cost=_c)
        elif kind == "memset":
            o, val = a
            P.add("DVE", lambda e: e.memset(o, val), rd, wr, tag=tag, cost=_c)

    def stats_chain(tileap, tile_rd, W, c, tag=""):
        sc = stats[:, c:c + 1]
        sc2 = stats[:, c + 1:c + 2]
        sc3 = stats[:, c + 2:c + 3]
        g_c = rg("stats", c, c + 4)
        dve("red", sc, tileap, rd=tile_rd, wr=g_c, tag=tag + ".sum")
        act(None, sqd[:, 0:W], tileap, tile_rd, rg("sqd", 0, W) | g_c, AF.Square,
            accum=sc3, tag=tag + ".sq")
        dve("ts", sc, sc, 1.0 / W, None, ALU.mult, ALU.bypass, rd=g_c, wr=g_c, tag=tag + ".mu")
        dve("ts", sc3, sc3, 1.0 / W, EPS, ALU.mult, ALU.add, rd=g_c, wr=g_c, tag=tag + ".m2")
        dve("ts1", sc2, sc, -1.0, ALU.mult, rd=g_c, wr=g_c, tag=tag + ".nmu")
        P.add("DVE", lambda e, o=sc2, i0=sc, s=sc2, i1=sc3:
              e.scalar_tensor_tensor(o, i0, s, i1, ALU.mult, ALU.add), g_c, g_c, tag=tag + ".var")
        act(None, sc2, sc2, g_c, g_c, AF.Ln, tag=tag + ".ln")
        act(None, sc2, sc2, g_c, g_c, AF.Exp, scale=-0.5, tag=tag + ".rstd")
        return sc, sc2

    dma("SP", ident[:, :], ident_i[:, :], set(), rg("ident"), tag="ld.ident")
    dma("SP", mask8[:, :], mask_i[:, :], set(), rg("mask8"), tag="ld.mask")
    dma("SP", lnv[:, :], lnv_i[:, :], set(), rg("lnv"))
    dma("SP", gnv[:, :], gnv_i[:, :], set(), rg("gnv"))
    dma("POOL", b1c[:, :], b1c_i[:, :], set(), rg("b1c"))
    dma("POOL", bexpc[:, :], bexp_i[:, :], set(), rg("bexpc"))
    dma("POOL", upbg[:, :], upbg_i[:, :], set(), rg("upbg"))
    dma("SP", rows[:, :], rows_i[:, :], set(), rg("rows"))
    dma("SP", esel[:, :], esel_i[:, :], set(), rg("esel"))
    dma("SP", stats[:, 0:16], posg_i[:, :], set(), rg("stats", 0, 16))
    dma("SP", xres[:, 0:64], jrep_i[:, :], set(), rg("xres", 0, 64))
    dma("SP", xres[:, 64:72], hrep_i[:, :], set(), rg("xres", 64, 72))
    P.add("POOL", lambda e: e.iota(stats[:, 16:17], [[0, 1]], channel_multiplier=1,
                                   allow_small_or_imprecise_dtypes=True),
          set(), rg("stats", 16, 17), tag="iota.p")

    jr = xres[:, 0:64]
    g_j = rg("xres", 0, 64)
    invf = xres[:, 128:192]
    lnb = xres[:, 192:256]
    g_iv = rg("xres", 128, 256)
    act(None, invf, jr, g_j, g_iv, AF.Exp, scale=-LN1E4 / 64.0, tag="tab.invf")
    dve("ts", lnb, jr, 2.0 / 179.2, 51.2 / 179.2, ALU.mult, ALU.add, rd=g_j, wr=g_iv, tag="tab.b")
    act(None, lnb, lnb, g_iv, g_iv, AF.Ln, tag="tab.lnb")
    posb = stats[:, 0:16].unsqueeze(2).broadcast_to([128, 16, 64])
    g_pos = rg("stats", 0, 17)
    g_sq = rg("sqd")
    g_ws = rg("qh") | rg("kh")
    ivb = invf.unsqueeze(1).broadcast_to([128, 16, 64])
    ang = sqd[:, :].rearrange("p (t j) -> p t j", t=16)
    angc = Sst[:, 0:1024].rearrange("p (t j) -> p t j", t=16)
    g_ss = rg("Sst")
    dve("tt", ang, posb, ivb, ALU.mult, rd=g_pos | g_iv, wr=g_sq, tag="tab.ang")
    dve("ts1", Sst[:, 0:1024], sqd[:, :], HALFPI, ALU.add, rd=g_sq, wr=g_ss, tag="tab.angc")
    for tgt, g_t in ((sqd[:, :], g_sq), (Sst[:, 0:1024], g_ss)):
        kf = x1f[:, :]
        dve("ts", kf, tgt, 1.0 / TWOPI, 0.5, ALU.mult, ALU.add, rd=g_t, wr=g_ws)
        dve("cp", kf.bitcast(mybir.dt.int32), kf, rd=g_ws, wr=g_ws, tag="tab.ri")
        dve("cp", kf, kf.bitcast(mybir.dt.int32), rd=g_ws, wr=g_ws)
        dve("stt", tgt, kf, -TWOPI, tgt, ALU.mult, ALU.add, rd=g_t | g_ws, wr=g_t)
    act(None, sqd[:, :], sqd[:, :], g_sq, g_sq, AF.Sin, tag="tab.sin")
    act(None, Sst[:, 0:1024], Sst[:, 0:1024], g_ss, g_ss, AF.Sin, tag="tab.cos")
    lnsc = x1f[:, :].rearrange("p (t j) -> p t j", t=16)
    dve("tt", lnsc, posb, lnb.unsqueeze(1).broadcast_to([128, 16, 64]), ALU.mult,
        rd=g_pos | g_iv, wr=g_ws, tag="tab.lnsc")
    dve("ts1", x1f[:, :], x1f[:, :], 1.0 / 512.0, ALU.mult, rd=g_ws, wr=g_ws)
    sinv = sqd[:, :]
    cosv = Sst[:, 0:1024]
    for nm, srcv, g_src, sgn in (("tqc", cosv, g_ss, 1.0), ("tqs", sinv, g_sq, 1.0),
                                 ("tkc", cosv, g_ss, -1.0), ("tks", sinv, g_sq, -1.0)):
        t_v = views[nm][0]
        tv = t_v[:, :, :].rearrange("p t j -> p (t j)")
        act(None, tv, x1f[:, :], g_ws, rg(nm), AF.Exp, scale=sgn, tag="tab.e" + nm)
        dve("tt", tv, tv, srcv, ALU.mult, rd=rg(nm) | g_src, wr=rg(nm), tag="tab.m" + nm)
    hr = xres[:, 64:72]
    g_h = rg("xres", 64, 72)
    lng = stats[:, 32:40]
    g_lng = rg("stats", 32, 40)
    dve("ts", lng, hr, float((np.log(1 / 512) - np.log(1 / 32)) / 7), float(np.log(1 / 32)),
        ALU.mult, ALU.add, rd=g_h, wr=g_lng, tag="gam.lin")
    act(None, lng, lng, g_lng, g_lng, AF.Exp)
    dve("ts", lng, lng, -1.0, 1.0, ALU.mult, ALU.add, rd=g_lng, wr=g_lng)
    act(None, lng, lng, g_lng, g_lng, AF.Ln)
    lngb = lng.unsqueeze(2).broadcast_to([128, 8, 128])
    pc = stats[:, 16:17]
    dve("ts1", sqd[:, :].rearrange("p (h e) -> p h e", h=8), lngb, pc, ALU.mult,
        rd=g_lng | g_pos, wr=g_sq, tag="gam.pe")
    act(None, gamw[:, :, :].rearrange("p h e -> p (h e)"), sqd[:, :], g_sq, rg("gamw"), AF.Exp)
    act(None, gamiw[:, :, :].rearrange("p h e -> p (h e)"), sqd[:, :], g_sq, rg("gamiw"),
        AF.Exp, scale=-1.0)

    facC = sb("facC", (128, 4096), F32, at=BIG1 + 49152)

    def compose(fsrc, nst, dst_ap, dst_rg, tag):
        dma("SP", facA[:, :], fsrc[0], set(), rg("facA"), tag=tag + ".l0")
        bufs = [(facB, rg("facB")), (facC, rg("facC"))]
        dma("POOL", bufs[0][0][:, :], fsrc[1], set(), bufs[0][1], tag=tag + ".l1")
        for s in range(1, nst):
            cu, cu_g = bufs[(s - 1) % 2]
            if s + 1 < nst:
                nx, nx_g = bufs[s % 2]
                dma("SP" if s % 2 else "POOL", nx[:, :], fsrc[s + 1], set(), nx_g,
                    tag=tag + f".l{s+1}")
            last = s == nst - 1
            o = dst_ap if last else facA[:, :]
            og = dst_rg if last else rg("facA")
            dve("tt", o, facA[:, :], cu[:, :], ALU.mult, rd=rg("facA") | cu_g, wr=og,
                tag=tag + f".m{s}")

    if stages >= 2:
     compose(fs2u, 5, c2up[:, :, :].rearrange("p t c -> p (t c)"), rg("c2up"), "c2u")
     compose(fs1u, 7, comp1[:, :], rg("comp1"), "c1u")

    G_STR = [rg("stream", t * 1024, (t + 1) * 1024) for t in range(NT)]

    def xnt_slice(dt, c0, c1):
        return xnT[:, dt, c0:c1]

    def g_xnt(dt, c0, c1):
        base = dt * 2048
        return rg("xnT", base + c0, base + c1)

    def g_fmaj(fb, c0, c1):
        base = fb * 2048
        return rg("fmaj", base + c0, base + c1)

    def stats8(c_in, c_in2, c_mu, c_rstd, tag):
        s_sum = stats[:, c_in:c_in + 8]
        s_sq = stats[:, c_in2:c_in2 + 8]
        s_mu = stats[:, c_mu:c_mu + 8]
        s_rs = stats[:, c_rstd:c_rstd + 8]
        g_all = rg("stats", c_in, c_rstd + 8)
        dve("ts", s_mu, s_sum, 1.0 / 128, None, ALU.mult, ALU.bypass, rd=g_all, wr=g_all, tag=tag + ".mu")
        dve("ts", s_sq, s_sq, 1.0 / 128, EPS, ALU.mult, ALU.add, rd=g_all, wr=g_all)
        dve("ts", s_rs, s_mu, -1.0, None, ALU.mult, ALU.bypass, rd=g_all, wr=g_all)
        dve("tt", s_rs, s_rs, s_mu, ALU.mult, rd=g_all, wr=g_all)
        dve("tt", s_rs, s_rs, s_sq, ALU.add, rd=g_all, wr=g_all)
        act(None, s_rs, s_rs, g_all, g_all, AF.Ln, tag=tag + ".ln")
        act(None, s_rs, s_rs, g_all, g_all, AF.Exp, scale=-0.5)
        return s_mu, s_rs

    def we_iter(ft):
        w2bf = w2b[:, :, :].rearrange("p a b -> p (a b)")
        wcb, wcn = wcrot[ft % 8], f"wcrot{ft % 8}"
        dma("POOL", wcb[:, :], wexpT[ft * 128:(ft + 1) * 128, :], set(), rg(wcn),
            tag=f"we.l{ft}")
        c1t = comp1[:, ft * 128:(ft + 1) * 128]
        g_c1t = rg("comp1", ft * 128, (ft + 1) * 128)
        for dq in range(2):
            ps, pg = (pbA, PS["pbA"]) if dq == 0 else (pbB, PS["pbB"])
            for di in range(4):
                dt = dq * 4 + di
                mm(ps[:, di * 128:(di + 1) * 128], wcb[:, dt * 128:(dt + 1) * 128], c1t,
                   rg(wcn) | g_c1t, pg, tag=f"we.mm{ft}.{dt}")
            dst = w2bf.rearrange("p (dt a b c) -> p dt a b c", dt=8, a=32, b=4)[:, dq * 4:(dq + 1) * 4, :, :, ft]
            srcp = ps[:, 0:512].rearrange("p (di a b) -> p di a b", di=4, a=32)
            e0 = dq * 4 * 4096
            if dq == 0:
                P.add("DVE", lambda e, o=dst, i=srcp: e.tensor_copy(o, i),
                      pg, rg("w2b", e0, e0 + 16384), tag=f"we.ev{ft}.{dq}")
            else:
                P.add("ACT", lambda e, o=dst, i=srcp: e.activation(o, i, AF.Copy),
                      pg, rg("w2b", e0, e0 + 16384), tag=f"we.ev{ft}.{dq}")
        mm(pbC[0:1, ft % 4 * 128:(ft % 4 + 1) * 128], bexpc[:, ft:ft + 1], c1t,
           rg("bexpc") | g_c1t, PS["pbC"], tag=f"bt.mm{ft}")
        btdst = fstg[0:1, :].rearrange("p (a b c) -> p a b c", a=32, b=4)[:, :, :, ft]
        P.add("ACT", lambda e, o=btdst,
              i=pbC[0:1, ft % 4 * 128:(ft % 4 + 1) * 128].rearrange("p (a b) -> p a b", a=32):
              e.activation(o, i, AF.Copy),
              PS["pbC"], rg("fstg"), tag=f"bt.ev{ft}")

    def we_finish():
        for dt in range(8):
            dma("SP", wtexpd[dt, :, :], w2b[:, :, :].rearrange("p a b -> p (a b)")
                .rearrange("p (dt c) -> p dt c", dt=8)[:, dt, :],
                rg("w2b", dt * 4096, (dt + 1) * 4096), H("wtexpd", dt), tag=f"we.st{dt}")
        for gt in range(32):
            mm(pbD[0:128, gt:gt + 1], fstg[0:1, gt * 128:(gt + 1) * 128], ident[0:1, 0:1],
               rg("fstg") | rg("ident"), PS["pbD"], tag=f"bt.t{gt}")
        dve("cp", btldc[:, :], pbD[:, 0:32], rd=PS["pbD"], wr=rg("btldc"), tag="bt.col")

    def a1_iter(gt):
        wcb, wcn = wcrot[gt % 8], f"wcrot{gt % 8}"
        hsb, hsn = [(hstage, "hstage"), (wconb, "wconb"), (wconbB, "wconbB"), (tmA, "tmA")][gt % 4]
        dma("POOL", wcb[:, :], wcong[gt * 128:(gt + 1) * 128, :], set(), rg(wcn),
            tag=f"a1.l{gt}")
        for half in range(2):
            ps, pg = (pbC, PS["pbC"]) if half == 0 else (pbD, PS["pbD"])
            mm(ps[:, :], comp1[:, gt * 128:(gt + 1) * 128],
               wcb[:, half * 512:(half + 1) * 512],
               rg(wcn) | rg("comp1", gt * 128, (gt + 1) * 128), pg, tag=f"a1.mm{gt}.{half}")
            P.add("ACT", lambda e, o=hsb[:, half * 512:(half + 1) * 512], i=ps[:, :]:
                  e.activation(o, i, AF.Copy), pg,
                  rg(hsn, half * 512, (half + 1) * 512), tag=f"a1.ev{gt}.{half}")
        dma("SP", amidd[gt * 128:(gt + 1) * 128, :], hsb[:, :], rg(hsn),
            H("amidd", gt), tag=f"a1.st{gt}")

    def a2_iter(ft):
        wcb, wcn = wcrot[ft % 8], f"wcrot{ft % 8}"
        hsb, hsn = [(hstage, "hstage"), (wconb, "wconb"), (wconbB, "wconbB"), (tmA, "tmA")][ft % 4]
        srcap = amidd.ap().rearrange("(a b c) d -> a b c d", a=32, b=4)[:, :, ft, :]
        dma("SP", wcb[:, :], srcap, Hr("amidd", 0, 32), rg(wcn), tag=f"a2.l{ft}")
        for half in range(2):
            ps, pg = (pbC, PS["pbC"]) if half == 0 else (pbD, PS["pbD"])
            mm(ps[:, :], comp1[:, ft * 128:(ft + 1) * 128],
               wcb[:, half * 512:(half + 1) * 512],
               rg(wcn) | rg("comp1", ft * 128, (ft + 1) * 128), pg, tag=f"a2.mm{ft}.{half}")
            P.add("ACT", lambda e, o=hsb[:, half * 512:(half + 1) * 512], i=ps[:, :]:
                  e.activation(o, i, AF.Copy), pg,
                  rg(hsn, half * 512, (half + 1) * 512), tag=f"a2.ev{ft}.{half}")
        dma("SP", wtcond[ft * 128:(ft + 1) * 128, :], hsb[:, :], rg(hsn),
            H("wtcond", ft), tag=f"a2.st{ft}")

    def ln0_tile(t):
        tc = (t * 128, (t + 1) * 128)
        dma("SP", xres[:, :], x0[tc[0]:tc[1], :], set(), rg("xres"), tag=f"ln0.l{t}")
        mu, rs = stats_chain(xres[:, :], rg("xres"), 1024, t * 4, tag=f"ln0.{t}")
        dve("ts", stream[:, t, :], xres[:, :], mu, rs, ALU.subtract, ALU.mult,
            rd=rg("xres") | rg("stats", t * 4, t * 4 + 4), wr=G_STR[t],
            tag=f"ln0.n{t}")

    ln0_left = (list(range(0, 8)) + list(range(10, 16))) if stages >= 4 else []
    if stages >= 3:
        for ft in range(32):
            we_iter(ft)
        we_finish()
        compose(fs2d, 5, comp1[:, :], rg("comp1"), "c2d")
        for gt in range(32):
            a1_iter(gt)
        compose(fs1d, 7, comp1[:, :], rg("comp1"), "c1d")
        for ft in range(32):
            a2_iter(ft)
            if ft % 2 == 1 and ln0_left:
                ln0_tile(ln0_left.pop(0))
    ln0_left += [8, 9] if stages >= 4 else []
    while ln0_left:
        ln0_tile(ln0_left.pop(0))

    SCR = [
        {"qh": (qh, "qh"), "kh": (kh, "kh"), "vv": (vv, "vv"),
         "tmA": (tmA, "tmA"), "sqd": (sqd, "sqd"), "sqd_bf": (sqd_bf, "sqd")},
        {"qh": (qhB, "qhB"), "kh": (khB, "khB"), "vv": (vvB, "vvB"),
         "tmA": (tmAB, "tmAB"), "sqd": (sqdB, "sqdB"), "sqd_bf": (sqdB_bf, "sqdB")},
    ]

    def attn_layer(li, sub=99):
        ST_A = []
        for t in range(NT):
            strt = stream[:, t, :]
            dma("SP", spillX[t * 128:(t + 1) * 128, :], strt, G_STR[t], H("spX", t),
                tag=f"L{li}.spX{t}")
            ST_A.append(stats_chain(strt, G_STR[t], 1024, t * 4, tag=f"L{li}.A{t}"))
        for t in range(NT):
            par = t % 2
            tA, tAn = SCR[par]["tmA"]
            strt = stream[:, t, :]
            mu, rs = ST_A[t]
            g_st = rg("stats", t * 4, t * 4 + 4)
            dve("ts", tA[:, :], strt, mu, rs, ALU.subtract, ALU.mult,
                rd=G_STR[t] | g_st, wr=rg(tAn), tag=f"L{li}.An{t}")
            for dt in range(8):
                tp(pbT[:, dt * 128:(dt + 1) * 128], tA[:, dt * 128:(dt + 1) * 128],
                   rg(tAn, dt * 128, (dt + 1) * 128) | rg("ident"), PS["pbT"],
                   tag=f"L{li}.At{t}.{dt}")
            for dt in range(8):
                act(None, xnt_slice(dt, t * 128, (t + 1) * 128),
                    pbT[:, dt * 128:(dt + 1) * 128], PS["pbT"],
                    g_xnt(dt, t * 128, (t + 1) * 128), AF.Identity,
                    bias=lnv[:, li * 32 + 8 + dt:li * 32 + 8 + dt + 1],
                    scale=lnv[:, li * 32 + dt:li * 32 + dt + 1], tag=f"L{li}.Ae{t}.{dt}")
        if sub < 2:
            return
        dma("POOL", wqkvb[:, :, :], wqkv.ap()[li].rearrange("(dt p) c -> p dt c", p=128),
            set(), rg("wqkvb"), tag=f"L{li}.wqkv")
        dma("POOL", wowg[:, :, :], wg.ap()[li].rearrange("(dt p) c -> p dt c", p=128),
            set(), rg("wowg"), tag=f"L{li}.wg")
        for fb in range(8):
            for ch in range(4):
                ps, pg = (pbA, PS["pbA"]) if (fb * 4 + ch) % 2 == 0 else (pbB, PS["pbB"])
                for dt in range(8):
                    mm(ps[:, :], wowg[:, dt, fb * 128:(fb + 1) * 128],
                       xnT[:, dt, ch * 512:(ch + 1) * 512],
                       rg("wowg", dt * 1024 + fb * 128, dt * 1024 + (fb + 1) * 128)
                       | g_xnt(dt, ch * 512, (ch + 1) * 512),
                       pg, start=(dt == 0), stop=(dt == 7), tag=f"L{li}.G{fb}.{ch}.{dt}")
                act(None, fmaj[:, fb, ch * 512:(ch + 1) * 512], ps[:, :], pg,
                    g_fmaj(fb, ch * 512, (ch + 1) * 512), AF.Silu, tag=f"L{li}.Gs{fb}.{ch}")
        if sub < 3:
            return
        nret = NT if sub >= 4 else 2
        dve("memset", Sst[:, :], 0.0, rd=set(), wr=rg("Sst"), tag=f"L{li}.S0")
        dve("memset", Sbf[:, :], 0.0, rd=set(), wr=rg("Sbf"), tag=f"L{li}.Sb0")

        def qkv_tile(t):
            par = t % 2
            q_h, qn = SCR[par]["qh"]
            k_h, kn = SCR[par]["kh"]
            v_v, vn = SCR[par]["vv"]
            tc0, tc1 = t * 128, (t + 1) * 128
            for which, dst, dgn in (("q", q_h, qn), ("k", k_h, kn), ("v", v_v, vn)):
                base = {"q": 0, "k": 1024, "v": 2048}[which]
                for half in range(2):
                    ps, pg = (pbA, PS["pbA"]) if half == 0 else (pbB, PS["pbB"])
                    for dt in range(8):
                        mm(ps[:, :], xnT[:, dt, tc0:tc1],
                           wqkvb[:, dt, base + half * 512:base + (half + 1) * 512],
                           g_xnt(dt, tc0, tc1)
                           | rg("wqkvb", dt * 3072 + base + half * 512,
                                dt * 3072 + base + (half + 1) * 512),
                           pg, start=(dt == 0), stop=(dt == 7),
                           tag=f"L{li}.{which}{t}.{half}.{dt}")
                    act(None, dst[:, half * 512:(half + 1) * 512], ps[:, :], pg,
                        rg(dgn, half * 512, (half + 1) * 512), AF.Copy,
                        tag=f"L{li}.{which}e{t}.{half}")

        def ret_core(t):
            par = t % 2
            q_h, qn = SCR[par]["qh"]
            k_h, kn = SCR[par]["kh"]
            v_v, vn = SCR[par]["vv"]
            tA, tAn = SCR[par]["tmA"]
            sQ, sQn = SCR[par]["sqd"]
            sQb, _ = SCR[par]["sqd_bf"]
            gnb = 64 + 32 * par
            tc0, tc1 = t * 128, (t + 1) * 128
            for src_, sgn, cb, sb_, gw, gwg in (
                    (q_h, qn, tqc, tqs, gamw, rg("gamw")),
                    (k_h, kn, tkc, tks, gamiw, rg("gamiw"))):
                sg = rg(sgn)
                sv = src_[:, :].rearrange("p (h two j) -> p h two j", h=8, two=2)
                e_v = sv[:, :, 0, :]
                o_v = sv[:, :, 1, :]
                Cb = cb[:, t, :].unsqueeze(1).broadcast_to([128, 8, 64])
                Sb = sb_[:, t, :].unsqueeze(1).broadcast_to([128, 8, 64])
                g_tab = rg("tqc" if cb is tqc else "tkc") | rg("tqs" if sb_ is tqs else "tks")
                t1 = sQb[:, 1024:1536].rearrange("p (h j) -> p h j", h=8)
                t2 = sQb[:, 1536:2048].rearrange("p (h j) -> p h j", h=8)
                g_t12 = rg(sQn, 512, 1024)
                dve("tt", t1, e_v, Sb, ALU.mult, rd=sg | g_tab, wr=g_t12)
                dve("tt", t2, o_v, Sb, ALU.mult, rd=sg | g_tab, wr=g_t12, tag=f"L{li}.xpp{t}")
                dve("tt", e_v, e_v, Cb, ALU.mult, rd=sg | g_tab, wr=sg)
                dve("tt", e_v, e_v, t2, ALU.subtract, rd=sg | g_t12, wr=sg)
                dve("tt", o_v, o_v, Cb, ALU.mult, rd=sg | g_tab, wr=sg)
                dve("tt", o_v, o_v, t1, ALU.add, rd=sg | g_t12, wr=sg)
                dve("tt", src_[:, :], src_[:, :],
                    gw[:, :, :].rearrange("p h e -> p (h e)"), ALU.mult,
                    rd=sg | gwg, wr=sg, tag=f"L{li}.xp{t}")
            for h in range(NH):
                ps, pg = (pbA, PS["pbA"]) if h < 4 else (pbB, PS["pbB"])
                mm(ps[:, (h % 4) * 128:(h % 4 + 1) * 128], k_h[:, h * 128:(h + 1) * 128],
                   v_v[:, h * 128:(h + 1) * 128],
                   rg(kn, h * 128, (h + 1) * 128) | rg(vn, h * 128, (h + 1) * 128),
                   pg, tag=f"L{li}.T{t}.{h}")
            for h in range(NH):
                tp(pbT[:, h * 128:(h + 1) * 128], q_h[:, h * 128:(h + 1) * 128],
                   rg(qn, h * 128, (h + 1) * 128) | rg("ident"), PS["pbT"],
                   tag=f"L{li}.tq{t}.{h}")
            dve("cp", q_h[:, :], pbT[:, :], rd=PS["pbT"], wr=rg(qn), tag=f"L{li}.tqe{t}")
            for h in range(NH):
                tp(pbM[:, h * 128:(h + 1) * 128], k_h[:, h * 128:(h + 1) * 128],
                   rg(kn, h * 128, (h + 1) * 128) | rg("ident"), PS["pbM"],
                   tag=f"L{li}.tk{t}.{h}")
            dve("cp", k_h[:, :], pbM[:, :], rd=PS["pbM"], wr=rg(kn), tag=f"L{li}.tke{t}")
            for h in range(NH):
                pg = PS["pbY0"] if h < 4 else PS["pbY1"]
                mm(pbY[:, h * 128:(h + 1) * 128], k_h[:, h * 128:(h + 1) * 128],
                   q_h[:, h * 128:(h + 1) * 128],
                   rg(kn, h * 128, (h + 1) * 128) | rg(qn, h * 128, (h + 1) * 128),
                   pg, tag=f"L{li}.A{t}.{h}")
            for half in range(2):
                dve("tt", tA[:, half * 512:(half + 1) * 512],
                    pbY[:, half * 512:(half + 1) * 512], mask8[:, half * 512:(half + 1) * 512],
                    ALU.mult, rd=(PS["pbY0"] if half == 0 else PS["pbY1"]) | rg("mask8"),
                    wr=rg(tAn, half * 512, (half + 1) * 512), tag=f"L{li}.Am{t}.{half}")
            for h in range(NH):
                pg = PS["pbY0"] if h < 4 else PS["pbY1"]
                mm(pbY[:, h * 128:(h + 1) * 128], tA[:, h * 128:(h + 1) * 128],
                   v_v[:, h * 128:(h + 1) * 128],
                   rg(tAn, h * 128, (h + 1) * 128) | rg(vn, h * 128, (h + 1) * 128),
                   pg, start=True, stop=(t == 0), tag=f"L{li}.Y{t}.{h}")
                if t > 0:
                    mm(pbY[:, h * 128:(h + 1) * 128], q_h[:, h * 128:(h + 1) * 128],
                       Sbf[:, h * 128:(h + 1) * 128],
                       rg(qn, h * 128, (h + 1) * 128) | rg("Sbf", h * 128, (h + 1) * 128),
                       pg, start=False, stop=True, tag=f"L{li}.Yc{t}.{h}")
            for half, (ps, pg) in enumerate(((pbA, PS["pbA"]), (pbB, PS["pbB"]))):
                dve("tt", Sst[:, half * 512:(half + 1) * 512],
                    Sst[:, half * 512:(half + 1) * 512], ps[:, :], ALU.add,
                    rd=rg("Sst", half * 512, (half + 1) * 512) | pg,
                    wr=rg("Sst", half * 512, (half + 1) * 512), tag=f"L{li}.Su{t}.{half}")
            for h in range(NH):
                dve("ts1", Sst[:, h * 128:(h + 1) * 128], Sst[:, h * 128:(h + 1) * 128],
                    GAMC[h], ALU.mult, rd=rg("Sst", h * 128, (h + 1) * 128),
                    wr=rg("Sst", h * 128, (h + 1) * 128), tag=f"L{li}.Sg{t}.{h}")
            dve("cp", Sbf[:, :], Sst[:, :], rd=rg("Sst"), wr=rg("Sbf"), tag=f"L{li}.Sb{t}")
            act(None, sQ[:, :], pbY[:, :], PS["pbY"], rg(sQn, 0, 1024), AF.Square,
                tag=f"L{li}.gsq{t}")
            dve("red", stats[:, gnb:gnb + 8], pbY[:, :].rearrange("p (h e) -> p h e", h=8),
                rd=PS["pbY"], wr=rg("stats", gnb, gnb + 8), tag=f"L{li}.gr{t}")
            dve("red", stats[:, gnb + 8:gnb + 16], sQ[:, :].rearrange("p (h e) -> p h e", h=8),
                rd=rg(sQn, 0, 1024), wr=rg("stats", gnb + 8, gnb + 16))
            mu8, rs8 = stats8(gnb, gnb + 8, gnb + 16, gnb + 24, tag=f"L{li}.g8{t}")
            for h in range(NH):
                pg = PS["pbY0"] if h < 4 else PS["pbY1"]
                dve("ts", tA[:, h * 128:(h + 1) * 128], pbY[:, h * 128:(h + 1) * 128],
                    stats[:, gnb + 16 + h:gnb + 17 + h],
                    stats[:, gnb + 24 + h:gnb + 25 + h], ALU.subtract, ALU.mult,
                    rd=pg | rg("stats", gnb + 16, gnb + 32), wr=rg(tAn, h * 128, (h + 1) * 128),
                    tag=f"L{li}.gn{t}.{h}")
            for h in range(NH):
                tp(pbT[:, h * 128:(h + 1) * 128], tA[:, h * 128:(h + 1) * 128],
                   rg(tAn, h * 128, (h + 1) * 128) | rg("ident"), PS["pbT"],
                   tag=f"L{li}.gt{t}.{h}")
            for h in range(NH):
                act(None, sQb[:, h * 128:(h + 1) * 128], pbT[:, h * 128:(h + 1) * 128],
                    PS["pbT"], rg(sQn, 0, 512), AF.Identity,
                    bias=gnv[:, li * 16 + 8 + h:li * 16 + 8 + h + 1],
                    scale=gnv[:, li * 16 + h:li * 16 + h + 1], tag=f"L{li}.ga{t}.{h}")
            g_fm = set()
            for fb in range(8):
                g_fm |= rg("fmaj", fb * 2048 + tc0, fb * 2048 + tc1)
            dve("tt", fmaj[:, :, tc0:tc1],
                sQb[:, 0:1024].rearrange("p (h c) -> p h c", h=8),
                fmaj[:, :, tc0:tc1], ALU.mult,
                rd=rg(sQn, 0, 512) | g_fm, wr=g_fm, tag=f"L{li}.z{t}")

        for t in range(nret):
            if t == 0:
                qkv_tile(0)
            if t + 1 < nret:
                qkv_tile(t + 1)
            ret_core(t)

    def attn_layer_tail(li):
        dma("POOL", wowg[:, :, :], wo.ap()[li].rearrange("(dt p) c -> p dt c", p=128),
            set(), rg("wowg"), tag=f"L{li}.wo")
        for t in range(NT):
            par = t % 2
            xr = xres if par == 0 else x1f
            g_xr = rg("xres") if par == 0 else (rg("qh") | rg("kh"))
            tc0, tc1 = t * 128, (t + 1) * 128
            dma("SP", xr[:, :], spillX[tc0:tc1, :], H("spX", t), g_xr, tag=f"L{li}.rX{t}")
            for half in range(2):
                ps, pg = (pbA, PS["pbA"]) if half == 0 else (pbB, PS["pbB"])
                for vt in range(8):
                    mm(ps[:, :], fmaj[:, vt, tc0:tc1],
                       wowg[:, vt, half * 512:(half + 1) * 512],
                       rg("fmaj", vt * 2048 + tc0, vt * 2048 + tc1)
                       | rg("wowg", vt * 1024 + half * 512, vt * 1024 + (half + 1) * 512),
                       pg, start=(vt == 0), stop=(vt == 7), tag=f"L{li}.O{t}.{half}.{vt}")
                dve("tt", stream[:, t, half * 512:(half + 1) * 512], ps[:, :],
                    xr[:, half * 512:(half + 1) * 512], ALU.add,
                    rd=pg | g_xr,
                    wr=rg("stream", t * 1024 + half * 512, t * 1024 + (half + 1) * 512),
                    tag=f"L{li}.Oe{t}.{half}")
        ST_B = []
        for t in range(NT):
            strt = stream[:, t, :]
            tc0, tc1 = t * 128, (t + 1) * 128
            dma("SP", spillY[tc0:tc1, :], strt, G_STR[t], H("spY", t), tag=f"L{li}.spY{t}")
            ST_B.append(stats_chain(strt, G_STR[t], 1024, t * 4, tag=f"L{li}.B{t}"))
        for t in range(NT):
            par = t % 2
            tA, tAn = SCR[par]["tmA"]
            strt = stream[:, t, :]
            tc0, tc1 = t * 128, (t + 1) * 128
            mu, rs = ST_B[t]
            g_st = rg("stats", t * 4, t * 4 + 4)
            dve("ts", tA[:, :], strt, mu, rs, ALU.subtract, ALU.mult,
                rd=G_STR[t] | g_st, wr=rg(tAn), tag=f"L{li}.Bn{t}")
            for dt in range(8):
                tp(pbT[:, dt * 128:(dt + 1) * 128], tA[:, dt * 128:(dt + 1) * 128],
                   rg(tAn, dt * 128, (dt + 1) * 128) | rg("ident"), PS["pbT"],
                   tag=f"L{li}.Bt{t}.{dt}")
            for dt in range(8):
                act(None, xnt_slice(dt, tc0, tc1), pbT[:, dt * 128:(dt + 1) * 128],
                    PS["pbT"], g_xnt(dt, tc0, tc1), AF.Identity,
                    bias=lnv[:, li * 32 + 24 + dt:li * 32 + 24 + dt + 1],
                    scale=lnv[:, li * 32 + 16 + dt:li * 32 + 16 + dt + 1],
                    tag=f"L{li}.Be{t}.{dt}")
        dma("POOL", w1b[:, :, :], w1.ap()[li].rearrange("(dt p) c -> p dt c", p=128),
            set(), rg("w1b"), tag=f"L{li}.w1")
        for ch in range(4):
            cc0, cc1 = ch * 512, (ch + 1) * 512
            for fb in range(32):
                ps, pg = (pbA, PS["pbA"]) if fb % 2 == 0 else (pbB, PS["pbB"])
                for dt in range(8):
                    mm(ps[:, :], w1b[:, dt, fb * 128:(fb + 1) * 128],
                       xnT[:, dt, cc0:cc1],
                       rg("w1b", dt * 4096 + fb * 128, dt * 4096 + (fb + 1) * 128)
                       | g_xnt(dt, cc0, cc1),
                       pg, start=(dt == 0), stop=(dt == 7), tag=f"L{li}.F1.{ch}.{fb}.{dt}")
                sl4 = fb % 4
                slb, sln, slo = wcrot[sl4 // 2], f"wcrot{sl4 // 2}", (sl4 % 2) * 512
                act(None, slb[:, slo:slo + 512], ps[:, :], pg,
                    rg(sln, slo, slo + 512), AF.Gelu,
                    bias=b1c[:, li * 32 + fb:li * 32 + fb + 1],
                    tag=f"L{li}.F1g.{ch}.{fb}")
                dma("SP", spillH.ap()[ch * 4:(ch + 1) * 4, :, fb, :]
                    .rearrange("tt p c -> p tt c"),
                    slb[:, slo:slo + 512].rearrange("p (tt c) -> p tt c", tt=4),
                    rg(sln, slo, slo + 512), H("spH", ch * 32 + fb),
                    tag=f"L{li}.F1s.{ch}.{fb}")
        dma("POOL", w2b[:, :, :], w2.ap()[li].rearrange("(ft p) c -> p ft c", p=128),
            set(), rg("w2b"), tag=f"L{li}.w2")
        hload0 = nc.alloc_sbuf_tensor_at(f"hload0{li}", [128, 32, 128], BF16,
                                         offset=views["tmA"][1])
        hload1 = nc.alloc_sbuf_tensor_at(f"hload1{li}", [128, 32, 128], BF16,
                                         offset=WOW0)
        g_h0 = rg("tmA") | rg("qh") | rg("kh") | rg("vv")
        g_h1 = rg("sqdB") | rg("tmAB") | rg("qhB")
        for t in range(NT):
            par = t % 2
            hl = hload0 if par == 0 else hload1
            g_hl = g_h0 if par == 0 else g_h1
            xr = xres if par == 0 else xresB
            g_xr = rg("xres") if par == 0 else rg("xresB")
            tc0, tc1 = t * 128, (t + 1) * 128
            ch = t // 4
            dma("SP", hl[:, :, :], spillH.ap()[t, :, :, :],
                {("h", "spH", ch * 32 + fb) for fb in range(32)}, g_hl,
                tag=f"L{li}.F2l{t}")
            dma("SP", xr[:, :], spillY[tc0:tc1, :], H("spY", t), g_xr,
                tag=f"L{li}.rY{t}")
            for half in range(2):
                ps, pg = (pbA, PS["pbA"]) if half == 0 else (pbB, PS["pbB"])
                for ft in range(32):
                    mm(ps[:, :], hl[:, ft, :],
                       w2b[:, ft, half * 512:(half + 1) * 512],
                       g_hl | rg("w2b", ft * 1024 + half * 512, ft * 1024 + (half + 1) * 512),
                       pg, start=(ft == 0), stop=False, tag=f"L{li}.F2.{t}.{half}.{ft}")
                mm(ps[:, :], esel[:, li * 128:(li + 1) * 128],
                   rows[:, 128 + half * 512:128 + (half + 1) * 512],
                   rg("rows") | rg("esel"), pg, start=False, stop=True,
                   tag=f"L{li}.F2b.{t}.{half}")
                dve("tt", stream[:, t, half * 512:(half + 1) * 512], ps[:, :],
                    xr[:, half * 512:(half + 1) * 512], ALU.add,
                    rd=pg | g_xr,
                    wr=rg("stream", t * 1024 + half * 512, t * 1024 + (half + 1) * 512),
                    tag=f"L{li}.F2e.{t}.{half}")

    if stages >= 5:
        attn_layer(0, sub={51: 1, 52: 2, 53: 3}.get(stages, 99))
    if stages >= 6 and stages < 50:
        attn_layer_tail(0)
    if stages >= 7 and stages < 50:
        attn_layer(1)
        attn_layer_tail(1)

    for t in range(NT if (stages >= 8 and stages < 50) else 0):
        tc0, tc1 = t * 128, (t + 1) * 128
        strt = stream[:, t, :]
        dma("SP", xres[:, :], x0[tc0:tc1, :], set(), rg("xres"), tag=f"x1.l{t}")
        mu, rs = stats_chain(strt, G_STR[t], 1024, t * 4, tag=f"x1.a{t}")
        g_st = rg("stats", t * 4, t * 4 + 4)
        dve("ts", sqd[:, :], strt, mu, rs, ALU.subtract, ALU.mult,
            rd=G_STR[t] | g_st, wr=rg("sqd"), tag=f"x1.n{t}")
        dve("tt", strt, sqd[:, :], xres[:, :], ALU.add,
            rd=rg("sqd") | rg("xres"), wr=G_STR[t], tag=f"x1.s{t}")
        dma("SP", spillX[tc0:tc1, :], strt, G_STR[t], H("spX", t), tag=f"x1.sp{t}")
        mu2, rs2 = stats_chain(strt, G_STR[t], 1024, t * 4 + 64, tag=f"x1.b{t}")
        tA, tAn = SCR[t % 2]["tmA"]
        g_st2 = rg("stats", t * 4 + 64, t * 4 + 68)
        dve("ts", tA[:, :], strt, mu2, rs2, ALU.subtract, ALU.mult,
            rd=G_STR[t] | g_st2, wr=rg(tAn), tag=f"x1.n2{t}")
        for dt in range(8):
            tp(pbT[:, dt * 128:(dt + 1) * 128], tA[:, dt * 128:(dt + 1) * 128],
               rg(tAn, dt * 128, (dt + 1) * 128) | rg("ident"), PS["pbT"],
               tag=f"x1.t{t}.{dt}")
        for dt in range(8):
            dve("cp", xnt_slice(dt, tc0, tc1), pbT[:, dt * 128:(dt + 1) * 128],
                rd=PS["pbT"], wr=g_xnt(dt, tc0, tc1), tag=f"x1.e{t}.{dt}")

    if stages >= 9 and stages < 50:
        full9 = stages == 9
        dma("SP", wteb[:, :, :], wtexpd.ap(), Hr("wtexpd", 0, 8), rg("wteb"), tag="fs.wte")
        GT = 256
        h1g2 = nc.alloc_sbuf_tensor_at("h1g2", [128, 32, GT], BF16, offset=views["fmaj"][1])
        zg2 = nc.alloc_sbuf_tensor_at("zg2", [128, 32, GT], BF16,
                                      offset=views["fmaj"][1] + 16384)
        def g_h1(gt, c0, c1):
            return rg("fmaj", gt * GT + c0, gt * GT + c1)
        def g_zg(gt, c0, c1):
            return rg("fmaj", 8192 + gt * GT + c0, 8192 + gt * GT + c1)

        def fs_e(gi, gt):
            gc0, gc1 = gi * GT, (gi + 1) * GT
            ps, pg = (pbA, PS["pbA"]) if gt % 2 == 0 else (pbB, PS["pbB"])
            for dt in range(8):
                mm(ps[:, 0:GT], wteb[:, dt, gt * 128:(gt + 1) * 128],
                   xnT[:, dt, gc0:gc1],
                   rg("wteb", dt * 4096 + gt * 128, dt * 4096 + (gt + 1) * 128)
                   | g_xnt(dt, gc0, gc1), pg,
                   start=(dt == 0), stop=(dt == 7), tag=f"fs.e{gi}.{gt}.{dt}")
            act(None, h1g2[:, gt, :], ps[:, 0:GT], pg, g_h1(gt, 0, GT),
                AF.Identity, bias=btldc[:, gt:gt + 1], tag=f"fs.h1{gi}.{gt}")

        def fs_c2(gi, gt):
            ps, pg = (pbC, PS["pbC"]) if gt % 2 == 0 else (pbD, PS["pbD"])
            mm(ps[:, 0:GT], c2up[:, gt, :], h1g2[:, gt, :],
               rg("c2up", gt * 128, (gt + 1) * 128) | g_h1(gt, 0, GT),
               pg, tag=f"fs.c2{gi}.{gt}")
            act(None, zg2[:, gt, :], ps[:, 0:GT], pg, g_zg(gt, 0, GT),
                AF.Gelu, bias=upbg[:, gt:gt + 1], tag=f"fs.z{gi}.{gt}")

        def fs_y(gi, gt):
            wcb = wcrot[gt % 8]
            wcn = f"wcrot{gt % 8}"
            srcap = wtcond.ap().rearrange("(r2 gtx r1) d -> r1 r2 gtx d", r2=32, gtx=32)[:, :, gt, :]
            dma("SP", wcb[:, :], srcap, Hr("wtcond", 0, 32), rg(wcn),
                tag=f"fs.wc{gi}.{gt}")
            for tt in range(2):
                pslist = ((pbY, PS["pbY0"], 0), (pbY, PS["pbY1"], 512)) if tt == 0 else \
                         ((pbC, PS["pbC"], 0), (pbD, PS["pbD"], 512))
                for ps, pg, off in pslist:
                    o = ps[:, off:off + 512] if ps is pbY else ps[:, :]
                    mm(o, zg2[:, gt, tt * 128:(tt + 1) * 128], wcb[:, off:off + 512],
                       g_zg(gt, tt * 128, (tt + 1) * 128) | rg(wcn, off, off + 512),
                       pg, start=(gt == 0), stop=False, tag=f"fs.y{gi}.{gt}.{tt}.{off}")

        def fs_tail(gi):
            for tt in range(2):
                t = gi * 2 + tt
                tc0, tc1 = t * 128, (t + 1) * 128
                par = t % 2
                xr1 = x1f if par == 0 else xresB
                g_xr1 = (rg("qh") | rg("kh")) if par == 0 else rg("xresB")
                pslist = ((pbY, PS["pbY0"], 0), (pbY, PS["pbY1"], 512)) if tt == 0 else \
                         ((pbC, PS["pbC"], 0), (pbD, PS["pbD"], 512))
                dma("SP", xr1[:, :], spillX[tc0:tc1, :], H("spX", t), g_xr1,
                    tag=f"fs.x1{t}")
                for ps, pg, off in pslist:
                    o = ps[:, off:off + 512] if ps is pbY else ps[:, :]
                    mm(o, esel[:, 256:384], rows[:, 128 + off:128 + off + 512],
                       rg("rows") | rg("esel"), pg, start=False, stop=False, tag=f"fs.yb{t}.{off}")
                    mm(o, esel[:, 256:384], rows[:, 1152 + off:1152 + off + 512],
                       rg("rows") | rg("esel"), pg, start=False, stop=True, tag=f"fs.yd{t}.{off}")
                    dve("cp", xres[:, off:off + 512], o, rd=pg, wr=rg("xres", off, off + 512),
                        tag=f"fs.ye{t}.{off}")
                mu, rs = stats_chain(xres[:, :], rg("xres"), 1024, 128 + (t % 4) * 4, tag=f"fs.st{t}")
                g_st = rg("stats", 128 + (t % 4) * 4, 132 + (t % 4) * 4)
                dve("ts", sqd[:, :], xres[:, :], mu, rs, ALU.subtract, ALU.mult,
                    rd=rg("xres") | g_st, wr=rg("sqd"), tag=f"fs.n{t}")
                dve("tt", outstg[:, :], sqd[:, :], xr1[:, :], ALU.add,
                    rd=rg("sqd") | g_xr1, wr=rg("Sst"), tag=f"fs.o{t}")
                dma("SP", out_t[tc0:tc1, :], outstg[:, :], rg("Sst"), H("out", t),
                    tag=f"fs.s{t}")

        NG = L // GT if full9 else 1
        for gi in range(NG + 1):
            for gt in range(32):
                if gi < NG:
                    fs_e(gi, gt)
                if gi > 0 and (full9 or stages >= 93):
                    fs_y(gi - 1, gt)
            if gi > 0 and (full9 or stages >= 95):
                fs_tail(gi - 1)
            if gi < NG and (full9 or stages >= 92):
                for gt in range(32):
                    fs_c2(gi, gt)
    P.add("SP", lambda e: e.nop(), {("h", "out", t) for t in range(NT)}, set(),
          tag="final")

    P.finalize(nc, reorder=reorder)
    return nc, P


def _host_prep(P):
    bf = ml_dtypes.bfloat16
    g2f = f_of_g()
    up_w = np.asarray(P["L0_ffn_up_weights"], np.float32)
    dn_w = np.asarray(P["L0_ffn_down_weights"], np.float32)

    def deint(w):
        wd = np.ascontiguousarray(np.transpose(w, (1, 0, 2)))
        wd = np.concatenate([wd[:, :, ::2], wd[:, :, 1::2]], axis=2)
        return wd.reshape(1024, 1024)

    wqkv = np.stack([
        np.concatenate([deint(np.asarray(P["L0_attn_Wq"][li], np.float32)),
                        deint(np.asarray(P["L0_attn_Wk"][li], np.float32)),
                        np.transpose(np.asarray(P["L0_attn_Wv"][li], np.float32),
                                     (1, 0, 2)).reshape(1024, 1024)], axis=1)
        for li in range(2)])

    ftile = lambda t: np.arange(t * 128, (t + 1) * 128)
    gtile = lambda t: g2f[t * 128:(t + 1) * 128]
    fs1u = factor_stack(up_w, range(0, 7), ftile, False)
    fs2u = factor_stack(up_w, range(7, 12), gtile, False)
    fs2d = factor_stack(dn_w, range(7, 12), gtile, True)
    fs1d = factor_stack(dn_w, range(0, 7), ftile, True)

    lnarr = np.stack([np.stack([np.asarray(P[k][li], np.float32) for k in
                                ("L0_attn_ln1_w", "L0_attn_ln1_b",
                                 "L0_attn_ln2_w", "L0_attn_ln2_b")])
                      for li in range(2)])
    lnv = np.ascontiguousarray(lnarr.reshape(2, 4, 8, 128).transpose(3, 0, 1, 2)
                               .reshape(128, 64))
    gnarr = np.stack([np.stack([np.asarray(P[k][li], np.float32) for k in
                                ("L0_attn_gn_w", "L0_attn_gn_b")]) for li in range(2)])
    gnv = np.ascontiguousarray(gnarr.reshape(2, 2, 8, 128).transpose(3, 0, 1, 2)
                               .reshape(128, 32))
    b1c = np.ascontiguousarray(np.asarray(P["L0_attn_ffn_b1"], np.float32)
                               .reshape(2, 32, 128).transpose(2, 0, 1).reshape(128, 64))
    bexp = np.ascontiguousarray(np.asarray(P["L0_ffn_b_exp"], np.float32)
                                .reshape(32, 128).T)
    upbg = np.ascontiguousarray(np.asarray(P["L0_ffn_up_bias"], np.float32)[g2f]
                                .reshape(32, 128).T)
    rows = np.zeros((128, 2176), np.float32)
    rows[0, 0:128] = 1.0
    rows[32, 0:128] = 1.0
    rows[64, 0:128] = 1.0
    rows[0, 128:1152] = np.asarray(P["L0_attn_ffn_b2"][0], np.float32)
    rows[32, 128:1152] = np.asarray(P["L0_attn_ffn_b2"][1], np.float32)
    rows[64, 128:1152] = np.asarray(P["L0_ffn_b_con"], np.float32)
    rows[64, 1152:2176] = np.asarray(P["L0_ffn_down_bias"], np.float32)
    tri = np.triu(np.ones((128, 128), np.float32)).astype(bf)
    esel = np.zeros((128, 384), np.float32)
    esel[0, 0:128] = 1.0
    esel[32, 128:256] = 1.0
    esel[64, 256:384] = 1.0
    com = {
        "esel": esel,
        "wqkv": wqkv,
        "wg": np.asarray(P["L0_attn_W_G"], np.float32),
        "wo": np.asarray(P["L0_attn_W_O"], np.float32),
        "w1": np.asarray(P["L0_attn_ffn_w1"], np.float32),
        "w2": np.asarray(P["L0_attn_ffn_w2"], np.float32),
        "wexpT": np.ascontiguousarray(np.asarray(P["L0_ffn_W_exp"], np.float32).T),
        "wcong": np.ascontiguousarray(np.asarray(P["L0_ffn_W_con"], np.float32)[g2f]),
        "fs1u": fs1u, "fs2u": fs2u, "fs2d": fs2d, "fs1d": fs1d,
        "lnv": lnv, "gnv": gnv, "b1c": b1c, "bexp": bexp, "upbg": upbg,
        "rows": rows,
        "identbf": np.eye(128, dtype=np.float32).astype(bf),
        "mask8": np.tile(np.triu(np.ones((128, 128), np.float32)), (1, 8)),
        "jrep": np.broadcast_to(np.arange(64, dtype=np.float32), (128, 64)).copy(),
        "posg": (np.arange(16, dtype=np.float32)[None, :] * 128
                 + np.arange(128, dtype=np.float32)[:, None]).copy(),
        "hrep": np.broadcast_to(np.arange(8, dtype=np.float32), (128, 8)).copy(),
    }
    return com


_NC_CACHE = {}


def kernel(**inputs):
    com = _host_prep(inputs)
    if "nc" not in _NC_CACHE:
        _NC_CACHE["nc"] = build_program()[0]
    nc = _NC_CACHE["nc"]
    X = np.asarray(inputs["L0_attn_X"], np.float32)
    in_maps = [dict(com, x0=np.ascontiguousarray(X[b])) for b in range(NCORES)]
    res = run_bass_kernel_spmd(nc, in_maps, core_ids=list(range(NCORES)))
    return np.stack([np.asarray(res.results[b]["out"], np.float32)
                     for b in range(NCORES)])


# version 17
# speedup vs baseline: 1.1856x; 1.1079x over previous
import os
import numpy as np
import ml_dtypes
import concourse.bass as bass
import concourse.mybir as mybir
from concourse.bass_utils import run_bass_kernel_spmd

F32 = mybir.dt.float32
BF16 = mybir.dt.bfloat16
AF = mybir.ActivationFunctionType
ALU = mybir.AluOpType
AX = mybir.AxisListType

NCORES = 8
L = 2048
D = 1024
FFN = 4096
NT = 16
NH = 8
EPS = 1e-5
GRAN = 256


def g_of_f():
    f = np.arange(4096)
    hi, lo = f // 128, f % 128
    return 128 * (lo // 4) + 32 * (lo % 4) + hi


def f_of_g():
    g = g_of_f()
    inv = np.empty_like(g)
    inv[g] = np.arange(4096)
    return inv


def stage_factor(weights, stages, s, I, J):
    I = np.asarray(I)
    J = np.asarray(J)
    sbits = 0
    for t in stages:
        sbits |= 1 << t
    done = 0
    for t in stages:
        if t == s:
            break
        done |= 1 << t
    ok = (I & ~sbits) == (J & ~sbits)
    s2 = 1 << s
    Z = (I & done) | (J & ~done)
    c_out = (I >> s) & 1
    c_in = (J >> s) & 1
    k = ((Z >> (s + 1)) << s) | (Z & (s2 - 1))
    vals = weights[s][k, c_out, c_in]
    return np.where(ok, vals, np.float32(0.0)).astype(np.float32)


def factor_stack(weights, stages, tile_idx, rows_are_out):
    stages = list(stages)
    out = np.zeros((len(stages), 128, 32 * 128), np.float32)
    for t in range(32):
        idx = tile_idx(t)
        for si, s in enumerate(stages):
            if rows_are_out:
                blk = stage_factor(weights, stages, s, idx[:, None], idx[None, :])
            else:
                blk = stage_factor(weights, stages, s, idx[None, :], idx[:, None])
            out[si, :, t * 128:(t + 1) * 128] = blk
    return out


ENGS = ("TEN", "DVE", "ACT", "POOL", "SP")


class Op:
    __slots__ = ("eng", "fn", "reads", "writes", "dma", "idx", "dma_no",
                 "need_inc", "inc_no", "deps", "tag", "cost", "seq", "waits")

    def __init__(self, eng, fn, reads, writes, dma, tag="", cost=300.0):
        self.eng = eng
        self.fn = fn
        self.reads = reads
        self.writes = writes
        self.dma = dma
        self.need_inc = False
        self.inc_no = None
        self.deps = []
        self.tag = tag
        self.cost = cost
        self.seq = None


class Prog:
    def __init__(self):
        self.ops = {e: [] for e in ENGS}
        self.reg_w = {}
        self.reg_r = {}
        self.known = {e: {} for e in ENGS}

    def _dep(self, op, ref, kind):
        peng, pidx, pdma = ref
        if peng == op.eng and not pdma and not op.dma and peng == "TEN":
            kind = "order"
        op.deps.append((ref, kind))

    TEN_CHAIN = True

    def add(self, eng, fn, reads=(), writes=(), dma=False, tag="", cost=300.0):
        idx = len(self.ops[eng])
        op = Op(eng, fn, reads, writes, dma, tag, cost)
        if self.TEN_CHAIN and eng in () and idx > 0:
            op.deps.append(((eng, idx - 1, False), "order"))
        deps = {}
        for g in reads:
            w = self.reg_w.get(g)
            if w is not None:
                deps[w] = "raw"
        for g in writes:
            w = self.reg_w.get(g)
            if w is not None and w not in deps:
                deps[w] = "waw"
            for r in self.reg_r.get(g, ()):
                if r not in deps:
                    deps[r] = "war"
        myref = (eng, idx, dma)
        for ref, kind in deps.items():
            if ref == myref:
                continue
            self._dep(op, ref, kind)
        for g in writes:
            self.reg_w[g] = myref
            self.reg_r[g] = []
        for g in reads:
            self.reg_r.setdefault(g, []).append(myref)
        op.idx = idx
        op.dma_no = None
        self.ops[eng].append(op)
        return op

    NSLOT = 8

    WINDOW_NS = float(os.environ.get('WIN_NS', 60000.0))

    def schedule(self):
        import heapq
        allops = []
        for e in ENGS:
            allops.extend(self.ops[e])
        idmap = {}
        for e in ENGS:
            for i, o in enumerate(self.ops[e]):
                idmap[(e, i)] = o
        succs = {id(o): [] for o in allops}
        npred = {id(o): 0 for o in allops}
        for o in allops:
            for (ref, kind) in o.deps:
                p = idmap[(ref[0], ref[1])]
                succs[id(p)].append(o)
                npred[id(o)] += 1
        avail = {e: 0.0 for e in ENGS}
        finish = {}
        ready = []
        ctr = 0
        rank = {}
        for e in ENGS:
            for i, o in enumerate(self.ops[e]):
                rank[id(o)] = len(rank)
        rt = {}
        rtc = {e: 0.0 for e in ENGS}
        scale = {e: 1.0 for e in ENGS}
        for e in ENGS:
            for o in self.ops[e]:
                rt[id(o)] = rtc[e]
                rtc[e] += o.cost
        W = self.WINDOW_NS
        dep_finish = {id(o): 0.0 for o in allops}
        for o in allops:
            if npred[id(o)] == 0:
                heapq.heappush(ready, (max(0.0, rt[id(o)] - W), rank[id(o)], id(o), o))
        order = []
        emitted = {e: 0 for e in ENGS}
        while ready:
            est, rk, oid, o = heapq.heappop(ready)
            t0 = max(avail[o.eng], dep_finish[oid], rt[oid] - W)
            t1 = t0 + o.cost
            if o.dma:
                avail[o.eng] = t0 + 150.0
                fin = t1 + 1600.0
            else:
                avail[o.eng] = t1
                fin = t1
            finish[oid] = fin
            order.append(o)
            for s in succs[oid]:
                sid = id(s)
                dep_finish[sid] = max(dep_finish[sid], fin)
                npred[sid] -= 1
                if npred[sid] == 0:
                    heapq.heappush(ready, (max(dep_finish[sid], rt[sid] - W),
                                           rank[sid], sid, s))
        assert len(order) == len(allops), (len(order), len(allops))
        streams = {e: [] for e in ENGS}
        for o in order:
            o.seq = len(streams[o.eng])
            streams[o.eng].append(o)
        return streams

    def finalize(self, nc, reorder=True):
        if reorder:
            streams = self.schedule()
        else:
            streams = {e: list(self.ops[e]) for e in ENGS}
            for e in ENGS:
                for i, o in enumerate(streams[e]):
                    o.seq = i
        for e in ENGS:
            n = 0
            for o in streams[e]:
                if o.dma:
                    o.dma_no = n
                    n += 1
        for e in ENGS:
            known = {}
            for o in streams[e]:
                o.waits = []
                for (ref, kind) in o.deps:
                    if kind == "order":
                        continue
                    peng, pidx, pdma = ref
                    p = self.ops[peng][pidx]
                    if peng == e and not pdma and not o.dma and p.seq < o.seq:
                        pass
                    if pdma:
                        k = ("d", peng, p.dma_no % self.NSLOT)
                        v = 16 * (p.dma_no // self.NSLOT + 1)
                    else:
                        k = ("e", peng)
                        v = p.seq
                    if known.get(k, -1) >= v:
                        continue
                    known[k] = v
                    o.waits.append((peng, pidx, pdma))
                    if not pdma:
                        p.need_inc = True
        incno = {}
        for e in ENGS:
            n = 0
            for o in streams[e]:
                if (not o.dma) and o.need_inc:
                    n += 1
                    o.inc_no = n
        sems = {}
        dsems = {}
        for e in ENGS:
            sems[e] = nc.alloc_semaphore("sem_" + e)
            if any(o.dma for o in streams[e]):
                dsems[e] = [nc.alloc_semaphore(f"dsem_{e}{s}") for s in range(self.NSLOT)]

        def emit_for(engobj, e):
            known = {}

            def waitv(sem, key, v):
                if known.get(key, -1) < v:
                    known[key] = v
                    engobj.wait_ge(sem, v)
            for o in streams[e]:
                for (pe, pi, pdma) in o.waits:
                    p = self.ops[pe][pi]
                    if pdma:
                        waitv(dsems[pe][p.dma_no % self.NSLOT],
                              ("d", pe, p.dma_no % self.NSLOT),
                              16 * (p.dma_no // self.NSLOT + 1))
                    else:
                        v = p.inc_no
                        assert v is not None, (e, o.tag, pe, pi)
                        waitv(sems[pe], ("e", pe), v)
                if o.dma:
                    sl = o.dma_no % self.NSLOT
                    if o.dma_no >= self.NSLOT:
                        waitv(dsems[e][sl], ("d", e, sl), 16 * (o.dma_no // self.NSLOT))
                inst = o.fn(engobj)
                if o.tag:
                    inst.annotate(o.tag)
                if o.dma:
                    inst.then_inc(dsems[e][o.dma_no % self.NSLOT], 16)
                elif o.need_inc:
                    inst.then_inc(sems[e], 1)
            ndma = sum(1 for o in streams[e] if o.dma)
            for sl in range(min(self.NSLOT, ndma)):
                cnt = (ndma - sl + self.NSLOT - 1) // self.NSLOT
                waitv(dsems[e][sl], ("d", e, sl), 16 * cnt)

        with nc.Block() as block:
            @block.tensor
            def _(t):
                emit_for(t, "TEN")

            @block.vector
            def _(v):
                emit_for(v, "DVE")

            @block.scalar
            def _(s):
                emit_for(s, "ACT")

            @block.gpsimd
            def _(g):
                emit_for(g, "POOL")

            @block.sync
            def _(sy):
                emit_for(sy, "SP")
        return sems, dsems


class View:

    def __init__(self, h, off, itemsize):
        self.h = h
        self.off = off
        self.it = itemsize

    def g(self, c0=0, c1=None):
        n = self.h.shape[-1] if c1 is None else c1
        tot = 1
        for s in self.h.shape[1:-1]:
            tot *= s
        b0 = self.off + c0 * self.it
        b1 = self.off + n * self.it
        return set(range(b0 // GRAN, (b1 + GRAN - 1) // GRAN))


HREG_CNT = [0]


def hreg(name):
    return {("hbm", name)}


def hregs(name, i0, i1):
    return {("hbm", name, i) for i in range(i0, i1)}


TWOPI = float(2 * np.pi)
HALFPI = float(np.pi / 2)
LN1E4 = float(np.log(10000.0))
GAMMAS = (1.0 - np.exp(np.linspace(np.log(1 / 32), np.log(1 / 512), NH))).astype(np.float64)
GAMC = [float(g ** 128) for g in GAMMAS]


def build_program(stages=9, reorder=False):
    nc = bass.Bass("TRN2", target_bir_lowering=False)

    di = {}
    def inp(name, shape, dt=F32):
        di[name] = nc.dram_tensor(name, list(shape), dt, kind="ExternalInput")
        return di[name]

    x0 = inp("x0", (L, D))
    wqkv = inp("wqkv", (2, D, 3 * D))
    wg = inp("wg", (2, D, D))
    wo = inp("wo", (2, D, D))
    w1 = inp("w1", (2, D, FFN))
    w2 = inp("w2", (2, FFN, D))
    wexpT = inp("wexpT", (FFN, D))
    wcong = inp("wcong", (FFN, D))
    fs1u = inp("fs1u", (7, 128, FFN))
    fs2u = inp("fs2u", (5, 128, FFN))
    fs2d = inp("fs2d", (5, 128, FFN))
    fs1d = inp("fs1d", (7, 128, FFN))
    lnv_i = inp("lnv", (128, 64))
    gnv_i = inp("gnv", (128, 32))
    b1c_i = inp("b1c", (128, 64))
    bexp_i = inp("bexp", (128, 32))
    upbg_i = inp("upbg", (128, 32))
    rows_i = inp("rows", (128, 2176))
    esel_i = inp("esel", (128, 384))
    ident_i = inp("identbf", (128, 128), BF16)
    mask_i = inp("mask8", (128, 1024))
    jrep_i = inp("jrep", (128, 64))
    posg_i = inp("posg", (128, 16))
    hrep_i = inp("hrep", (128, 8))
    out_t = nc.dram_tensor("out", [L, D], F32, kind="ExternalOutput")

    spillX = nc.dram_tensor("spillX", [L, D], F32, kind="Internal")
    spillY = nc.dram_tensor("spillY", [L, D], F32, kind="Internal")
    spillH = nc.dram_tensor("spillH", [NT, 128, 32, 128], BF16, kind="Internal")
    wtexpd = nc.dram_tensor("wtexpd", [8, 128, FFN], BF16, kind="Internal")
    amidd = nc.dram_tensor("amidd", [FFN, D], BF16, kind="Internal")
    wtcond = nc.dram_tensor("wtcond", [FFN, D], BF16, kind="Internal")

    SB0 = (nc.sbuf_base + 31) // 32 * 32
    cur = [SB0]
    views = {}

    def sb(name, shape, dt, at=None, size=None):
        nbytes = int(np.prod(shape[1:])) * (2 if dt == BF16 else 4)
        if at is None:
            at = cur[0]
            cur[0] += (nbytes + 31) // 32 * 32
        h = nc.alloc_sbuf_tensor_at(name, list(shape), dt, offset=at)
        views[name] = (h, at, nbytes)
        return h

    def rg(name, e0=None, e1=None):
        h, at, nbytes = views[name]
        it = 2 if h.dtype == BF16 else 4
        b0 = at if e0 is None else at + e0 * it
        b1 = at + nbytes if e1 is None else at + e1 * it
        return set(range(b0 // GRAN, (b1 + GRAN - 1) // GRAN))

    ident = sb("ident", (128, 128), BF16)
    mask8 = sb("mask8", (128, 1024), F32)
    tqc = sb("tqc", (128, 16, 64), BF16)
    tqs = sb("tqs", (128, 16, 64), BF16)
    tkc = sb("tkc", (128, 16, 64), BF16)
    tks = sb("tks", (128, 16, 64), BF16)
    gamw = sb("gamw", (128, 8, 128), BF16)
    gamiw = sb("gamiw", (128, 8, 128), BF16)
    c2up = sb("c2up", (128, 32, 128), BF16)
    lnv = sb("lnv", (128, 64), F32)
    gnv = sb("gnv", (128, 32), F32)
    b1c = sb("b1c", (128, 64), BF16)
    bexpc = sb("bexpc", (128, 32), BF16)
    upbg = sb("upbg", (128, 32), BF16)
    btldc = sb("btldc", (128, 32), BF16)
    rows = sb("rows", (128, 2176), F32)
    esel = sb("esel", (128, 384), F32)
    stats = sb("stats", (128, 144), F32)
    sqd = sb("sqd", (128, 1024), F32)
    sqd_bf = sb("sqd_bf", (128, 2048), BF16, at=views["sqd"][1])
    tmA = sb("tmA", (128, 1024), BF16)
    qh = sb("qh", (128, 1024), BF16)
    kh = sb("kh", (128, 1024), BF16)
    vv = sb("vv", (128, 1024), BF16)
    x1f = sb("x1f", (128, 1024), F32, at=views["qh"][1])
    xres = sb("xres", (128, 1024), F32)
    Sst = sb("Sst", (128, 1024), F32)
    outstg = sb("outstg", (128, 1024), F32, at=views["Sst"][1])
    Sbf = sb("Sbf", (128, 1024), BF16)
    hstage = sb("hstage", (128, 1024), BF16)
    wconb = sb("wconb", (128, 1024), BF16)
    wconbB = sb("wconbB", (128, 1024), BF16)
    wowg = sb("wowg", (128, 8, 1024), BF16)
    WOW0 = views["wowg"][1]
    sqdB = sb("sqdB", (128, 1024), F32, at=WOW0)
    sqdB_bf = sb("sqdB_bf", (128, 2048), BF16, at=WOW0)
    tmAB = sb("tmAB", (128, 1024), BF16, at=WOW0 + 4096)
    qhB = sb("qhB", (128, 1024), BF16, at=WOW0 + 6144)
    khB = sb("khB", (128, 1024), BF16, at=WOW0 + 8192)
    vvB = sb("vvB", (128, 1024), BF16, at=WOW0 + 10240)
    xresB = sb("xresB", (128, 1024), F32, at=WOW0 + 12288)
    wcrot = [sb(f"wcrot{i}", (128, 1024), BF16, at=WOW0 + i * 2048) for i in range(8)]
    BIG1 = cur[0]
    stream = sb("stream", (128, 16, 1024), F32)
    wqkvb = sb("wqkvb", (128, 8, 3072), BF16, at=BIG1)
    w1b = sb("w1b", (128, 8, 4096), BF16, at=BIG1)
    wteb = sb("wteb", (128, 8, 4096), BF16, at=BIG1)
    facA = sb("facA", (128, 4096), F32, at=BIG1)
    facB = sb("facB", (128, 4096), F32, at=BIG1 + 16384)
    comp1 = sb("comp1", (128, 4096), BF16, at=BIG1 + 32768)
    fstg = sb("fstg", (128, 4096), BF16, at=BIG1 + 40960)
    XNT0 = cur[0]
    xnT = sb("xnT", (128, 8, 2048), BF16)
    fmaj = sb("fmaj", (128, 8, 2048), BF16)
    w2b = sb("w2b", (128, 32, 1024), BF16, at=XNT0)
    h1g = sb("h1g", (128, 32, 256), BF16, at=views["fmaj"][1])
    zg = sb("zg", (128, 32, 256), BF16, at=views["fmaj"][1] + 16384)
    assert cur[0] <= 229000, (cur[0], SB0)
    import sys; print('SBUF cur:', cur[0], 'base:', SB0, file=sys.stderr)

    pbA = nc.alloc_psum_tensor("pbA", [128, 512], F32)
    pbB = nc.alloc_psum_tensor("pbB", [128, 512], F32)
    pbC = nc.alloc_psum_tensor("pbC", [128, 512], F32)
    pbD = nc.alloc_psum_tensor("pbD", [128, 512], F32)
    pbT = nc.alloc_psum_tensor("pbT", [128, 1024], BF16)
    pbM = nc.alloc_psum_tensor("pbM", [128, 1024], BF16)
    pbY = nc.alloc_psum_tensor("pbY", [128, 1024], F32)
    PS = {"pbA": {("ps", 0)}, "pbB": {("ps", 1)}, "pbC": {("ps", 2)},
          "pbD": {("ps", 3)}, "pbT": {("ps", 4)}, "pbM": {("ps", 5)},
          "pbY": {("ps", 6), ("ps", 7)}, "pbY0": {("ps", 6)}, "pbY1": {("ps", 7)}}

    P = Prog()

    def H(name, i=0):
        return {("h", name, i)}

    def Hr(name, i0, i1):
        return {("h", name, i) for i in range(i0, i1)}

    def rgs(c0, c1):
        return {("st", c) for c in range(c0, c1)}

    def dma(q, dst, src, reads, writes, tag=""):
        try:
            nb = dst.nbytes() if hasattr(dst, "nbytes") else 262144
        except Exception:
            nb = 262144
        P.add(q, lambda e, d=dst, s=src: e.dma_start(d, s), reads, writes, dma=True,
              tag=tag, cost=float(nb) / 350.0)

    def mm(outap, lhs, rhs, rd, wr, start=True, stop=True, tag=""):
        try:
            ncols = outap.shape[-1]
        except Exception:
            ncols = 512
        P.add("TEN", lambda e, o=outap, l=lhs, r=rhs, st=start, sp=stop:
              e.matmul(o, l, r, start=st, stop=sp), rd, wr, tag=tag,
              cost=110.0 + 0.45 * ncols)

    def tp(outap, inap, rd, wr, tag=""):
        P.add("TEN", lambda e, o=outap, i=inap: e.transpose(o, i, ident[:, :]),
              rd, wr, tag=tag, cost=190.0)

    def act(fn, outap, inap, rd, wr, func, bias=0.0, scale=1.0, accum=None, tag=""):
        try:
            fd = 1
            for s_ in outap.shape[1:]:
                fd *= s_
        except Exception:
            fd = 512
        P.add("ACT", lambda e, o=outap, i=inap, f=func, b=bias, s=scale, a=accum:
              e.activation(o, i, f, bias=b, scale=s, accum_out=a), rd, wr, tag=tag,
              cost=170.0 + fd / 2.0)

    def dve(kind, *a, rd=(), wr=(), tag=""):
        try:
            fd = 1
            for s_ in a[0].shape[1:]:
                fd *= s_
        except Exception:
            fd = 512
        _c = 70.0 + fd * (1.0 if kind in ("tt", "stt", "red") else 0.5)
        kwargs = dict(rd=rd, wr=wr, tag=tag)
        if kind == "tt":
            o, i0, i1, op = a
            P.add("DVE", lambda e: e.tensor_tensor(o, i0, i1, op), rd, wr, tag=tag, cost=_c)
        elif kind == "ts":
            o, i0, s1, s2, op0, op1 = a
            P.add("DVE", lambda e: e.tensor_scalar(o, i0, s1, s2, op0, op1), rd, wr, tag=tag, cost=_c)
        elif kind == "ts1":
            o, i0, s1, op0 = a
            P.add("DVE", lambda e: e.tensor_scalar(o, i0, s1, None, op0), rd, wr, tag=tag, cost=_c)
        elif kind == "stt":
            o, i0, s, i1, op0, op1 = a
            P.add("DVE", lambda e: e.scalar_tensor_tensor(o, i0, s, i1, op0, op1), rd, wr, tag=tag, cost=_c)
        elif kind == "red":
            o, i0 = a
            P.add("DVE", lambda e: e.reduce_sum(o, i0, axis=AX.X), rd, wr, tag=tag, cost=_c)
        elif kind == "cp":
            o, i0 = a
            P.add("DVE", lambda e: e.tensor_copy(o, i0), rd, wr, tag=tag, cost=_c)
        elif kind == "memset":
            o, val = a
            P.add("DVE", lambda e: e.memset(o, val), rd, wr, tag=tag, cost=_c)

    def stats_chain(tileap, tile_rd, W, c, tag=""):
        sc = stats[:, c:c + 1]
        sc2 = stats[:, c + 1:c + 2]
        sc3 = stats[:, c + 2:c + 3]
        g_c = rgs(c, c + 4)
        dve("red", sc, tileap, rd=tile_rd, wr=g_c, tag=tag + ".sum")
        act(None, sqd[:, 0:W], tileap, tile_rd, rg("sqd", 0, W) | g_c, AF.Square,
            accum=sc3, tag=tag + ".sq")
        dve("ts", sc, sc, 1.0 / W, None, ALU.mult, ALU.bypass, rd=g_c, wr=g_c, tag=tag + ".mu")
        dve("ts", sc3, sc3, 1.0 / W, EPS, ALU.mult, ALU.add, rd=g_c, wr=g_c, tag=tag + ".m2")
        dve("ts1", sc2, sc, -1.0, ALU.mult, rd=g_c, wr=g_c, tag=tag + ".nmu")
        P.add("DVE", lambda e, o=sc2, i0=sc, s=sc2, i1=sc3:
              e.scalar_tensor_tensor(o, i0, s, i1, ALU.mult, ALU.add), g_c, g_c, tag=tag + ".var")
        act(None, sc2, sc2, g_c, g_c, AF.Ln, tag=tag + ".ln")
        act(None, sc2, sc2, g_c, g_c, AF.Exp, scale=-0.5, tag=tag + ".rstd")
        return sc, sc2

    dma("SP", ident[:, :], ident_i[:, :], set(), rg("ident"), tag="ld.ident")
    dma("SP", mask8[:, :], mask_i[:, :], set(), rg("mask8"), tag="ld.mask")
    dma("SP", lnv[:, :], lnv_i[:, :], set(), rg("lnv"))
    dma("SP", gnv[:, :], gnv_i[:, :], set(), rg("gnv"))
    dma("POOL", b1c[:, :], b1c_i[:, :], set(), rg("b1c"))
    dma("POOL", bexpc[:, :], bexp_i[:, :], set(), rg("bexpc"))
    dma("POOL", upbg[:, :], upbg_i[:, :], set(), rg("upbg"))
    dma("SP", rows[:, :], rows_i[:, :], set(), rg("rows"))
    dma("SP", esel[:, :], esel_i[:, :], set(), rg("esel"))
    dma("SP", stats[:, 0:16], posg_i[:, :], set(), rgs(0, 16))
    dma("SP", xres[:, 0:64], jrep_i[:, :], set(), rg("xres", 0, 64))
    dma("SP", xres[:, 64:72], hrep_i[:, :], set(), rg("xres", 64, 72))
    P.add("POOL", lambda e: e.iota(stats[:, 16:17], [[0, 1]], channel_multiplier=1,
                                   allow_small_or_imprecise_dtypes=True),
          set(), rgs(16, 17), tag="iota.p")

    jr = xres[:, 0:64]
    g_j = rg("xres", 0, 64)
    invf = xres[:, 128:192]
    lnb = xres[:, 192:256]
    g_iv = rg("xres", 128, 256)
    act(None, invf, jr, g_j, g_iv, AF.Exp, scale=-LN1E4 / 64.0, tag="tab.invf")
    dve("ts", lnb, jr, 2.0 / 179.2, 51.2 / 179.2, ALU.mult, ALU.add, rd=g_j, wr=g_iv, tag="tab.b")
    act(None, lnb, lnb, g_iv, g_iv, AF.Ln, tag="tab.lnb")
    posb = stats[:, 0:16].unsqueeze(2).broadcast_to([128, 16, 64])
    g_pos = rgs(0, 17)
    g_sq = rg("sqd")
    g_ws = rg("qh") | rg("kh")
    ivb = invf.unsqueeze(1).broadcast_to([128, 16, 64])
    ang = sqd[:, :].rearrange("p (t j) -> p t j", t=16)
    angc = Sst[:, 0:1024].rearrange("p (t j) -> p t j", t=16)
    g_ss = rg("Sst")
    dve("tt", ang, posb, ivb, ALU.mult, rd=g_pos | g_iv, wr=g_sq, tag="tab.ang")
    dve("ts1", Sst[:, 0:1024], sqd[:, :], HALFPI, ALU.add, rd=g_sq, wr=g_ss, tag="tab.angc")
    for tgt, g_t in ((sqd[:, :], g_sq), (Sst[:, 0:1024], g_ss)):
        kf = x1f[:, :]
        dve("ts", kf, tgt, 1.0 / TWOPI, 0.5, ALU.mult, ALU.add, rd=g_t, wr=g_ws)
        dve("cp", kf.bitcast(mybir.dt.int32), kf, rd=g_ws, wr=g_ws, tag="tab.ri")
        dve("cp", kf, kf.bitcast(mybir.dt.int32), rd=g_ws, wr=g_ws)
        dve("stt", tgt, kf, -TWOPI, tgt, ALU.mult, ALU.add, rd=g_t | g_ws, wr=g_t)
    act(None, sqd[:, :], sqd[:, :], g_sq, g_sq, AF.Sin, tag="tab.sin")
    act(None, Sst[:, 0:1024], Sst[:, 0:1024], g_ss, g_ss, AF.Sin, tag="tab.cos")
    lnsc = x1f[:, :].rearrange("p (t j) -> p t j", t=16)
    dve("tt", lnsc, posb, lnb.unsqueeze(1).broadcast_to([128, 16, 64]), ALU.mult,
        rd=g_pos | g_iv, wr=g_ws, tag="tab.lnsc")
    dve("ts1", x1f[:, :], x1f[:, :], 1.0 / 512.0, ALU.mult, rd=g_ws, wr=g_ws)
    sinv = sqd[:, :]
    cosv = Sst[:, 0:1024]
    for nm, srcv, g_src, sgn in (("tqc", cosv, g_ss, 1.0), ("tqs", sinv, g_sq, 1.0),
                                 ("tkc", cosv, g_ss, -1.0), ("tks", sinv, g_sq, -1.0)):
        t_v = views[nm][0]
        tv = t_v[:, :, :].rearrange("p t j -> p (t j)")
        act(None, tv, x1f[:, :], g_ws, rg(nm), AF.Exp, scale=sgn, tag="tab.e" + nm)
        dve("tt", tv, tv, srcv, ALU.mult, rd=rg(nm) | g_src, wr=rg(nm), tag="tab.m" + nm)
    hr = xres[:, 64:72]
    g_h = rg("xres", 64, 72)
    lng = stats[:, 32:40]
    g_lng = rgs(32, 40)
    dve("ts", lng, hr, float((np.log(1 / 512) - np.log(1 / 32)) / 7), float(np.log(1 / 32)),
        ALU.mult, ALU.add, rd=g_h, wr=g_lng, tag="gam.lin")
    act(None, lng, lng, g_lng, g_lng, AF.Exp)
    dve("ts", lng, lng, -1.0, 1.0, ALU.mult, ALU.add, rd=g_lng, wr=g_lng)
    act(None, lng, lng, g_lng, g_lng, AF.Ln)
    lngb = lng.unsqueeze(2).broadcast_to([128, 8, 128])
    pc = stats[:, 16:17]
    dve("ts1", sqd[:, :].rearrange("p (h e) -> p h e", h=8), lngb, pc, ALU.mult,
        rd=g_lng | g_pos, wr=g_sq, tag="gam.pe")
    act(None, gamw[:, :, :].rearrange("p h e -> p (h e)"), sqd[:, :], g_sq, rg("gamw"), AF.Exp)
    act(None, gamiw[:, :, :].rearrange("p h e -> p (h e)"), sqd[:, :], g_sq, rg("gamiw"),
        AF.Exp, scale=-1.0)

    facC = sb("facC", (128, 4096), F32, at=BIG1 + 49152)

    def compose(fsrc, nst, dst_ap, dst_rg, tag):
        dma("SP", facA[:, :], fsrc[0], set(), rg("facA"), tag=tag + ".l0")
        bufs = [(facB, rg("facB")), (facC, rg("facC"))]
        dma("POOL", bufs[0][0][:, :], fsrc[1], set(), bufs[0][1], tag=tag + ".l1")
        for s in range(1, nst):
            cu, cu_g = bufs[(s - 1) % 2]
            if s + 1 < nst:
                nx, nx_g = bufs[s % 2]
                dma("SP" if s % 2 else "POOL", nx[:, :], fsrc[s + 1], set(), nx_g,
                    tag=tag + f".l{s+1}")
            last = s == nst - 1
            o = dst_ap if last else facA[:, :]
            og = dst_rg if last else rg("facA")
            dve("tt", o, facA[:, :], cu[:, :], ALU.mult, rd=rg("facA") | cu_g, wr=og,
                tag=tag + f".m{s}")

    if stages >= 2:
     compose(fs2u, 5, c2up[:, :, :].rearrange("p t c -> p (t c)"), rg("c2up"), "c2u")
     compose(fs1u, 7, comp1[:, :], rg("comp1"), "c1u")

    G_STR = [rg("stream", t * 1024, (t + 1) * 1024) for t in range(NT)]

    def xnt_slice(dt, c0, c1):
        return xnT[:, dt, c0:c1]

    def g_xnt(dt, c0, c1):
        base = dt * 2048
        return rg("xnT", base + c0, base + c1)

    def g_fmaj(fb, c0, c1):
        base = fb * 2048
        return rg("fmaj", base + c0, base + c1)

    def stats8(c_in, c_in2, c_mu, c_rstd, tag):
        s_sum = stats[:, c_in:c_in + 8]
        s_sq = stats[:, c_in2:c_in2 + 8]
        s_mu = stats[:, c_mu:c_mu + 8]
        s_rs = stats[:, c_rstd:c_rstd + 8]
        g_all = rgs(c_in, c_rstd + 8)
        dve("ts", s_mu, s_sum, 1.0 / 128, None, ALU.mult, ALU.bypass, rd=g_all, wr=g_all, tag=tag + ".mu")
        dve("ts", s_sq, s_sq, 1.0 / 128, EPS, ALU.mult, ALU.add, rd=g_all, wr=g_all)
        dve("ts", s_rs, s_mu, -1.0, None, ALU.mult, ALU.bypass, rd=g_all, wr=g_all)
        dve("tt", s_rs, s_rs, s_mu, ALU.mult, rd=g_all, wr=g_all)
        dve("tt", s_rs, s_rs, s_sq, ALU.add, rd=g_all, wr=g_all)
        act(None, s_rs, s_rs, g_all, g_all, AF.Ln, tag=tag + ".ln")
        act(None, s_rs, s_rs, g_all, g_all, AF.Exp, scale=-0.5)
        return s_mu, s_rs

    def we_iter(ft):
        w2bf = w2b[:, :, :].rearrange("p a b -> p (a b)")
        wcb, wcn = wcrot[ft % 8], f"wcrot{ft % 8}"
        dma("POOL", wcb[:, :], wexpT[ft * 128:(ft + 1) * 128, :], set(), rg(wcn),
            tag=f"we.l{ft}")
        c1t = comp1[:, ft * 128:(ft + 1) * 128]
        g_c1t = rg("comp1", ft * 128, (ft + 1) * 128)
        for dq in range(2):
            ps, pg = (pbA, PS["pbA"]) if dq == 0 else (pbB, PS["pbB"])
            for di in range(4):
                dt = dq * 4 + di
                mm(ps[:, di * 128:(di + 1) * 128], wcb[:, dt * 128:(dt + 1) * 128], c1t,
                   rg(wcn) | g_c1t, pg, tag=f"we.mm{ft}.{dt}")
            dst = w2bf.rearrange("p (dt a b c) -> p dt a b c", dt=8, a=32, b=4)[:, dq * 4:(dq + 1) * 4, :, :, ft]
            srcp = ps[:, 0:512].rearrange("p (di a b) -> p di a b", di=4, a=32)
            e0 = dq * 4 * 4096
            if dq == 0:
                P.add("DVE", lambda e, o=dst, i=srcp: e.tensor_copy(o, i),
                      pg, rg("w2b", e0, e0 + 16384), tag=f"we.ev{ft}.{dq}")
            else:
                P.add("ACT", lambda e, o=dst, i=srcp: e.activation(o, i, AF.Copy),
                      pg, rg("w2b", e0, e0 + 16384), tag=f"we.ev{ft}.{dq}")
        mm(pbC[0:1, ft % 4 * 128:(ft % 4 + 1) * 128], bexpc[:, ft:ft + 1], c1t,
           rg("bexpc") | g_c1t, PS["pbC"], tag=f"bt.mm{ft}")
        btdst = fstg[0:1, :].rearrange("p (a b c) -> p a b c", a=32, b=4)[:, :, :, ft]
        P.add("ACT", lambda e, o=btdst,
              i=pbC[0:1, ft % 4 * 128:(ft % 4 + 1) * 128].rearrange("p (a b) -> p a b", a=32):
              e.activation(o, i, AF.Copy),
              PS["pbC"], rg("fstg"), tag=f"bt.ev{ft}")

    def we_finish():
        for dt in range(8):
            dma("SP", wtexpd[dt, :, :], w2b[:, :, :].rearrange("p a b -> p (a b)")
                .rearrange("p (dt c) -> p dt c", dt=8)[:, dt, :],
                rg("w2b", dt * 4096, (dt + 1) * 4096), H("wtexpd", dt), tag=f"we.st{dt}")
        for gt in range(32):
            mm(pbD[0:128, gt:gt + 1], fstg[0:1, gt * 128:(gt + 1) * 128], ident[0:1, 0:1],
               rg("fstg") | rg("ident"), PS["pbD"], tag=f"bt.t{gt}")
        dve("cp", btldc[:, :], pbD[:, 0:32], rd=PS["pbD"], wr=rg("btldc"), tag="bt.col")

    def a1_iter(gt):
        wcb, wcn = wcrot[gt % 8], f"wcrot{gt % 8}"
        hsb, hsn = [(hstage, "hstage"), (wconb, "wconb"), (wconbB, "wconbB"), (tmA, "tmA")][gt % 4]
        dma("POOL", wcb[:, :], wcong[gt * 128:(gt + 1) * 128, :], set(), rg(wcn),
            tag=f"a1.l{gt}")
        for half in range(2):
            ps, pg = (pbC, PS["pbC"]) if half == 0 else (pbD, PS["pbD"])
            mm(ps[:, :], comp1[:, gt * 128:(gt + 1) * 128],
               wcb[:, half * 512:(half + 1) * 512],
               rg(wcn) | rg("comp1", gt * 128, (gt + 1) * 128), pg, tag=f"a1.mm{gt}.{half}")
            P.add("ACT", lambda e, o=hsb[:, half * 512:(half + 1) * 512], i=ps[:, :]:
                  e.activation(o, i, AF.Copy), pg,
                  rg(hsn, half * 512, (half + 1) * 512), tag=f"a1.ev{gt}.{half}")
        dma("SP", amidd[gt * 128:(gt + 1) * 128, :], hsb[:, :], rg(hsn),
            H("amidd", gt), tag=f"a1.st{gt}")

    def a2_iter(ft):
        wcb, wcn = wcrot[ft % 8], f"wcrot{ft % 8}"
        hsb, hsn = [(hstage, "hstage"), (wconb, "wconb"), (wconbB, "wconbB"), (tmA, "tmA")][ft % 4]
        srcap = amidd.ap().rearrange("(a b c) d -> a b c d", a=32, b=4)[:, :, ft, :]
        dma("SP", wcb[:, :], srcap, Hr("amidd", 0, 32), rg(wcn), tag=f"a2.l{ft}")
        for half in range(2):
            ps, pg = (pbC, PS["pbC"]) if half == 0 else (pbD, PS["pbD"])
            mm(ps[:, :], comp1[:, ft * 128:(ft + 1) * 128],
               wcb[:, half * 512:(half + 1) * 512],
               rg(wcn) | rg("comp1", ft * 128, (ft + 1) * 128), pg, tag=f"a2.mm{ft}.{half}")
            P.add("ACT", lambda e, o=hsb[:, half * 512:(half + 1) * 512], i=ps[:, :]:
                  e.activation(o, i, AF.Copy), pg,
                  rg(hsn, half * 512, (half + 1) * 512), tag=f"a2.ev{ft}.{half}")
        dma("SP", wtcond.ap().rearrange("(gt r1 f) d -> gt r1 f d", gt=32, r1=4)[:, :, ft, :],
            hsb[:, :], rg(hsn), H("wtcond", ft), tag=f"a2.st{ft}")

    def ln0_tile(t):
        tc = (t * 128, (t + 1) * 128)
        dma("SP", xres[:, :], x0[tc[0]:tc[1], :], set(), rg("xres"), tag=f"ln0.l{t}")
        mu, rs = stats_chain(xres[:, :], rg("xres"), 1024, t * 4, tag=f"ln0.{t}")
        dve("ts", stream[:, t, :], xres[:, :], mu, rs, ALU.subtract, ALU.mult,
            rd=rg("xres") | rgs(t * 4, t * 4 + 4), wr=G_STR[t],
            tag=f"ln0.n{t}")

    ln0_left = (list(range(0, 8)) + list(range(10, 16))) if stages >= 4 else []
    if stages >= 3:
        for ft in range(32):
            we_iter(ft)
        we_finish()
        compose(fs2d, 5, comp1[:, :], rg("comp1"), "c2d")
        for gt in range(32):
            a1_iter(gt)
        compose(fs1d, 7, comp1[:, :], rg("comp1"), "c1d")
        for ft in range(32):
            a2_iter(ft)
            if ft % 2 == 1 and ln0_left:
                ln0_tile(ln0_left.pop(0))
    ln0_left += [8, 9] if stages >= 4 else []
    while ln0_left:
        ln0_tile(ln0_left.pop(0))

    SCR = [
        {"qh": (qh, "qh"), "kh": (kh, "kh"), "vv": (vv, "vv"),
         "tmA": (tmA, "tmA"), "sqd": (sqd, "sqd"), "sqd_bf": (sqd_bf, "sqd")},
        {"qh": (qhB, "qhB"), "kh": (khB, "khB"), "vv": (vvB, "vvB"),
         "tmA": (tmAB, "tmAB"), "sqd": (sqdB, "sqdB"), "sqd_bf": (sqdB_bf, "sqdB")},
    ]

    def attn_layer(li, sub=99):
        ST_A = []
        for t in range(NT):
            strt = stream[:, t, :]
            dma("SP", spillX[t * 128:(t + 1) * 128, :], strt, G_STR[t], H("spX", t),
                tag=f"L{li}.spX{t}")
            ST_A.append(stats_chain(strt, G_STR[t], 1024, t * 4, tag=f"L{li}.A{t}"))
        for t in range(NT):
            par = t % 2
            tA, tAn = SCR[par]["tmA"]
            pbX, pbXg = (pbT, PS["pbT"]) if par == 0 else (pbM, PS["pbM"])
            strt = stream[:, t, :]
            mu, rs = ST_A[t]
            g_st = rgs(t * 4, t * 4 + 4)
            dve("ts", tA[:, :], strt, mu, rs, ALU.subtract, ALU.mult,
                rd=G_STR[t] | g_st, wr=rg(tAn), tag=f"L{li}.An{t}")
            for dt in range(8):
                tp(pbX[:, dt * 128:(dt + 1) * 128], tA[:, dt * 128:(dt + 1) * 128],
                   rg(tAn, dt * 128, (dt + 1) * 128) | rg("ident"), pbXg,
                   tag=f"L{li}.At{t}.{dt}")
            for dt in range(8):
                act(None, xnt_slice(dt, t * 128, (t + 1) * 128),
                    pbX[:, dt * 128:(dt + 1) * 128], pbXg,
                    g_xnt(dt, t * 128, (t + 1) * 128), AF.Identity,
                    bias=lnv[:, li * 32 + 8 + dt:li * 32 + 8 + dt + 1],
                    scale=lnv[:, li * 32 + dt:li * 32 + dt + 1], tag=f"L{li}.Ae{t}.{dt}")
        if sub < 2:
            return
        dma("POOL", wqkvb[:, :, :], wqkv.ap()[li].rearrange("(dt p) c -> p dt c", p=128),
            set(), rg("wqkvb"), tag=f"L{li}.wqkv")
        dma("POOL", wowg[:, :, :], wg.ap()[li].rearrange("(dt p) c -> p dt c", p=128),
            set(), rg("wowg"), tag=f"L{li}.wg")
        PROT = [(pbA, PS["pbA"]), (pbB, PS["pbB"]), (pbC, PS["pbC"]), (pbD, PS["pbD"])]
        for fb in range(8):
            for ch in range(4):
                ps, pg = PROT[(fb * 4 + ch) % 4]
                for dt in range(8):
                    mm(ps[:, :], wowg[:, dt, fb * 128:(fb + 1) * 128],
                       xnT[:, dt, ch * 512:(ch + 1) * 512],
                       rg("wowg", dt * 1024 + fb * 128, dt * 1024 + (fb + 1) * 128)
                       | g_xnt(dt, ch * 512, (ch + 1) * 512),
                       pg, start=(dt == 0), stop=(dt == 7), tag=f"L{li}.G{fb}.{ch}.{dt}")
                act(None, fmaj[:, fb, ch * 512:(ch + 1) * 512], ps[:, :], pg,
                    g_fmaj(fb, ch * 512, (ch + 1) * 512), AF.Silu, tag=f"L{li}.Gs{fb}.{ch}")
        if sub < 3:
            return
        nret = NT if sub >= 4 else 2
        dve("memset", Sst[:, :], 0.0, rd=set(), wr=rg("Sst"), tag=f"L{li}.S0")
        dve("memset", Sbf[:, :], 0.0, rd=set(), wr=rg("Sbf"), tag=f"L{li}.Sb0")

        def qkv_tile(t):
            par = t % 2
            q_h, qn = SCR[par]["qh"]
            k_h, kn = SCR[par]["kh"]
            v_v, vn = SCR[par]["vv"]
            tc0, tc1 = t * 128, (t + 1) * 128
            for which, dst, dgn in (("q", q_h, qn), ("k", k_h, kn), ("v", v_v, vn)):
                base = {"q": 0, "k": 1024, "v": 2048}[which]
                for half in range(2):
                    ps, pg = (pbA, PS["pbA"]) if half == 0 else (pbB, PS["pbB"])
                    for dt in range(8):
                        mm(ps[:, :], xnT[:, dt, tc0:tc1],
                           wqkvb[:, dt, base + half * 512:base + (half + 1) * 512],
                           g_xnt(dt, tc0, tc1)
                           | rg("wqkvb", dt * 3072 + base + half * 512,
                                dt * 3072 + base + (half + 1) * 512),
                           pg, start=(dt == 0), stop=(dt == 7),
                           tag=f"L{li}.{which}{t}.{half}.{dt}")
                    act(None, dst[:, half * 512:(half + 1) * 512], ps[:, :], pg,
                        rg(dgn, half * 512, (half + 1) * 512), AF.Copy,
                        tag=f"L{li}.{which}e{t}.{half}")

        def ret_core(t):
            par = t % 2
            q_h, qn = SCR[par]["qh"]
            k_h, kn = SCR[par]["kh"]
            v_v, vn = SCR[par]["vv"]
            tA, tAn = SCR[par]["tmA"]
            sQ, sQn = SCR[par]["sqd"]
            sQb, _ = SCR[par]["sqd_bf"]
            gnb = 64 + 32 * par
            tc0, tc1 = t * 128, (t + 1) * 128
            for src_, sgn, cb, sb_, gw, gwg in (
                    (q_h, qn, tqc, tqs, gamw, rg("gamw")),
                    (k_h, kn, tkc, tks, gamiw, rg("gamiw"))):
                sg = rg(sgn)
                sv = src_[:, :].rearrange("p (h two j) -> p h two j", h=8, two=2)
                e_v = sv[:, :, 0, :]
                o_v = sv[:, :, 1, :]
                Cb = cb[:, t, :].unsqueeze(1).broadcast_to([128, 8, 64])
                Sb = sb_[:, t, :].unsqueeze(1).broadcast_to([128, 8, 64])
                g_tab = rg("tqc" if cb is tqc else "tkc") | rg("tqs" if sb_ is tqs else "tks")
                t1 = sQb[:, 1024:1536].rearrange("p (h j) -> p h j", h=8)
                t2 = sQb[:, 1536:2048].rearrange("p (h j) -> p h j", h=8)
                g_t12 = rg(sQn, 512, 1024)
                dve("tt", t1, e_v, Sb, ALU.mult, rd=sg | g_tab, wr=g_t12)
                dve("tt", t2, o_v, Sb, ALU.mult, rd=sg | g_tab, wr=g_t12, tag=f"L{li}.xpp{t}")
                dve("tt", e_v, e_v, Cb, ALU.mult, rd=sg | g_tab, wr=sg)
                dve("tt", e_v, e_v, t2, ALU.subtract, rd=sg | g_t12, wr=sg)
                dve("tt", o_v, o_v, Cb, ALU.mult, rd=sg | g_tab, wr=sg)
                dve("tt", o_v, o_v, t1, ALU.add, rd=sg | g_t12, wr=sg)
                dve("tt", src_[:, :], src_[:, :],
                    gw[:, :, :].rearrange("p h e -> p (h e)"), ALU.mult,
                    rd=sg | gwg, wr=sg, tag=f"L{li}.xp{t}")
            for h in range(NH):
                ps, pg = (pbA, PS["pbA"]) if h < 4 else (pbB, PS["pbB"])
                mm(ps[:, (h % 4) * 128:(h % 4 + 1) * 128], k_h[:, h * 128:(h + 1) * 128],
                   v_v[:, h * 128:(h + 1) * 128],
                   rg(kn, h * 128, (h + 1) * 128) | rg(vn, h * 128, (h + 1) * 128),
                   pg, tag=f"L{li}.T{t}.{h}")
            for h in range(NH):
                tp(pbT[:, h * 128:(h + 1) * 128], q_h[:, h * 128:(h + 1) * 128],
                   rg(qn, h * 128, (h + 1) * 128) | rg("ident"), PS["pbT"],
                   tag=f"L{li}.tq{t}.{h}")
            dve("cp", q_h[:, :], pbT[:, :], rd=PS["pbT"], wr=rg(qn), tag=f"L{li}.tqe{t}")
            for h in range(NH):
                tp(pbM[:, h * 128:(h + 1) * 128], k_h[:, h * 128:(h + 1) * 128],
                   rg(kn, h * 128, (h + 1) * 128) | rg("ident"), PS["pbM"],
                   tag=f"L{li}.tk{t}.{h}")
            dve("cp", k_h[:, :], pbM[:, :], rd=PS["pbM"], wr=rg(kn), tag=f"L{li}.tke{t}")
            for h in range(NH):
                pg = PS["pbY0"] if h < 4 else PS["pbY1"]
                mm(pbY[:, h * 128:(h + 1) * 128], k_h[:, h * 128:(h + 1) * 128],
                   q_h[:, h * 128:(h + 1) * 128],
                   rg(kn, h * 128, (h + 1) * 128) | rg(qn, h * 128, (h + 1) * 128),
                   pg, tag=f"L{li}.A{t}.{h}")
            for half in range(2):
                dve("tt", tA[:, half * 512:(half + 1) * 512],
                    pbY[:, half * 512:(half + 1) * 512], mask8[:, half * 512:(half + 1) * 512],
                    ALU.mult, rd=(PS["pbY0"] if half == 0 else PS["pbY1"]) | rg("mask8"),
                    wr=rg(tAn, half * 512, (half + 1) * 512), tag=f"L{li}.Am{t}.{half}")
            for h in range(NH):
                pg = PS["pbY0"] if h < 4 else PS["pbY1"]
                mm(pbY[:, h * 128:(h + 1) * 128], tA[:, h * 128:(h + 1) * 128],
                   v_v[:, h * 128:(h + 1) * 128],
                   rg(tAn, h * 128, (h + 1) * 128) | rg(vn, h * 128, (h + 1) * 128),
                   pg, start=True, stop=(t == 0), tag=f"L{li}.Y{t}.{h}")
                if t > 0:
                    mm(pbY[:, h * 128:(h + 1) * 128], q_h[:, h * 128:(h + 1) * 128],
                       Sbf[:, h * 128:(h + 1) * 128],
                       rg(qn, h * 128, (h + 1) * 128) | rg("Sbf", h * 128, (h + 1) * 128),
                       pg, start=False, stop=True, tag=f"L{li}.Yc{t}.{h}")
            for half, (ps, pg) in enumerate(((pbA, PS["pbA"]), (pbB, PS["pbB"]))):
                dve("tt", Sst[:, half * 512:(half + 1) * 512],
                    Sst[:, half * 512:(half + 1) * 512], ps[:, :], ALU.add,
                    rd=rg("Sst", half * 512, (half + 1) * 512) | pg,
                    wr=rg("Sst", half * 512, (half + 1) * 512), tag=f"L{li}.Su{t}.{half}")
            for h in range(NH):
                dve("ts1", Sst[:, h * 128:(h + 1) * 128], Sst[:, h * 128:(h + 1) * 128],
                    GAMC[h], ALU.mult, rd=rg("Sst", h * 128, (h + 1) * 128),
                    wr=rg("Sst", h * 128, (h + 1) * 128), tag=f"L{li}.Sg{t}.{h}")
            dve("cp", Sbf[:, :], Sst[:, :], rd=rg("Sst"), wr=rg("Sbf"), tag=f"L{li}.Sb{t}")
            act(None, sQ[:, :], pbY[:, :], PS["pbY"], rg(sQn, 0, 1024), AF.Square,
                tag=f"L{li}.gsq{t}")
            dve("red", stats[:, gnb:gnb + 8], pbY[:, :].rearrange("p (h e) -> p h e", h=8),
                rd=PS["pbY"], wr=rgs(gnb, gnb + 8), tag=f"L{li}.gr{t}")
            dve("red", stats[:, gnb + 8:gnb + 16], sQ[:, :].rearrange("p (h e) -> p h e", h=8),
                rd=rg(sQn, 0, 1024), wr=rgs(gnb + 8, gnb + 16))
            mu8, rs8 = stats8(gnb, gnb + 8, gnb + 16, gnb + 24, tag=f"L{li}.g8{t}")
            for h in range(NH):
                pg = PS["pbY0"] if h < 4 else PS["pbY1"]
                dve("ts", tA[:, h * 128:(h + 1) * 128], pbY[:, h * 128:(h + 1) * 128],
                    stats[:, gnb + 16 + h:gnb + 17 + h],
                    stats[:, gnb + 24 + h:gnb + 25 + h], ALU.subtract, ALU.mult,
                    rd=pg | rgs(gnb + 16, gnb + 32), wr=rg(tAn, h * 128, (h + 1) * 128),
                    tag=f"L{li}.gn{t}.{h}")
            for h in range(NH):
                tp(pbT[:, h * 128:(h + 1) * 128], tA[:, h * 128:(h + 1) * 128],
                   rg(tAn, h * 128, (h + 1) * 128) | rg("ident"), PS["pbT"],
                   tag=f"L{li}.gt{t}.{h}")
            for h in range(NH):
                act(None, sQb[:, h * 128:(h + 1) * 128], pbT[:, h * 128:(h + 1) * 128],
                    PS["pbT"], rg(sQn, 0, 512), AF.Identity,
                    bias=gnv[:, li * 16 + 8 + h:li * 16 + 8 + h + 1],
                    scale=gnv[:, li * 16 + h:li * 16 + h + 1], tag=f"L{li}.ga{t}.{h}")
            g_fm = set()
            for fb in range(8):
                g_fm |= rg("fmaj", fb * 2048 + tc0, fb * 2048 + tc1)
            dve("tt", fmaj[:, :, tc0:tc1],
                sQb[:, 0:1024].rearrange("p (h c) -> p h c", h=8),
                fmaj[:, :, tc0:tc1], ALU.mult,
                rd=rg(sQn, 0, 512) | g_fm, wr=g_fm, tag=f"L{li}.z{t}")

        for t in range(nret):
            if t == 0:
                qkv_tile(0)
            if t + 1 < nret:
                qkv_tile(t + 1)
            ret_core(t)

    def attn_layer_tail(li):
        dma("POOL", wowg[:, :, :], wo.ap()[li].rearrange("(dt p) c -> p dt c", p=128),
            set(), rg("wowg"), tag=f"L{li}.wo")
        for t in range(NT):
            par = t % 2
            xr = xres if par == 0 else x1f
            g_xr = rg("xres") if par == 0 else (rg("qh") | rg("kh"))
            tc0, tc1 = t * 128, (t + 1) * 128
            dma("SP", xr[:, :], spillX[tc0:tc1, :], H("spX", t), g_xr, tag=f"L{li}.rX{t}")
            for half in range(2):
                ps, pg = [(pbA, PS["pbA"]), (pbB, PS["pbB"]), (pbC, PS["pbC"]),
                          (pbD, PS["pbD"])][(t % 2) * 2 + half]
                for vt in range(8):
                    mm(ps[:, :], fmaj[:, vt, tc0:tc1],
                       wowg[:, vt, half * 512:(half + 1) * 512],
                       rg("fmaj", vt * 2048 + tc0, vt * 2048 + tc1)
                       | rg("wowg", vt * 1024 + half * 512, vt * 1024 + (half + 1) * 512),
                       pg, start=(vt == 0), stop=(vt == 7), tag=f"L{li}.O{t}.{half}.{vt}")
                dve("tt", stream[:, t, half * 512:(half + 1) * 512], ps[:, :],
                    xr[:, half * 512:(half + 1) * 512], ALU.add,
                    rd=pg | g_xr,
                    wr=rg("stream", t * 1024 + half * 512, t * 1024 + (half + 1) * 512),
                    tag=f"L{li}.Oe{t}.{half}")
        ST_B = []
        for t in range(NT):
            strt = stream[:, t, :]
            tc0, tc1 = t * 128, (t + 1) * 128
            dma("SP", spillY[tc0:tc1, :], strt, G_STR[t], H("spY", t), tag=f"L{li}.spY{t}")
            ST_B.append(stats_chain(strt, G_STR[t], 1024, t * 4, tag=f"L{li}.B{t}"))
        for t in range(NT):
            par = t % 2
            tA, tAn = SCR[par]["tmA"]
            pbX, pbXg = (pbT, PS["pbT"]) if par == 0 else (pbM, PS["pbM"])
            strt = stream[:, t, :]
            tc0, tc1 = t * 128, (t + 1) * 128
            mu, rs = ST_B[t]
            g_st = rgs(t * 4, t * 4 + 4)
            dve("ts", tA[:, :], strt, mu, rs, ALU.subtract, ALU.mult,
                rd=G_STR[t] | g_st, wr=rg(tAn), tag=f"L{li}.Bn{t}")
            for dt in range(8):
                tp(pbX[:, dt * 128:(dt + 1) * 128], tA[:, dt * 128:(dt + 1) * 128],
                   rg(tAn, dt * 128, (dt + 1) * 128) | rg("ident"), pbXg,
                   tag=f"L{li}.Bt{t}.{dt}")
            for dt in range(8):
                act(None, xnt_slice(dt, tc0, tc1), pbX[:, dt * 128:(dt + 1) * 128],
                    pbXg, g_xnt(dt, tc0, tc1), AF.Identity,
                    bias=lnv[:, li * 32 + 24 + dt:li * 32 + 24 + dt + 1],
                    scale=lnv[:, li * 32 + 16 + dt:li * 32 + 16 + dt + 1],
                    tag=f"L{li}.Be{t}.{dt}")
        dma("POOL", w1b[:, :, :], w1.ap()[li].rearrange("(dt p) c -> p dt c", p=128),
            set(), rg("w1b"), tag=f"L{li}.w1")
        for ch in range(4):
            cc0, cc1 = ch * 512, (ch + 1) * 512
            for fb in range(32):
                ps, pg = [(pbA, PS["pbA"]), (pbB, PS["pbB"]), (pbC, PS["pbC"]),
                          (pbD, PS["pbD"])][fb % 4]
                for dt in range(8):
                    mm(ps[:, :], w1b[:, dt, fb * 128:(fb + 1) * 128],
                       xnT[:, dt, cc0:cc1],
                       rg("w1b", dt * 4096 + fb * 128, dt * 4096 + (fb + 1) * 128)
                       | g_xnt(dt, cc0, cc1),
                       pg, start=(dt == 0), stop=(dt == 7), tag=f"L{li}.F1.{ch}.{fb}.{dt}")
                sl4 = fb % 4
                slb, sln, slo = wcrot[sl4 // 2], f"wcrot{sl4 // 2}", (sl4 % 2) * 512
                act(None, slb[:, slo:slo + 512], ps[:, :], pg,
                    rg(sln, slo, slo + 512), AF.Gelu,
                    bias=b1c[:, li * 32 + fb:li * 32 + fb + 1],
                    tag=f"L{li}.F1g.{ch}.{fb}")
                dma("SP", spillH.ap()[ch * 4:(ch + 1) * 4, :, fb, :]
                    .rearrange("tt p c -> p tt c"),
                    slb[:, slo:slo + 512].rearrange("p (tt c) -> p tt c", tt=4),
                    rg(sln, slo, slo + 512), H("spH", ch * 32 + fb),
                    tag=f"L{li}.F1s.{ch}.{fb}")
        dma("POOL", w2b[:, :, :], w2.ap()[li].rearrange("(ft p) c -> p ft c", p=128),
            set(), rg("w2b"), tag=f"L{li}.w2")
        hload0 = nc.alloc_sbuf_tensor_at(f"hload0{li}", [128, 32, 128], BF16,
                                         offset=views["tmA"][1])
        hload1 = nc.alloc_sbuf_tensor_at(f"hload1{li}", [128, 32, 128], BF16,
                                         offset=WOW0)
        g_h0 = rg("tmA") | rg("qh") | rg("kh") | rg("vv")
        g_h1 = rg("sqdB") | rg("tmAB") | rg("qhB")
        for t in range(NT):
            par = t % 2
            hl = hload0 if par == 0 else hload1
            g_hl = g_h0 if par == 0 else g_h1
            xr = xres if par == 0 else xresB
            g_xr = rg("xres") if par == 0 else rg("xresB")
            tc0, tc1 = t * 128, (t + 1) * 128
            ch = t // 4
            dma("SP", hl[:, :, :], spillH.ap()[t, :, :, :],
                {("h", "spH", ch * 32 + fb) for fb in range(32)}, g_hl,
                tag=f"L{li}.F2l{t}")
            dma("SP", xr[:, :], spillY[tc0:tc1, :], H("spY", t), g_xr,
                tag=f"L{li}.rY{t}")
            for half in range(2):
                ps, pg = [(pbA, PS["pbA"]), (pbB, PS["pbB"]), (pbC, PS["pbC"]),
                          (pbD, PS["pbD"])][(t % 2) * 2 + half]
                for ft in range(32):
                    mm(ps[:, :], hl[:, ft, :],
                       w2b[:, ft, half * 512:(half + 1) * 512],
                       g_hl | rg("w2b", ft * 1024 + half * 512, ft * 1024 + (half + 1) * 512),
                       pg, start=(ft == 0), stop=False, tag=f"L{li}.F2.{t}.{half}.{ft}")
                mm(ps[:, :], esel[:, li * 128:(li + 1) * 128],
                   rows[:, 128 + half * 512:128 + (half + 1) * 512],
                   rg("rows") | rg("esel"), pg, start=False, stop=True,
                   tag=f"L{li}.F2b.{t}.{half}")
                dve("tt", stream[:, t, half * 512:(half + 1) * 512], ps[:, :],
                    xr[:, half * 512:(half + 1) * 512], ALU.add,
                    rd=pg | g_xr,
                    wr=rg("stream", t * 1024 + half * 512, t * 1024 + (half + 1) * 512),
                    tag=f"L{li}.F2e.{t}.{half}")

    if stages >= 5:
        attn_layer(0, sub={51: 1, 52: 2, 53: 3}.get(stages, 99))
    if stages >= 6 and stages < 50:
        attn_layer_tail(0)
    if stages >= 7 and stages < 50:
        attn_layer(1)
        attn_layer_tail(1)

    for t in range(NT if (stages >= 8 and stages < 50) else 0):
        tc0, tc1 = t * 128, (t + 1) * 128
        strt = stream[:, t, :]
        dma("SP", xres[:, :], x0[tc0:tc1, :], set(), rg("xres"), tag=f"x1.l{t}")
        mu, rs = stats_chain(strt, G_STR[t], 1024, t * 4, tag=f"x1.a{t}")
        g_st = rgs(t * 4, t * 4 + 4)
        dve("ts", sqd[:, :], strt, mu, rs, ALU.subtract, ALU.mult,
            rd=G_STR[t] | g_st, wr=rg("sqd"), tag=f"x1.n{t}")
        dve("tt", strt, sqd[:, :], xres[:, :], ALU.add,
            rd=rg("sqd") | rg("xres"), wr=G_STR[t], tag=f"x1.s{t}")
        dma("SP", spillX[tc0:tc1, :], strt, G_STR[t], H("spX", t), tag=f"x1.sp{t}")
        mu2, rs2 = stats_chain(strt, G_STR[t], 1024, t * 4 + 64, tag=f"x1.b{t}")
        tA, tAn = SCR[t % 2]["tmA"]
        pbX, pbXg = (pbT, PS["pbT"]) if t % 2 == 0 else (pbM, PS["pbM"])
        g_st2 = rgs(t * 4 + 64, t * 4 + 68)
        dve("ts", tA[:, :], strt, mu2, rs2, ALU.subtract, ALU.mult,
            rd=G_STR[t] | g_st2, wr=rg(tAn), tag=f"x1.n2{t}")
        for dt in range(8):
            tp(pbX[:, dt * 128:(dt + 1) * 128], tA[:, dt * 128:(dt + 1) * 128],
               rg(tAn, dt * 128, (dt + 1) * 128) | rg("ident"), pbXg,
               tag=f"x1.t{t}.{dt}")
        for dt in range(8):
            dve("cp", xnt_slice(dt, tc0, tc1), pbX[:, dt * 128:(dt + 1) * 128],
                rd=pbXg, wr=g_xnt(dt, tc0, tc1), tag=f"x1.e{t}.{dt}")

    if stages >= 9 and stages < 50:
        full9 = stages == 9
        dma("SP", wteb[:, :, :], wtexpd.ap(), Hr("wtexpd", 0, 8), rg("wteb"), tag="fs.wte")
        GT = 256
        h1g2 = nc.alloc_sbuf_tensor_at("h1g2", [128, 32, GT], BF16, offset=views["fmaj"][1])
        zg2 = nc.alloc_sbuf_tensor_at("zg2", [128, 32, GT], BF16,
                                      offset=views["fmaj"][1] + 16384)
        def g_h1(gt, c0, c1):
            return rg("fmaj", gt * GT + c0, gt * GT + c1)
        def g_zg(gt, c0, c1):
            return rg("fmaj", 8192 + gt * GT + c0, 8192 + gt * GT + c1)

        def fs_e(gi, gt):
            gc0, gc1 = gi * GT, (gi + 1) * GT
            ps, pg = (pbA, PS["pbA"]) if gt % 2 == 0 else (pbB, PS["pbB"])
            for dt in range(8):
                mm(ps[:, 0:GT], wteb[:, dt, gt * 128:(gt + 1) * 128],
                   xnT[:, dt, gc0:gc1],
                   rg("wteb", dt * 4096 + gt * 128, dt * 4096 + (gt + 1) * 128)
                   | g_xnt(dt, gc0, gc1), pg,
                   start=(dt == 0), stop=(dt == 7), tag=f"fs.e{gi}.{gt}.{dt}")
            act(None, h1g2[:, gt, :], ps[:, 0:GT], pg, g_h1(gt, 0, GT),
                AF.Identity, bias=btldc[:, gt:gt + 1], tag=f"fs.h1{gi}.{gt}")

        def fs_c2(gi, gt):
            ps, pg = (pbC, PS["pbC"]) if gt % 2 == 0 else (pbD, PS["pbD"])
            mm(ps[:, 0:GT], c2up[:, gt, :], h1g2[:, gt, :],
               rg("c2up", gt * 128, (gt + 1) * 128) | g_h1(gt, 0, GT),
               pg, tag=f"fs.c2{gi}.{gt}")
            act(None, zg2[:, gt, :], ps[:, 0:GT], pg, g_zg(gt, 0, GT),
                AF.Gelu, bias=upbg[:, gt:gt + 1], tag=f"fs.z{gi}.{gt}")

        def fs_y(gi, gt):
            wcb = wcrot[gt % 8]
            wcn = f"wcrot{gt % 8}"
            dma("SP" if gt % 2 == 0 else "POOL", wcb[:, :],
                wtcond[gt * 128:(gt + 1) * 128, :], Hr("wtcond", 0, 32), rg(wcn),
                tag=f"fs.wc{gi}.{gt}")
            for tt in range(2):
                pslist = ((pbY, PS["pbY0"], 0), (pbY, PS["pbY1"], 512)) if tt == 0 else \
                         ((pbC, PS["pbC"], 0), (pbD, PS["pbD"], 512))
                for ps, pg, off in pslist:
                    o = ps[:, off:off + 512] if ps is pbY else ps[:, :]
                    mm(o, zg2[:, gt, tt * 128:(tt + 1) * 128], wcb[:, off:off + 512],
                       g_zg(gt, tt * 128, (tt + 1) * 128) | rg(wcn, off, off + 512),
                       pg, start=(gt == 0), stop=False, tag=f"fs.y{gi}.{gt}.{tt}.{off}")

        def fs_tail(gi):
            for tt in range(2):
                t = gi * 2 + tt
                tc0, tc1 = t * 128, (t + 1) * 128
                par = t % 2
                xr1 = x1f if par == 0 else xresB
                g_xr1 = (rg("qh") | rg("kh")) if par == 0 else rg("xresB")
                pslist = ((pbY, PS["pbY0"], 0), (pbY, PS["pbY1"], 512)) if tt == 0 else \
                         ((pbC, PS["pbC"], 0), (pbD, PS["pbD"], 512))
                dma("SP", xr1[:, :], spillX[tc0:tc1, :], H("spX", t), g_xr1,
                    tag=f"fs.x1{t}")
                for ps, pg, off in pslist:
                    o = ps[:, off:off + 512] if ps is pbY else ps[:, :]
                    mm(o, esel[:, 256:384], rows[:, 128 + off:128 + off + 512],
                       rg("rows") | rg("esel"), pg, start=False, stop=False, tag=f"fs.yb{t}.{off}")
                    mm(o, esel[:, 256:384], rows[:, 1152 + off:1152 + off + 512],
                       rg("rows") | rg("esel"), pg, start=False, stop=True, tag=f"fs.yd{t}.{off}")
                    dve("cp", xres[:, off:off + 512], o, rd=pg, wr=rg("xres", off, off + 512),
                        tag=f"fs.ye{t}.{off}")
                mu, rs = stats_chain(xres[:, :], rg("xres"), 1024, 128 + (t % 4) * 4, tag=f"fs.st{t}")
                g_st = rgs(128 + (t % 4) * 4, 132 + (t % 4) * 4)
                dve("ts", sqd[:, :], xres[:, :], mu, rs, ALU.subtract, ALU.mult,
                    rd=rg("xres") | g_st, wr=rg("sqd"), tag=f"fs.n{t}")
                dve("tt", outstg[:, :], sqd[:, :], xr1[:, :], ALU.add,
                    rd=rg("sqd") | g_xr1, wr=rg("Sst"), tag=f"fs.o{t}")
                dma("SP", out_t[tc0:tc1, :], outstg[:, :], rg("Sst"), H("out", t),
                    tag=f"fs.s{t}")

        NG = L // GT if full9 else 1
        for gi in range(NG + 1):
            for gt in range(32):
                if gi < NG:
                    fs_e(gi, gt)
                if gi > 0 and (full9 or stages >= 93):
                    fs_y(gi - 1, gt)
            if gi > 0 and (full9 or stages >= 95):
                fs_tail(gi - 1)
            if gi < NG and (full9 or stages >= 92):
                for gt in range(32):
                    fs_c2(gi, gt)
    P.add("SP", lambda e: e.nop(), {("h", "out", t) for t in range(NT)}, set(),
          tag="final")

    P.finalize(nc, reorder=reorder)
    return nc, P


def _host_prep(P):
    bf = ml_dtypes.bfloat16
    g2f = f_of_g()
    up_w = np.asarray(P["L0_ffn_up_weights"], np.float32)
    dn_w = np.asarray(P["L0_ffn_down_weights"], np.float32)

    def deint(w):
        wd = np.ascontiguousarray(np.transpose(w, (1, 0, 2)))
        wd = np.concatenate([wd[:, :, ::2], wd[:, :, 1::2]], axis=2)
        return wd.reshape(1024, 1024)

    wqkv = np.stack([
        np.concatenate([deint(np.asarray(P["L0_attn_Wq"][li], np.float32)),
                        deint(np.asarray(P["L0_attn_Wk"][li], np.float32)),
                        np.transpose(np.asarray(P["L0_attn_Wv"][li], np.float32),
                                     (1, 0, 2)).reshape(1024, 1024)], axis=1)
        for li in range(2)])

    ftile = lambda t: np.arange(t * 128, (t + 1) * 128)
    gtile = lambda t: g2f[t * 128:(t + 1) * 128]
    fs1u = factor_stack(up_w, range(0, 7), ftile, False)
    fs2u = factor_stack(up_w, range(7, 12), gtile, False)
    fs2d = factor_stack(dn_w, range(7, 12), gtile, True)
    fs1d = factor_stack(dn_w, range(0, 7), ftile, True)

    lnarr = np.stack([np.stack([np.asarray(P[k][li], np.float32) for k in
                                ("L0_attn_ln1_w", "L0_attn_ln1_b",
                                 "L0_attn_ln2_w", "L0_attn_ln2_b")])
                      for li in range(2)])
    lnv = np.ascontiguousarray(lnarr.reshape(2, 4, 8, 128).transpose(3, 0, 1, 2)
                               .reshape(128, 64))
    gnarr = np.stack([np.stack([np.asarray(P[k][li], np.float32) for k in
                                ("L0_attn_gn_w", "L0_attn_gn_b")]) for li in range(2)])
    gnv = np.ascontiguousarray(gnarr.reshape(2, 2, 8, 128).transpose(3, 0, 1, 2)
                               .reshape(128, 32))
    b1c = np.ascontiguousarray(np.asarray(P["L0_attn_ffn_b1"], np.float32)
                               .reshape(2, 32, 128).transpose(2, 0, 1).reshape(128, 64))
    bexp = np.ascontiguousarray(np.asarray(P["L0_ffn_b_exp"], np.float32)
                                .reshape(32, 128).T)
    upbg = np.ascontiguousarray(np.asarray(P["L0_ffn_up_bias"], np.float32)[g2f]
                                .reshape(32, 128).T)
    rows = np.zeros((128, 2176), np.float32)
    rows[0, 0:128] = 1.0
    rows[32, 0:128] = 1.0
    rows[64, 0:128] = 1.0
    rows[0, 128:1152] = np.asarray(P["L0_attn_ffn_b2"][0], np.float32)
    rows[32, 128:1152] = np.asarray(P["L0_attn_ffn_b2"][1], np.float32)
    rows[64, 128:1152] = np.asarray(P["L0_ffn_b_con"], np.float32)
    rows[64, 1152:2176] = np.asarray(P["L0_ffn_down_bias"], np.float32)
    tri = np.triu(np.ones((128, 128), np.float32)).astype(bf)
    esel = np.zeros((128, 384), np.float32)
    esel[0, 0:128] = 1.0
    esel[32, 128:256] = 1.0
    esel[64, 256:384] = 1.0
    com = {
        "esel": esel,
        "wqkv": wqkv,
        "wg": np.asarray(P["L0_attn_W_G"], np.float32),
        "wo": np.asarray(P["L0_attn_W_O"], np.float32),
        "w1": np.asarray(P["L0_attn_ffn_w1"], np.float32),
        "w2": np.asarray(P["L0_attn_ffn_w2"], np.float32),
        "wexpT": np.ascontiguousarray(np.asarray(P["L0_ffn_W_exp"], np.float32).T),
        "wcong": np.ascontiguousarray(np.asarray(P["L0_ffn_W_con"], np.float32)[g2f]),
        "fs1u": fs1u, "fs2u": fs2u, "fs2d": fs2d, "fs1d": fs1d,
        "lnv": lnv, "gnv": gnv, "b1c": b1c, "bexp": bexp, "upbg": upbg,
        "rows": rows,
        "identbf": np.eye(128, dtype=np.float32).astype(bf),
        "mask8": np.tile(np.triu(np.ones((128, 128), np.float32)), (1, 8)),
        "jrep": np.broadcast_to(np.arange(64, dtype=np.float32), (128, 64)).copy(),
        "posg": (np.arange(16, dtype=np.float32)[None, :] * 128
                 + np.arange(128, dtype=np.float32)[:, None]).copy(),
        "hrep": np.broadcast_to(np.arange(8, dtype=np.float32), (128, 8)).copy(),
    }
    return com


_NC_CACHE = {}


def kernel(**inputs):
    com = _host_prep(inputs)
    if "nc" not in _NC_CACHE:
        _NC_CACHE["nc"] = build_program()[0]
    nc = _NC_CACHE["nc"]
    X = np.asarray(inputs["L0_attn_X"], np.float32)
    in_maps = [dict(com, x0=np.ascontiguousarray(X[b])) for b in range(NCORES)]
    res = run_bass_kernel_spmd(nc, in_maps, core_ids=list(range(NCORES)))
    return np.stack([np.asarray(res.results[b]["out"], np.float32)
                     for b in range(NCORES)])
